# version 13
# baseline (speedup 1.0000x reference)
import numpy as np
import ml_dtypes
from contextlib import ExitStack

import concourse.bass as bass
import concourse.mybir as mybir
from concourse.alu_op_type import AluOpType as OP
import concourse.bass_utils as bass_utils

F32 = mybir.dt.float32
BF16 = mybir.dt.bfloat16
AF = mybir.ActivationFunctionType
AX = mybir.AxisListType

D = 1024
T = 2048
NT = 16
NQS = 4
NKC = 8
NPAIR = 8
NE = 8
HID = 4096
LN_EPS = 1e-5
NEG = -1.0e9


class Res:
    __slots__ = ("ready", "users")

    def __init__(self):
        self.ready = None
        self.users = []


class KB:
    ENGS = ("t", "v", "s", "g", "q")

    def __init__(self):
        self.ops = {e: [] for e in self.ENGS}
        self.cnt = {e: 0 for e in self.ENGS}
        self.waited = {e: {} for e in self.ENGS}
        self.dcnt = {}

    def em(self, e, fn, inc=0, serial=True):
        if e in ("v", "s"):
            if serial and self.cnt[e] > 0:
                self.ops[e].append((("WAIT", e, self.cnt[e]), 0))
            inc = max(inc, 1)
        self.ops[e].append((fn, inc))
        if inc:
            self.cnt[e] += inc
            return (e, self.cnt[e])
        return None

    def emd(self, e, fn, dkey):
        self.dcnt[dkey] = self.dcnt.get(dkey, 0) + 16
        self.ops[e].append((("DMA", fn, dkey), 0))
        return (dkey, self.dcnt[dkey])

    def wt(self, e, tok):
        if tok is None:
            return
        te, val = tok
        if te == e:
            return
        if self.waited[e].get(te, -1) >= val:
            return
        self.waited[e][te] = val
        self.ops[e].append((("WAIT", te, val), 0))

    def consume(self, e, res):
        self.wt(e, res.ready)

    def used(self, res, tok):
        if tok is not None:
            res.users.append(tok)

    def produce(self, res, tok):
        res.ready = tok
        res.users = []

    def reuse(self, e, res):
        for u in res.users:
            self.wt(e, u)
        if not res.users:
            self.wt(e, res.ready)


def build_nc():
    nc = bass.Bass("TRN2", target_bir_lowering=False)

    def din(name, shape, dt=F32):
        return nc.dram_tensor(name, shape, dt, kind="ExternalInput").ap()

    x_h = din("xcore", [T, D])
    aW = {w: din("L0_attn_W" + w, [D, D]) for w in "qkvo"}
    ab = {w: din("L0_attn_b" + w, [D]) for w in "qkvo"}
    fW = {w: din("L0_ffn_W" + w, [D, D]) for w in "qkvp"}
    fb = {w: din("L0_ffn_b" + w, [D]) for w in "qkvp"}
    g1_h = din("L0_ffn_ln1_g", [D])
    b1_h = din("L0_ffn_ln1_b", [D])
    g2_h = din("L0_ffn_ln2_g", [D])
    b2_h = din("L0_ffn_ln2_b", [D])
    gw_h = din("L0_ffn_gate_W", [D, NE])
    gb_h = din("L0_ffn_gate_b", [NE])
    w1_h = din("L0_ffn_exp_W1", [NE, D, HID])
    eb1_h = din("L0_ffn_exp_b1", [NE, HID])
    w2_h = din("L0_ffn_exp_W2", [NE, HID, D])
    eb2_h = din("L0_ffn_exp_b2", [NE, D])
    identb_h = din("cc_identb", [128, 128], BF16)
    identf_h = din("cc_identf", [128, 128])
    masks_h = din("cc_masks", [128, 2048], BF16)

    out_h = nc.dram_tensor("out", [T, D], F32, kind="ExternalOutput").ap()
    x1s_h = nc.dram_tensor("x1s", [T, D], F32, kind="Internal").ap()
    atts_h = nc.dram_tensor("atts", [T, D], F32, kind="Internal").ap()

    x_r = x_h.rearrange("(t p) d -> t p d", p=128)
    out_r = out_h.rearrange("(t p) d -> t p d", p=128)
    x1s_r = x1s_h.rearrange("(t p) d -> t p d", p=128)
    atts_r = atts_h.rearrange("(t p) d -> t p d", p=128)

    with ExitStack() as ctx:
        def sbt(name, shape, dt):
            return ctx.enter_context(nc.sbuf_tensor(name, shape, dt))

        def psb(name, shape, dt):
            return ctx.enter_context(nc.psum_tensor(name, shape, dt))

        T1 = sbt("T1", [128, 16384], BF16)
        T2 = sbt("T2", [128, 16384], BF16)
        T3 = sbt("T3", [128, 16384], BF16)
        T4 = sbt("T4", [128, 16512], BF16)
        T5 = sbt("T5", [128, 8192], F32)
        T6 = sbt("T6", [128, 8192], BF16)
        T7 = sbt("T7", [128, 2048], F32)
        T8 = sbt("T8", [128, 2048], F32)
        T9 = sbt("T9", [128, 256], BF16)
        T10 = sbt("T10", [128, 256], F32)
        PPB = sbt("PPB", [128, 96], F32)
        GWSB = sbt("GWSB", [128, 64], BF16)
        B2R = sbt("B2R", [1, 2048], F32)
        RSTG = sbt("RSTG", [1, 1024], F32)

        pw = [psb("pw%d" % i, [128, 1024], F32) for i in range(3)]
        pb = [pw[0][:, 0:512], pw[0][:, 512:1024], pw[1][:, 0:512], pw[1][:, 512:1024],
              pw[2][:, 0:512], pw[2][:, 512:1024]]
        pbE = psb("pbE", [128, 512], F32)
        pbO = psb("pbO", [128, 512], F32)
        pt0f = pb[5]
        ptf = [pb[4], pb[5]]

        sems = {e: ctx.enter_context(nc.semaphore("sem_" + e)) for e in KB.ENGS}
        kb = KB()

        identb = T9[:, 0:128]
        ones_b = T9[:, 128:256]
        identf = T10[:, 0:128]
        ones_f = T10[:, 128:256]
        T5b = T5[:].bitcast(BF16)
        wo_sb = T5b[:, 0:8192]
        sscr = [T5[:, 4096:4608], T5[:, 4608:5120]]
        scrY = T5[:, 5120:6144]
        bvb = T5[:, 6144:7168]
        bob = T5[:, 7168:8192]
        pring = [T6[:, i * 1024:(i + 1) * 1024] for i in range(4)]
        ctxsb = T6[:, 4096:8192]
        scrZ = T6[:, 0:2048].bitcast(F32)
        ctxZf = T6[:, 4096:8192].bitcast(F32)
        masksA = T2[:, 12288:14336]
        sscrW = T5[:, 4096:5120]
        sums_sb = T8[:, 0:512]
        lnscr = T8[:, 512:1024]
        rb_sb = T8[:, 1024:1536]
        m1s1 = T8[:, 1536:1568]
        wfull = T8[:, 1600:1728]
        gbb = T8[:, 1728:1736]
        gsc = T8[:, 1736:1800]
        bnsc = T8[:, 1800:1812]
        bn2 = T8[:, 1812:1814]
        tstat = T8[:, 1814:1824]
        b1pp = [T8[:, 1824 + 32 * i: 1856 + 32 * i] for i in range(2)]
        CONSTS = T8[:, 1888:1896]
        stg = T8[:, 1904:2032]
        T2half = T2[:, 12288:16384]
        qslab = T2[:, 8192:12288]
        wq_sb = T2[:, 0:8192]
        T3f = T3[:].bitcast(F32)
        T7b = T7[:].bitcast(BF16)
        u_lo = T4[:].bitcast(F32)

        def u_sl(tt, c0, c1):
            if tt < 8:
                return u_lo[:, tt * 1024 + c0: tt * 1024 + c1]
            return T5[:, (tt - 8) * 1024 + c0: (tt - 8) * 1024 + c1]

        r_pb = [Res() for _ in range(6)]
        r_pbE = Res()
        r_pbO = Res()
        r_pt = [r_pb[4], r_pb[5]]
        r_pring = [Res() for _ in range(4)]
        r_xr = [Res(), Res()]
        r_afm = [Res() for _ in range(NQS)]
        r_kfm = [Res() for _ in range(NPAIR)]
        r_vsb = [Res() for _ in range(NT)]
        r_qsl = Res()
        r_ctx = [Res() for _ in range(NPAIR)]
        r_wq = Res()
        r_whalfA = Res()
        r_whalfB = Res()
        r_wo = Res()
        r_sums = Res()
        r_scrY = Res()
        r_sscr = [Res(), Res()]
        r_bvb = Res()
        r_bob = Res()
        r_w2 = [Res(), Res(), Res()]
        r_h2 = Res()
        r_gmh = [Res() for _ in range(16)]
        r_b2r = [Res(), Res()]
        r_b1p = [Res(), Res()]
        r_stg = Res()
        r_zr = [Res(), Res()]
        r_w1r = [Res(), Res()]
        r_rstg = Res()

        dkeys = {}
        anon_ctr = [0]

        def dma(e, dst, src, res=None, waits=(), dkey=None):
            for w in waits:
                kb.wt(e, w)
            if res is not None:
                kb.reuse(e, res)
            if dkey is None:
                if res is not None:
                    dkey = dkeys.setdefault(id(res), "d%d" % len(dkeys))
                else:
                    anon_ctr[0] += 1
                    dkey = "a%d" % anon_ctr[0]
            tok = kb.emd(e, lambda E, d=dst, s=src: E.dma_start(out=d, in_=s), dkey)
            if res is not None:
                kb.produce(res, tok)
            return tok

        ln_prev_s = [None]

        def ln_stats(src1024, waits=()):
            for w in waits:
                kb.wt("v", w)
            kb.wt("v", ln_prev_s[0])
            kb.em("v", lambda E, s=src1024: E.bn_stats(out=bnsc[:, 0:6], in_=s[:, 0:512]))
            kb.em("v", lambda E, s=src1024: E.bn_stats(out=bnsc[:, 6:12], in_=s[:, 512:1024]))
            tv = kb.em("v", lambda E: E.bn_aggr(out=bn2[:], in_=bnsc[:]), inc=1)
            kb.wt("s", tv)
            kb.em("s", lambda E: E.activation(out=tstat[:, 0:1], in_=bn2[:, 1:2], func=AF.Ln, bias=CONSTS[:, 0:1]))
            ts_ = kb.em("s", lambda E: E.activation(out=tstat[:, 1:2], in_=tstat[:, 0:1], func=AF.Exp, scale=-0.5), inc=1)
            ln_prev_s[0] = ts_
            return bn2[:, 0:1], tstat[:, 1:2], ts_

        def transpose8(src1024, src_tok, evac, src_res=None):
            kb.wt("t", tok_idf)
            kb.wt("t", src_tok)
            for bank in range(2):
                kb.reuse("t", r_pt[bank])
                tk = None
                for j in range(4):
                    c = bank * 4 + j
                    tk = kb.em("t", lambda E, c=c, j=j, b=bank: E.transpose(
                        out=ptf[b][:, j * 128:(j + 1) * 128],
                        in_=src1024[:, c * 128:(c + 1) * 128],
                        identity=identf), inc=(1 if j == 3 else 0))
                kb.produce(r_pt[bank], tk)
                if src_res is not None:
                    kb.used(src_res, tk)
                kb.wt("v", tk)
                for j in range(4):
                    c = bank * 4 + j
                    tv = evac(c, ptf[bank][:, j * 128:(j + 1) * 128])
                    kb.used(r_pt[bank], tv)

        def scrz_write(fn):
            for rp in r_pring:
                kb.reuse("v", rp)
            tok = kb.em("v", fn, inc=1)
            for rp in r_pring:
                kb.produce(rp, tok)
            return tok

        def scrz_used(tok):
            for rp in r_pring:
                kb.used(rp, tok)

        tok_idb = dma("q", identb, identb_h)
        tok_idf = dma("q", identf, identf_h)
        ones_tok = kb.em("v", lambda E: E.memset(ones_b[:], 1.0), inc=1)
        ones_tok = kb.em("v", lambda E: E.memset(ones_f[:], 1.0), inc=1)
        kb.em("v", lambda E: E.memset(CONSTS[:, 0:1], LN_EPS))
        consts_tok = kb.em("v", lambda E: E.memset(CONSTS[:, 1:2], 1.0), inc=1)
        vaug4 = T4[:, 0:16512].rearrange("p (t pr c) -> p t pr c", pr=8, c=129)
        vones_tok = kb.em("v", lambda E: E.memset(vaug4[:, :, :, 64:65], 1.0), inc=1)

        pp_loads = [
            (ab["q"], PPB[:, 0:8]), (ab["k"], PPB[:, 8:16]),
            (fb["q"], PPB[:, 16:24]), (fb["k"], PPB[:, 24:32]),
            (g1_h, PPB[:, 32:40]), (b1_h, PPB[:, 40:48]),
            (g2_h, PPB[:, 48:56]), (b2_h, PPB[:, 56:64]),
        ]
        for src, dstcols in pp_loads:
            tokd = dma("q", stg[0:8, :], src.rearrange("(a p) -> a p", p=128), res=r_stg)
            kb.wt("t", tokd)
            kb.wt("t", tok_idf)
            kb.reuse("t", r_pb[5])
            tokt = kb.em("t", lambda E: E.transpose(
                out=pb[5][:, 0:8], in_=stg[0:8, :], identity=identf[0:8, 0:8]), inc=1)
            kb.produce(r_pb[5], tokt)
            kb.wt("v", tokt)
            tokv = kb.em("v", lambda E, dc=dstcols: E.tensor_copy(out=dc, in_=pb[5][:, 0:8]), inc=1)
            kb.used(r_pb[5], tokv)
            kb.used(r_stg, tokv)

        tok_gw = dma("g", GWSB[:, 0:64].rearrange("p (kc j) -> p kc j", j=8),
                     gw_h.rearrange("(kc p) j -> p kc j", p=128))
        tokd = dma("q", RSTG[0:1, 0:8], gb_h.rearrange("(a j) -> a j", a=1), res=r_rstg)
        kb.wt("t", tokd)
        kb.wt("t", ones_tok)
        kb.reuse("t", r_pb[5])
        tokt = kb.em("t", lambda E: E.matmul(out=pb[5][:, 0:8], lhsT=ones_f[0:1, 0:128],
                                             rhs=RSTG[0:1, 0:8], start=True, stop=True), inc=1)
        kb.produce(r_pb[5], tokt)
        kb.wt("v", tokt)
        tokv = kb.em("v", lambda E: E.tensor_copy(out=gbb, in_=pb[5][:, 0:8]), inc=1)
        kb.used(r_pb[5], tokv)
        kb.used(r_rstg, tokv)

        def load_bias_bcast(src_h, dst, res):
            tokd = dma("q", RSTG[0:1, :], src_h.rearrange("(a j) -> a j", a=1), res=r_rstg)
            kb.wt("t", tokd)
            kb.wt("t", ones_tok)
            toks = []
            for half in range(2):
                kb.reuse("t", r_pb[4 + half])
                tk = kb.em("t", lambda E, h=half: E.matmul(
                    out=pb[4 + h][:], lhsT=ones_f[0:1, 0:128],
                    rhs=RSTG[0:1, h * 512:(h + 1) * 512], start=True, stop=True), inc=1)
                kb.produce(r_pb[4 + half], tk)
                toks.append(tk)
            kb.reuse("v", res)
            vt = None
            for half in range(2):
                kb.wt("v", toks[half])
                vt = kb.em("v", lambda E, h=half, d=dst: E.tensor_copy(
                    out=d[:, h * 512:(h + 1) * 512], in_=pb[4 + h][:]), inc=1)
                kb.used(r_pb[4 + half], vt)
                kb.used(r_rstg, vt)
            kb.produce(res, vt)

        kb.em("v", lambda E: E.tensor_scalar(
            out=PPB[:, 64:72], in0=PPB[:, 0:8], scalar1=0.125, scalar2=None, op0=OP.mult))
        tok_bq8 = kb.em("v", lambda E: E.tensor_scalar(
            out=PPB[:, 72:80], in0=PPB[:, 16:24], scalar1=0.125, scalar2=None, op0=OP.mult), inc=1)

        load_bias_bcast(ab["v"], bvb, r_bvb)
        load_bias_bcast(ab["o"], bob, r_bob)

        for tt in range(NT):
            sl = tt % 2
            xr = T7[:, sl * 1024:(sl + 1) * 1024]
            tokd = dma("q", xr, x_r[tt], res=r_xr[sl])
            m, s, tok_s = ln_stats(xr, waits=(tokd,))
            kb.wt("v", tok_s)
            kb.reuse("v", r_scrY)
            ta = kb.em("v", lambda E, xr=xr: E.tensor_scalar(
                out=scrY, in0=xr, scalar1=bn2[:, 0:1], scalar2=tstat[:, 1:2],
                op0=OP.subtract, op1=OP.mult), inc=1)
            kb.produce(r_scrY, ta)
            kb.used(r_xr[sl], ta)

            qs = tt // 4

            def evac_a(c, pap, tt=tt):
                return kb.em("v", lambda E, c=c, pap=pap: E.tensor_copy(
                    out=T1[:, c * 2048 + tt * 128:(c * 2048 + tt * 128) + 128], in_=pap), inc=1)

            transpose8(scrY, ta, evac_a, src_res=r_scrY)
            if tt % 4 == 3:
                kb.produce(r_afm[qs], ("v", kb.cnt["v"]))

        spill_toks = []
        def emit_attention(wq_h, wk_h, wv_h, wout_h, bq8c, bkc, epi, first):
            wsrc = lambda w: w.rearrange("(kc p) d -> p kc d", p=128)
            tok_wq = dma("g", wq_sb.rearrange("p (kc d) -> p kc d", d=1024), wsrc(wq_h), res=r_wq)
            for rr, half_res in ((0, r_whalfA), (1, r_whalfB)):
                dstA = (T2half if rr == 0 else ctxsb).rearrange("p (kc d) -> p kc d", d=1024)
                if rr == 1:
                    for rc in r_ctx:
                        kb.reuse("g", rc)
                dma("g", dstA, wsrc(wk_h)[:, rr * 4:(rr + 1) * 4, :], res=half_res)
            tok_wo = dma("g", wo_sb.rearrange("p (kc d) -> p kc d", d=1024), wsrc(wout_h), res=r_wo)

            for c in range(NPAIR):
                kb.reuse("v", r_kfm[c])
                for tsl in range(4):
                    bank = (c * 4 + tsl) % 4
                    kb.reuse("t", r_pb[bank])
                    kb.consume("t", r_whalfA)
                    kb.consume("t", r_whalfB)
                    kb.consume("t", r_afm[tsl])
                    tk = None
                    for kc in range(NKC):
                        slot = T2half if kc < 4 else ctxsb
                        tk = kb.em("t", lambda E, c=c, tsl=tsl, kc=kc, slot=slot, bank=bank: E.matmul(
                            out=pb[bank][:],
                            lhsT=slot[:, (kc % 4) * 1024 + c * 128:(kc % 4) * 1024 + c * 128 + 128],
                            rhs=T1[:, kc * 2048 + tsl * 512:kc * 2048 + tsl * 512 + 512],
                            start=(kc == 0), stop=(kc == NKC - 1)), inc=(1 if kc == NKC - 1 else 0))
                    kb.used(r_whalfA, tk)
                    kb.used(r_whalfB, tk)
                    kb.used(r_afm[tsl], tk)
                    kb.produce(r_pb[bank], tk)
                    kb.wt("v", tk)
                    tv = kb.em("v", lambda E, c=c, tsl=tsl, bank=bank: E.tensor_scalar(
                        out=T3[:, c * 2048 + tsl * 512:c * 2048 + tsl * 512 + 512],
                        in0=pb[bank][:], scalar1=bkc[:, c:c + 1], scalar2=None, op0=OP.add), inc=1, serial=False)
                    kb.used(r_pb[bank], tv)
                kb.produce(r_kfm[c], ("v", kb.cnt["v"]))

            for rr, half_res in ((0, r_whalfA), (1, r_whalfB)):
                dstA = (T2half if rr == 0 else ctxsb).rearrange("p (kc d) -> p kc d", d=1024)
                dma("g", dstA, wsrc(wv_h)[:, rr * 4:(rr + 1) * 4, :], res=half_res)
            kb.consume("v", r_bvb)
            for tt in range(NT):
                kb.reuse("v", r_vsb[tt])
                last_v = None
                for dsh in range(2):
                    bank = (tt * 2 + dsh) % 4
                    kb.reuse("t", r_pb[bank])
                    tk = None
                    for kc in range(NKC):
                        slot = T2half if kc < 4 else ctxsb
                        res_w = r_whalfA if kc < 4 else r_whalfB
                        kb.consume("t", res_w)
                        kb.consume("t", r_afm[tt // 4])
                        tk = kb.em("t", lambda E, tt=tt, dsh=dsh, kc=kc, slot=slot, bank=bank: E.matmul(
                            out=pb[bank][:],
                            lhsT=T1[:, kc * 2048 + tt * 128:kc * 2048 + tt * 128 + 128],
                            rhs=slot[:, (kc % 4) * 1024 + dsh * 512:(kc % 4) * 1024 + dsh * 512 + 512],
                            start=(kc == 0), stop=(kc == NKC - 1)), inc=(1 if kc == NKC - 1 else 0))
                        if kc == NKC - 1:
                            kb.used(r_whalfA, tk)
                            kb.used(r_whalfB, tk)
                            kb.used(r_afm[tt // 4], tk)
                    kb.produce(r_pb[bank], tk)
                    kb.wt("v", tk)
                    kb.wt("v", vones_tok)
                    base = tt * 1032 + dsh * 4 * 129
                    vdst = T4[:, base: base + 516].rearrange("p (pr c) -> p pr c", c=129)
                    psrc = pb[bank][:].rearrange("p (pr c) -> p pr c", c=128)
                    bsrc = bvb[:, dsh * 512:(dsh + 1) * 512].rearrange("p (pr c) -> p pr c", c=128)
                    for half in range(2):
                        tv = kb.em("v", lambda E, vdst=vdst, psrc=psrc, bsrc=bsrc, h=half: E.scalar_tensor_tensor(
                            out=vdst[:, :, h * 65:h * 65 + 64],
                            in0=psrc[:, :, 0:64] if h == 0 else psrc[:, :, 64:128],
                            scalar=1.0,
                            in1=bsrc[:, :, 0:64] if h == 0 else bsrc[:, :, 64:128],
                            op0=OP.mult, op1=OP.add), inc=1, serial=False)
                        kb.used(r_pb[bank], tv)
                        kb.used(r_bvb, tv)
                        last_v = tv
                kb.produce(r_vsb[tt], last_v)

            kb.reuse("q", r_whalfA)
            tok_masks = dma("q", masksA, masks_h, dkey="dmask")

            for qs in range(NQS):
                kb.reuse("v", r_qsl)
                for c in range(NPAIR):
                    bank = c % 4
                    kb.reuse("t", r_pb[bank])
                    kb.consume("t", r_wq)
                    kb.consume("t", r_afm[qs])
                    tk = None
                    for kc in range(NKC):
                        tk = kb.em("t", lambda E, c=c, kc=kc, qs=qs, bank=bank: E.matmul(
                            out=pb[bank][:],
                            lhsT=wq_sb[:, kc * 1024 + c * 128:kc * 1024 + c * 128 + 128],
                            rhs=T1[:, kc * 2048 + qs * 512:kc * 2048 + qs * 512 + 512],
                            start=(kc == 0), stop=(kc == NKC - 1)), inc=(1 if kc == NKC - 1 else 0))
                    kb.used(r_wq, tk)
                    kb.used(r_afm[qs], tk)
                    kb.produce(r_pb[bank], tk)
                    kb.wt("s", tk)
                    kb.wt("s", tok_bq8)
                    tv = kb.em("s", lambda E, c=c, bank=bank: E.activation(
                        out=qslab[:, c * 512:(c + 1) * 512], in_=pb[bank][:],
                        func=AF.Identity, bias=bq8c[:, c:c + 1], scale=0.125), inc=1, serial=False)
                    kb.used(r_pb[bank], tv)
                kb.produce(r_qsl, ("s", kb.cnt["s"]))

                nkt = 4 * qs + 4
                def emit_pair_epilogue(p, rbT, rbb):
                        kb.wt("v", r_pbE.ready)
                        kb.wt("v", r_pbO.ready)
                        kb.reuse("v", r_sums)
                        kb.em("v", lambda E: E.tensor_copy(out=sums_sb[64:65, :], in_=pbE[64:65, :]))
                        tv = kb.em("v", lambda E: E.tensor_copy(out=sums_sb[96:97, :], in_=pbE[96:97, :]), inc=1)
                        kb.produce(r_sums, tv)
                        kb.used(r_pbE, tv)
                        kb.used(r_pbO, tv)
                        kb.wt("t", tv)
                        kb.reuse("t", r_pb[2 * rbT])
                        kb.reuse("t", r_pb[2 * rbT + 1])
                        kb.em("t", lambda E, rbb=rbb: E.matmul(
                            out=rbb[0:64, :], lhsT=ones_f[64:65, 0:64], rhs=sums_sb[64:65, :],
                            start=True, stop=True, tile_position=(64, 0)))
                        tk = kb.em("t", lambda E, rbb=rbb: E.matmul(
                            out=rbb[64:128, :], lhsT=ones_f[96:97, 64:128], rhs=sums_sb[96:97, :],
                            start=True, stop=True, tile_position=(96, 64), skip_group_check=True), inc=1)
                        kb.produce(r_pb[2 * rbT], tk)
                        kb.used(r_sums, tk)
                        kb.wt("s", tk)
                        kb.em("s", lambda E, rbb=rbb: E.activation(out=lnscr, in_=rbb, func=AF.Ln))
                        ts_ = kb.em("s", lambda E: E.activation(out=rb_sb, in_=lnscr, func=AF.Exp, scale=-1.0), inc=1)
                        kb.used(r_pb[2 * rbT], ts_)
                        kb.wt("v", ts_)
                        kb.reuse("v", r_ctx[p])
                        kb.em("v", lambda E, p=p: E.tensor_tensor(
                            out=ctxsb[0:64, p * 512:(p + 1) * 512], in0=pbE[0:64, :],
                            in1=rb_sb[0:64, :], op=OP.mult), serial=False)
                        tv = kb.em("v", lambda E, p=p: E.tensor_tensor(
                            out=ctxsb[64:128, p * 512:(p + 1) * 512], in0=pbO[64:128, :],
                            in1=rb_sb[64:128, :], op=OP.mult), inc=1, serial=False)
                        kb.produce(r_ctx[p], tv)
                        kb.used(r_pbE, tv)
                        kb.used(r_pbO, tv)

                pend_epi = [None]

                for p in range(NPAIR):
                    nb = nkt // 2
                    exp_toks = {}

                    def emit_scores_and_exp(b, p=p, qs=qs, nb=nb):
                        diag = 2 * b >= 4 * qs
                        tsrs = [(p * 2 * nb + 2 * b) % 3, (p * 2 * nb + 2 * b + 1) % 3]
                        for head in range(2):
                            kb.reuse("t", r_pb[2 * tsrs[head]])
                            kb.reuse("t", r_pb[2 * tsrs[head] + 1])
                        kb.consume("t", r_kfm[p])
                        kb.consume("t", r_qsl)
                        tks = [None, None]
                        for j01 in range(2):
                            kt = 2 * b + j01
                            for head in range(2):
                                tsr = tsrs[head]
                                base = head * 64
                                tks[head] = kb.em("t", lambda E, p=p, kt=kt, tsr=tsr, j01=j01, base=base, diag=diag: E.matmul(
                                    out=pw[tsr][:, j01 * 512:(j01 + 1) * 512],
                                    lhsT=T3[base:base + 64, p * 2048 + kt * 128:p * 2048 + kt * 128 + 128],
                                    rhs=qslab[base:base + 64, p * 512:(p + 1) * 512],
                                    start=True, stop=not diag), inc=(0 if diag else 1))
                            if diag:
                                jv = kt - 4 * qs
                                kb.wt("t", tok_idb)
                                kb.wt("t", tok_masks)
                                for head in range(2):
                                    tsr = tsrs[head]
                                    tks[head] = kb.em("t", lambda E, tsr=tsr, j01=j01, jv=jv: E.matmul(
                                        out=pw[tsr][:, j01 * 512:(j01 + 1) * 512],
                                        lhsT=identb,
                                        rhs=masksA[:, jv * 512:(jv + 1) * 512],
                                        start=False, stop=True), inc=1)
                                    kb.used(r_whalfA, tks[head])
                        for head in range(2):
                            tsr = tsrs[head]
                            tk = tks[head]
                            kb.produce(r_pb[2 * tsr], tk)
                            kb.produce(r_pb[2 * tsr + 1], tk)
                            kb.used(r_kfm[p], tk)
                            kb.used(r_qsl, tk)
                            slot = (head * 2) + b % 2
                            kb.wt("s", tk)
                            kb.reuse("s", r_pring[slot])
                            te = kb.em("s", lambda E, slot=slot, tsr=tsr: E.activation(
                                out=pring[slot], in_=pw[tsr][:], func=AF.Exp), inc=1, serial=False)
                            kb.used(r_pb[2 * tsr], te)
                            kb.used(r_pb[2 * tsr + 1], te)
                            kb.produce(r_pring[slot], te)
                            exp_toks[2 * b + head] = (slot, te)

                    def emit_ctx(b, p=p, nb=nb, nkt=nkt):
                        slot_e = exp_toks[2 * b][0]
                        slot_o = exp_toks[2 * b + 1][0]
                        if b == 0:
                            kb.reuse("t", r_pbE)
                            kb.reuse("t", r_pbO)
                        kb.consume("t", r_pring[slot_e])
                        kb.consume("t", r_pring[slot_o])
                        for j01 in range(2):
                            kt = 2 * b + j01
                            laste = (kt == nkt - 1)
                            kb.consume("t", r_vsb[kt])
                            pr_e = pring[slot_e][:, j01 * 512:(j01 + 1) * 512]
                            pr_o = pring[slot_o][:, j01 * 512:(j01 + 1) * 512]
                            tk_e = kb.em("t", lambda E, p=p, kt=kt, pr=pr_e, nkt=nkt: E.matmul(
                                out=pbE[0:65, :],
                                lhsT=T4[:, kt * 1032 + p * 129: kt * 1032 + p * 129 + 65],
                                rhs=pr, start=(kt == 0), stop=(kt == nkt - 1),
                                skip_group_check=True), inc=1)
                            kb.used(r_pring[slot_e], tk_e)
                            tk_s = kb.em("t", lambda E, kt=kt, pr=pr_o, nkt=nkt: E.matmul(
                                out=pbE[96:97, :], lhsT=ones_b[:, 0:1], rhs=pr,
                                start=(kt == 0), stop=(kt == nkt - 1),
                                tile_position=(0, 96), skip_group_check=True), inc=1)
                            tk_o = kb.em("t", lambda E, p=p, kt=kt, pr=pr_o, nkt=nkt: E.matmul(
                                out=pbO[64:128, :],
                                lhsT=T4[:, kt * 1032 + p * 129 + 65: kt * 1032 + p * 129 + 129],
                                rhs=pr, start=(kt == 0), stop=(kt == nkt - 1),
                                tile_position=(0, 64), skip_group_check=True), inc=1)
                            kb.used(r_pring[slot_o], tk_o)
                            kb.used(r_vsb[kt], tk_o)
                            if laste:
                                kb.produce(r_pbE, tk_s)
                                kb.produce(r_pbO, tk_o)

                    for b in range(nb):
                        if b == 1 and pend_epi[0] is not None:
                            pend_epi[0]()
                            pend_epi[0] = None
                        emit_scores_and_exp(b)
                        if b >= 1:
                            emit_ctx(b - 1)
                    emit_ctx(nb - 1)
                    rbT = (p * 2 * nb + 2 * nb + 2) % 3
                    pend_epi[0] = (lambda p=p, rbT=rbT: emit_pair_epilogue(p, rbT, pw[rbT][:, 0:512]))
                if pend_epi[0] is not None:
                    pend_epi[0]()
                    pend_epi[0] = None
                for i in range(4):
                    tt = qs * 4 + i
                    banks = (2 * (i % 2), 2 * (i % 2) + 1)
                    toks = []
                    for ds in range(2):
                        bank = banks[ds]
                        kb.reuse("t", r_pb[bank])
                        kb.consume("t", r_wo)
                        tk = None
                        for c in range(NPAIR):
                            kb.consume("t", r_ctx[c])
                            tk = kb.em("t", lambda E, c=c, i=i, ds=ds, bank=bank: E.matmul(
                                out=pb[bank][:],
                                lhsT=ctxsb[:, c * 512 + i * 128:c * 512 + i * 128 + 128],
                                rhs=wo_sb[:, c * 1024 + ds * 512:c * 1024 + ds * 512 + 512],
                                start=(c == 0), stop=(c == NPAIR - 1)), inc=(1 if c == NPAIR - 1 else 0))
                        for c in range(NPAIR):
                            kb.used(r_ctx[c], tk)
                        kb.used(r_wo, tk)
                        kb.produce(r_pb[bank], tk)
                        toks.append(tk)
                    epi(tt, banks, toks)

        def epi1(tt, banks, toks):
            qs = tt // 4
            kb.consume("v", r_bob)
            kb.reuse("v", r_scrY)
            ty = None
            for ds in range(2):
                kb.wt("v", toks[ds])
                ty = kb.em("v", lambda E, ds=ds, b=banks[ds]: E.scalar_tensor_tensor(
                    out=scrY[:, ds * 512:(ds + 1) * 512], in0=pb[b][:], scalar=1.0,
                    in1=bob[:, ds * 512:(ds + 1) * 512], op0=OP.mult, op1=OP.add), inc=1)
                kb.used(r_pb[banks[ds]], ty)
                kb.used(r_bob, ty)
            kb.produce(r_scrY, ty)
            m, s, tok_s = ln_stats(scrY)
            sl = tt % 2
            xr = T7[:, sl * 1024:(sl + 1) * 1024]
            tokd = dma("q", xr, x_r[tt], res=r_xr[sl])
            kb.wt("v", tok_s)
            tz = scrz_write(lambda E: E.tensor_scalar(
                out=scrZ, in0=scrY, scalar1=bn2[:, 0:1], scalar2=tstat[:, 1:2],
                op0=OP.subtract, op1=OP.mult))
            kb.used(r_scrY, tz)
            kb.wt("v", tokd)
            tx1 = kb.em("v", lambda E, xr=xr: E.tensor_tensor(
                out=xr, in0=scrZ, in1=xr, op=OP.add), inc=1)
            scrz_used(tx1)
            tok_sp = dma("q", x1s_r[tt], xr, waits=(tx1,))
            spill_toks.append(tok_sp)
            kb.used(r_xr[sl], tok_sp)
            m, s, tok_s = ln_stats(xr, waits=(tx1,))
            kb.wt("v", tok_s)
            kb.em("v", lambda E, tt=tt: E.tensor_copy(out=m1s1[:, tt:tt + 1], in_=bn2[:, 0:1]))
            tms = kb.em("v", lambda E, tt=tt: E.tensor_copy(out=m1s1[:, 16 + tt:17 + tt], in_=tstat[:, 1:2]), inc=1)
            kb.reuse("v", r_scrY)
            t1 = kb.em("v", lambda E, xr=xr, tt=tt: E.tensor_scalar(
                out=scrY, in0=xr, scalar1=m1s1[:, tt:tt + 1], scalar2=m1s1[:, 16 + tt:17 + tt],
                op0=OP.subtract, op1=OP.mult), inc=1)
            kb.produce(r_scrY, t1)
            kb.used(r_xr[sl], t1)

            def evac_h1(c, pap, tt=tt):
                return kb.em("v", lambda E, c=c, pap=pap: E.tensor_scalar(
                    out=T1[:, c * 2048 + tt * 128:c * 2048 + tt * 128 + 128], in0=pap,
                    scalar1=PPB[:, 32 + c:33 + c], scalar2=PPB[:, 40 + c:41 + c],
                    op0=OP.mult, op1=OP.add), inc=1)

            transpose8(scrY, t1, evac_h1, src_res=r_scrY)
            if tt % 4 == 3:
                kb.produce(r_afm[qs], ("v", kb.cnt["v"]))

        def epi2(tt, banks, toks):
            kb.consume("v", r_bob)
            if tt % 2 == 0:
                kb.reuse("v", r_scrY)
                dstw = scrY
            else:
                dstw = scrZ
            ta = None
            for ds in range(2):
                kb.wt("v", toks[ds])
                if tt % 2 == 1 and ds == 0:
                    for rp in r_pring:
                        kb.reuse("v", rp)
                ta = kb.em("v", lambda E, ds=ds, b=banks[ds], dstw=dstw: E.scalar_tensor_tensor(
                    out=dstw[:, ds * 512:(ds + 1) * 512], in0=pb[b][:], scalar=1.0,
                    in1=bob[:, ds * 512:(ds + 1) * 512], op0=OP.mult, op1=OP.add), inc=1)
                kb.used(r_pb[banks[ds]], ta)
                kb.used(r_bob, ta)
            if tt % 2 == 0:
                kb.produce(r_scrY, ta)
            else:
                for rp in r_pring:
                    kb.produce(rp, ta)
            tok_sp = dma("q", atts_r[tt], dstw, waits=(ta,))
            spill_toks.append(tok_sp)
            if tt % 2 == 0:
                kb.used(r_scrY, tok_sp)
            else:
                scrz_used(tok_sp)

        emit_attention(aW["q"], aW["k"], aW["v"], aW["o"], PPB[:, 64:72], PPB[:, 8:16], epi1, True)
        load_bias_bcast(fb["v"], bvb, r_bvb)
        load_bias_bcast(fb["p"], bob, r_bob)
        emit_attention(fW["q"], fW["k"], fW["v"], fW["p"], PPB[:, 72:80], PPB[:, 24:32], epi2, False)

        for rs in r_vsb + [r_wo, r_bvb, r_bob, r_scrY, r_sscr[0], r_sscr[1]]:
            kb.reuse("v", rs)
        for rs in r_ctx + r_pring:
            kb.reuse("v", rs)
        att_slots = [ctxZf[:, 0:1024], ctxZf[:, 1024:2048]]
        r_att = [Res(), Res()]
        for _tok in spill_toks:
            kb.wt("q", _tok)
        u_toks = [None] * NT

        w2buf = [T2[:, 0:8192], T2[:, 8192:16384], T6[:, 0:8192]]
        h2f = T1
        gmv = T3
        w1r = [T7b[:, 0:2048], T7b[:, 2048:4096]]

        def w2_load(idx):
            e, hq = divmod(idx, 4)
            buf = w2buf[idx % 3]
            res = r_w2[idx % 3]
            if idx == 0:
                kb.reuse("g", r_wq)
            if idx == 1:
                kb.reuse("g", r_qsl)
                kb.reuse("g", r_whalfA)
            if idx == 2:
                for rs in r_pring + r_ctx + [r_att[0], r_att[1]]:
                    kb.reuse("g", rs)
            src = w2_h[e, hq * 1024:(hq + 1) * 1024, :].rearrange("(hc p) d -> p hc d", p=128)
            return dma("g", buf.rearrange("p (hc d) -> p hc d", d=1024), src, res=res)

        w2_load(0)
        w2_load(1)

        def eb_load(e):
            sl = e % 2
            dma("q", B2R[0:1, sl * 1024:(sl + 1) * 1024], eb2_h[e:e + 1, :], res=r_b2r[sl])
            tokb = dma("q", stg[0:32, :], eb1_h[e].rearrange("(a p) -> a p", p=128), res=r_stg)
            kb.wt("t", tokb)
            kb.reuse("t", r_pb[5])
            tokt = kb.em("t", lambda E: E.transpose(
                out=pt0f[:, 0:32], in_=stg[0:32, :], identity=identf[0:32, 0:32]), inc=1)
            kb.produce(r_pb[5], tokt)
            kb.wt("v", tokt)
            kb.reuse("v", r_b1p[sl])
            tokv = kb.em("v", lambda E, sl=sl: E.tensor_copy(out=b1pp[sl], in_=pt0f[:, 0:32]), inc=1)
            kb.used(r_pb[5], tokv)
            kb.used(r_stg, tokv)
            kb.produce(r_b1p[sl], tokv)

        r_t2s = Res()
        kb.reuse("v", r_h2)
        for rs in r_vsb + [r_wo, r_bvb, r_bob, r_scrY, r_sscr[0], r_sscr[1]]:
            kb.reuse("v", rs)
        for rs in r_pring + r_ctx + r_kfm + [r_att[0], r_att[1]]:
            kb.reuse("v", rs)
        kb.consume("t", r_h2)
        kb.wt("t", tok_gw)
        gate_prev_s = None
        gbank = [pbE, pbO]
        gres = [r_pbE, r_pbO]
        for tt in range(NT):
            sl = tt % 2
            xr = T7[:, sl * 1024:(sl + 1) * 1024]
            tokx = dma("q", xr, x1s_r[tt], res=r_xr[sl])
            if tt < 2:
                for rc in r_ctx:
                    kb.reuse("q", rc)
            toka = dma("q", att_slots[sl], atts_r[tt], res=r_att[sl])
            kb.wt("v", tokx)
            th = kb.em("v", lambda E, xr=xr, tt=tt: E.tensor_scalar(
                out=u_sl(tt, 0, 1024), in0=xr, scalar1=m1s1[:, tt:tt + 1],
                scalar2=m1s1[:, 16 + tt:17 + tt], op0=OP.subtract, op1=OP.mult), inc=1)
            kb.used(r_xr[sl], th)
            kb.wt("v", toka)
            tu = kb.em("v", lambda E, tt=tt, sl=sl: E.tensor_tensor(
                out=u_sl(tt, 0, 1024), in0=att_slots[sl], in1=u_sl(tt, 0, 1024), op=OP.add), inc=1)
            kb.used(r_att[sl], tu)
            u_toks[tt] = tu
            m, s, tok_s = ln_stats(u_sl(tt, 0, 1024), waits=(tu,))
            kb.wt("v", tok_s)
            if tt % 4 == 0:
                kb.reuse("v", r_afm[tt // 4])
            t2stage = T3f[:, 0:1024]
            kb.reuse("v", r_t2s)
            t2tok = kb.em("v", lambda E, tt=tt: E.tensor_scalar(
                out=t2stage, in0=u_sl(tt, 0, 1024), scalar1=bn2[:, 0:1], scalar2=tstat[:, 1:2],
                op0=OP.subtract, op1=OP.mult), inc=1)
            kb.produce(r_t2s, t2tok)

            def evac_h2(c, pap, tt=tt):
                return kb.em("v", lambda E, c=c, pap=pap, tt=tt: E.tensor_scalar(
                    out=h2f[:, c * 2048 + tt * 128:c * 2048 + tt * 128 + 128], in0=pap,
                    scalar1=PPB[:, 48 + c:49 + c], scalar2=PPB[:, 56 + c:57 + c],
                    op0=OP.mult, op1=OP.add), inc=1)

            transpose8(t2stage, t2tok, evac_h2, src_res=r_t2s)
            tok_h2t = ("v", kb.cnt["v"])
            bank = tt % 2
            kb.reuse("t", gres[bank])
            kb.wt("t", tok_h2t)
            tk = None
            for kc in range(NKC):
                tk = kb.em("t", lambda E, tt=tt, kc=kc, bank=bank: E.matmul(
                    out=gbank[bank][:, 0:8],
                    lhsT=h2f[:, kc * 2048 + tt * 128:kc * 2048 + tt * 128 + 128],
                    rhs=GWSB[:, kc * 8:(kc + 1) * 8],
                    start=(kc == 0), stop=(kc == NKC - 1)), inc=(1 if kc == NKC - 1 else 0))
            kb.used(r_h2, tk)
            kb.produce(gres[bank], tk)
            kb.wt("v", tk)
            kb.wt("v", gate_prev_s)
            kb.em("v", lambda E, bank=bank: E.tensor_tensor(
                out=gsc[:, 0:8], in0=gbank[bank][:, 0:8], in1=gbb, op=OP.add))
            tg = kb.em("v", lambda E: E.reduce_max(out=gsc[:, 8:9], in_=gsc[:, 0:8], axis=AX.X), inc=1)
            kb.used(gres[bank], tg)
            kb.em("v", lambda E: E.tensor_scalar(
                out=gsc[:, 9:17], in0=gsc[:, 0:8], scalar1=gsc[:, 8:9], scalar2=None, op0=OP.is_equal))
            kb.em("v", lambda E: E.scalar_tensor_tensor(
                out=gsc[:, 17:25], in0=gsc[:, 9:17], scalar=-1.0e30, in1=gsc[:, 0:8],
                op0=OP.mult, op1=OP.add))
            kb.em("v", lambda E: E.reduce_max(out=gsc[:, 25:26], in_=gsc[:, 17:25], axis=AX.X))
            tg = kb.em("v", lambda E: E.tensor_tensor(
                out=gsc[:, 26:27], in0=gsc[:, 25:26], in1=gsc[:, 8:9], op=OP.subtract), inc=1)
            kb.wt("s", tg)
            kb.em("s", lambda E: E.activation(out=gsc[:, 27:28], in_=gsc[:, 26:27], func=AF.Exp))
            kb.em("s", lambda E: E.activation(out=gsc[:, 28:29], in_=gsc[:, 27:28], func=AF.Ln, bias=CONSTS[:, 1:2]))
            tw1 = kb.em("s", lambda E: E.activation(out=gsc[:, 29:30], in_=gsc[:, 28:29], func=AF.Exp, scale=-1.0), inc=1)
            gate_prev_s = tw1
            kb.wt("v", tw1)
            kb.em("v", lambda E: E.tensor_scalar(
                out=gsc[:, 30:31], in0=gsc[:, 29:30], scalar1=-1.0, scalar2=1.0,
                op0=OP.mult, op1=OP.add))
            kb.em("v", lambda E: E.tensor_scalar(
                out=gsc[:, 32:40], in0=gsc[:, 17:25], scalar1=gsc[:, 25:26], scalar2=None, op0=OP.is_equal))
            kb.em("v", lambda E: E.tensor_scalar(
                out=gsc[:, 40:48], in0=gsc[:, 32:40], scalar1=gsc[:, 30:31], scalar2=None, op0=OP.mult))
            kb.em("v", lambda E, tt=tt: E.scalar_tensor_tensor(
                out=wfull[:, tt * 8:(tt + 1) * 8], in0=gsc[:, 9:17], scalar=gsc[:, 29:30],
                in1=gsc[:, 40:48], op0=OP.mult, op1=OP.add), inc=1)
        kb.produce(r_h2, ("v", kb.cnt["v"]))

        T1f = T1[:].bitcast(F32)
        out_toks = []

        def emit_final(tt):
            sl = tt % 2
            xr = T7[:, sl * 1024:(sl + 1) * 1024]
            kb.reuse("q", r_w1r[sl])
            tokx = dma("q", xr, x1s_r[tt], res=r_xr[sl], waits=(u_toks[tt],))
            m, s, tok_s = ln_stats(u_sl(tt, 0, 1024), waits=(u_toks[tt],))
            zsl = T1f[:, sl * 1024:(sl + 1) * 1024]
            kb.wt("v", tok_s)
            kb.reuse("v", r_zr[sl])
            if tt < 2:
                kb.reuse("v", r_h2)
            tz = kb.em("v", lambda E, tt=tt, zsl=zsl: E.tensor_scalar(
                out=zsl, in0=u_sl(tt, 0, 1024), scalar1=bn2[:, 0:1], scalar2=tstat[:, 1:2],
                op0=OP.subtract, op1=OP.mult), inc=1)
            kb.wt("v", tokx)
            to = kb.em("v", lambda E, xr=xr, zsl=zsl: E.tensor_tensor(
                out=zsl, in0=xr, in1=zsl, op=OP.add), inc=1)
            kb.used(r_xr[sl], to)
            tokout = dma("q", out_r[tt], zsl, waits=(to,))
            out_toks.append(tokout)
            kb.produce(r_zr[sl], tokout)

        g4 = [pbE[:], pbO[:], pb[4], pb[5]]
        gres4 = [r_pbE, r_pbO, r_pb[4], r_pb[5]]
        for rk in r_kfm:
            kb.reuse("s", rk)
        eb_load(0)
        for e in range(NE):
            if e + 1 < NE:
                eb_load(e + 1)
            for hq in range(4):
                idx = e * 4 + hq
                for hcp in range(4):
                    slw = hcp % 2
                    if idx == 0 and hcp < 2:
                        kb.reuse("g", r_xr[slw])
                    src = w1_h[e, :, hq * 1024 + hcp * 256: hq * 1024 + (hcp + 1) * 256]
                    tokw = dma("g", w1r[slw].rearrange("p (kc j) -> p kc j", j=256),
                               src.rearrange("(kc p) j -> p kc j", p=128), res=r_w1r[slw])
                    for sub in range(2):
                        hc = hcp * 2 + sub
                        for ts_ in range(4):
                            bank = (hc * 4 + ts_) % 4
                            kb.reuse("t", r_pb[bank])
                            kb.wt("t", tokw)
                            kb.consume("t", r_h2)
                            tk = None
                            for kc in range(NKC):
                                tk = kb.em("t", lambda E, sub=sub, ts_=ts_, kc=kc, slw=slw, bank=bank: E.matmul(
                                    out=pb[bank][:],
                                    lhsT=w1r[slw][:, kc * 256 + sub * 128:kc * 256 + sub * 128 + 128],
                                    rhs=h2f[:, kc * 2048 + ts_ * 512:kc * 2048 + ts_ * 512 + 512],
                                    start=(kc == 0), stop=(kc == NKC - 1)), inc=(1 if kc == NKC - 1 else 0))
                            kb.used(r_w1r[slw], tk)
                            kb.used(r_h2, tk)
                            kb.produce(r_pb[bank], tk)
                            kb.wt("s", tk)
                            kb.consume("s", r_b1p[e % 2])
                            if ts_ == 0:
                                kb.reuse("s", r_gmh[hc])
                                kb.reuse("s", r_t2s)
                            tg = kb.em("s", lambda E, hc=hc, ts_=ts_, e=e, hq=hq, bank=bank: E.activation(
                                out=gmv[:, hc * 2048 + ts_ * 512:hc * 2048 + ts_ * 512 + 512],
                                in_=pb[bank][:], func=AF.Gelu,
                                bias=b1pp[e % 2][:, hq * 8 + hc:hq * 8 + hc + 1]), inc=1, serial=False)
                            kb.used(r_pb[bank], tg)
                            kb.used(r_b1p[e % 2], tg)
                            if ts_ == 3:
                                kb.produce(r_gmh[hc], tg)
                if idx + 2 < 32:
                    w2_load(idx + 2)
                kb.consume("t", r_w2[idx % 3])
                for tt in range(NT):
                    for ds in range(2):
                        bank = (tt * 2 + ds) % 4
                        kb.reuse("t", gres4[bank])
                        tk = None
                        for hc in range(8):
                            kb.consume("t", r_gmh[hc])
                            tk = kb.em("t", lambda E, tt=tt, ds=ds, hc=hc, bank=bank, idx=idx, hq=hq: E.matmul(
                                out=g4[bank],
                                lhsT=gmv[:, hc * 2048 + tt * 128:hc * 2048 + tt * 128 + 128],
                                rhs=w2buf[idx % 3][:, hc * 1024 + ds * 512:hc * 1024 + ds * 512 + 512],
                                start=(hc == 0), stop=(hc == 7 and hq != 0)), inc=(1 if (hc == 7 and hq != 0) else 0))
                        if hq == 0:
                            kb.consume("t", r_b2r[e % 2])
                            tk = kb.em("t", lambda E, ds=ds, e=e, bank=bank: E.matmul(
                                out=g4[bank], lhsT=ones_f[0:1, 0:128],
                                rhs=B2R[0:1, (e % 2) * 1024 + ds * 512:(e % 2) * 1024 + ds * 512 + 512],
                                start=False, stop=True), inc=1)
                            kb.used(r_b2r[e % 2], tk)
                        for hc in range(8):
                            kb.used(r_gmh[hc], tk)
                        kb.used(r_w2[idx % 3], tk)
                        kb.produce(gres4[bank], tk)
                        kb.wt("v", tk)
                        tv = kb.em("v", lambda E, tt=tt, ds=ds, e=e, bank=bank: E.scalar_tensor_tensor(
                            out=u_sl(tt, ds * 512, ds * 512 + 512), in0=g4[bank],
                            scalar=wfull[:, tt * 8 + e:tt * 8 + e + 1],
                            in1=u_sl(tt, ds * 512, ds * 512 + 512),
                            op0=OP.mult, op1=OP.add), inc=1, serial=False)
                        kb.used(gres4[bank], tv)
                        u_toks[tt] = tv
                        if e == NE - 1 and hq == 3 and ds == 1:
                            emit_final(tt)


        for _tok in out_toks:
            kb.ops["q"].append((("WAIT", _tok[0], _tok[1]), 0))

        for _k in kb.dcnt:
            sems[_k] = ctx.enter_context(nc.semaphore("sem_" + _k))
        block = ctx.enter_context(nc.Block())

        def runner(eng, e):
            for fn, inc in kb.ops[e]:
                if isinstance(fn, tuple):
                    if fn[0] == "WAIT":
                        eng.wait_ge(sems[fn[1]], fn[2])
                    elif fn[0] == "DMA":
                        instr = fn[1](eng)
                        instr.then_inc(sems[fn[2]], 16)
                    continue
                instr = fn(eng)
                if inc:
                    instr.then_inc(sems[e], inc)

        @block.tensor
        def _(eng):
            runner(eng, "t")

        @block.vector
        def _(eng):
            runner(eng, "v")

        @block.scalar
        def _(eng):
            runner(eng, "s")

        @block.gpsimd
        def _(eng):
            runner(eng, "g")

        @block.sync
        def _(eng):
            runner(eng, "q")

    return nc


_NC_CACHE = None


def kernel(**inputs):
    global _NC_CACHE
    if _NC_CACHE is None:
        _NC_CACHE = build_nc()
    nc = _NC_CACHE

    x = np.ascontiguousarray(np.asarray(inputs["L0_attn_x"], dtype=np.float32))
    consts = {
        "cc_identb": np.eye(128, dtype=np.float32).astype(ml_dtypes.bfloat16),
        "cc_identf": np.eye(128, dtype=np.float32),
    }
    mvs = []
    r = np.arange(128)[:, None]
    c = np.arange(512)[None, :]
    for j in range(4):
        mvs.append(np.where(r > c - 128 * j, np.float32(NEG), np.float32(0.0)))
    consts["cc_masks"] = np.concatenate(mvs, axis=1).astype(ml_dtypes.bfloat16)

    shared = {}
    for k, v in inputs.items():
        if k == "L0_attn_x":
            continue
        shared[k] = np.ascontiguousarray(np.asarray(v, dtype=np.float32))

    in_maps = []
    for b in range(8):
        m = {"xcore": np.ascontiguousarray(x[b])}
        m.update(shared)
        m.update(consts)
        in_maps.append(m)

    res = bass_utils.run_bass_kernel_spmd(nc, in_maps, core_ids=list(range(8)))
    out = np.stack([np.asarray(r["out"], dtype=np.float32) for r in res.results], axis=0)
    return out


# version 15
# speedup vs baseline: 1.1264x; 1.1264x over previous
import numpy as np
import ml_dtypes
from contextlib import ExitStack

import concourse.bass as bass
import concourse.mybir as mybir
from concourse.alu_op_type import AluOpType as OP
import concourse.bass_utils as bass_utils

F32 = mybir.dt.float32
BF16 = mybir.dt.bfloat16
AF = mybir.ActivationFunctionType
AX = mybir.AxisListType

D = 1024
T = 2048
NT = 16
NQS = 4
NKC = 8
NPAIR = 8
NE = 8
HID = 4096
LN_EPS = 1e-5
NEG = -1.0e9


class Res:
    __slots__ = ("ready", "users")

    def __init__(self):
        self.ready = None
        self.users = []


class KB:
    ENGS = ("t", "v", "s", "g", "q")

    def __init__(self):
        self.ops = {e: [] for e in self.ENGS}
        self.cnt = {e: 0 for e in self.ENGS}
        self.waited = {e: {} for e in self.ENGS}
        self.dcnt = {}

    def em(self, e, fn, inc=0, serial=True):
        if e in ("v", "s"):
            if serial and self.cnt[e] > 0:
                self.ops[e].append((("WAIT", e, self.cnt[e]), 0))
            inc = max(inc, 1)
        self.ops[e].append((fn, inc))
        if inc:
            self.cnt[e] += inc
            return (e, self.cnt[e])
        return None

    def emd(self, e, fn, dkey):
        self.dcnt[dkey] = self.dcnt.get(dkey, 0) + 16
        self.ops[e].append((("DMA", fn, dkey), 0))
        return (dkey, self.dcnt[dkey])

    def wt(self, e, tok):
        if tok is None:
            return
        te, val = tok
        if te == e:
            return
        if self.waited[e].get(te, -1) >= val:
            return
        self.waited[e][te] = val
        self.ops[e].append((("WAIT", te, val), 0))

    def consume(self, e, res):
        r = res.ready
        for t in (r if isinstance(r, list) else [r]):
            self.wt(e, t)

    def used(self, res, tok):
        if tok is not None:
            res.users.append(tok)

    def produce(self, res, tok):
        res.ready = tok
        res.users = []

    def reuse(self, e, res):
        for u in res.users:
            self.wt(e, u)
        if not res.users:
            r = res.ready
            for t in (r if isinstance(r, list) else [r]):
                self.wt(e, t)


def build_nc():
    nc = bass.Bass("TRN2", target_bir_lowering=False)

    def din(name, shape, dt=F32):
        return nc.dram_tensor(name, shape, dt, kind="ExternalInput").ap()

    x_h = din("xcore", [T, D])
    aW = {w: din("L0_attn_W" + w, [D, D]) for w in "qkvo"}
    ab = {w: din("L0_attn_b" + w, [D]) for w in "qkvo"}
    fW = {w: din("L0_ffn_W" + w, [D, D]) for w in "qkvp"}
    fb = {w: din("L0_ffn_b" + w, [D]) for w in "qkvp"}
    g1_h = din("L0_ffn_ln1_g", [D])
    b1_h = din("L0_ffn_ln1_b", [D])
    g2_h = din("L0_ffn_ln2_g", [D])
    b2_h = din("L0_ffn_ln2_b", [D])
    gw_h = din("L0_ffn_gate_W", [D, NE])
    gb_h = din("L0_ffn_gate_b", [NE])
    w1_h = din("L0_ffn_exp_W1", [NE, D, HID])
    eb1_h = din("L0_ffn_exp_b1", [NE, HID])
    w2_h = din("L0_ffn_exp_W2", [NE, HID, D])
    eb2_h = din("L0_ffn_exp_b2", [NE, D])
    identb_h = din("cc_identb", [128, 128], BF16)
    identf_h = din("cc_identf", [128, 128])
    masks_h = din("cc_masks", [128, 2048], BF16)

    out_h = nc.dram_tensor("out", [T, D], F32, kind="ExternalOutput").ap()
    x1s_h = nc.dram_tensor("x1s", [T, D], F32, kind="Internal").ap()
    atts_h = nc.dram_tensor("atts", [T, D], F32, kind="Internal").ap()

    x_r = x_h.rearrange("(t p) d -> t p d", p=128)
    out_r = out_h.rearrange("(t p) d -> t p d", p=128)
    x1s_r = x1s_h.rearrange("(t p) d -> t p d", p=128)
    atts_r = atts_h.rearrange("(t p) d -> t p d", p=128)

    with ExitStack() as ctx:
        def sbt(name, shape, dt):
            return ctx.enter_context(nc.sbuf_tensor(name, shape, dt))

        def psb(name, shape, dt):
            return ctx.enter_context(nc.psum_tensor(name, shape, dt))

        T1 = sbt("T1", [128, 16384], BF16)
        T2 = sbt("T2", [128, 16384], BF16)
        T3 = sbt("T3", [128, 16384], BF16)
        T4 = sbt("T4", [128, 16512], BF16)
        T5 = sbt("T5", [128, 8192], F32)
        T6 = sbt("T6", [128, 8192], BF16)
        T7 = sbt("T7", [128, 2048], F32)
        T8 = sbt("T8", [128, 2048], F32)
        T9 = sbt("T9", [128, 256], BF16)
        T10 = sbt("T10", [128, 256], F32)
        PPB = sbt("PPB", [128, 96], F32)
        GWSB = sbt("GWSB", [128, 64], BF16)
        B2R = sbt("B2R", [1, 2048], F32)
        RSTG = sbt("RSTG", [1, 1024], F32)

        pw = [psb("pw%d" % i, [128, 1024], F32) for i in range(3)]
        pb = [pw[0][:, 0:512], pw[0][:, 512:1024], pw[1][:, 0:512], pw[1][:, 512:1024],
              pw[2][:, 0:512], pw[2][:, 512:1024]]
        pbE = psb("pbE", [128, 512], F32)
        pbO = psb("pbO", [128, 512], F32)
        pt0f = pb[5]
        ptf = [pb[4], pb[5]]

        sems = {e: ctx.enter_context(nc.semaphore("sem_" + e)) for e in KB.ENGS}
        kb = KB()

        identb = T9[:, 0:128]
        ones_b = T9[:, 128:256]
        identf = T10[:, 0:128]
        ones_f = T10[:, 128:256]
        T5b = T5[:].bitcast(BF16)
        wo_sb = T5b[:, 0:8192]
        sscr = [T5[:, 4096:4608], T5[:, 4608:5120]]
        scrY = T5[:, 5120:6144]
        bvb = T5[:, 6144:7168]
        bob = T5[:, 7168:8192]
        pring = [T6[:, i * 1024:(i + 1) * 1024] for i in range(4)]
        ctxsb = T6[:, 4096:8192]
        scrZ = T6[:, 0:2048].bitcast(F32)
        ctxZf = T6[:, 4096:8192].bitcast(F32)
        masksA = T2[:, 12288:14336]
        sscrW = T5[:, 4096:5120]
        sums_sb = T8[:, 0:512]
        lnscr = T8[:, 512:1024]
        rb_sb = T8[:, 1024:1536]
        m1s1 = T8[:, 1536:1568]
        wfull = T8[:, 1600:1728]
        gbb = T8[:, 1728:1736]
        gsc = T8[:, 1736:1800]
        bnsc = T8[:, 1800:1812]
        bn2 = T8[:, 1812:1814]
        tstat = T8[:, 1814:1824]
        b1pp = [T8[:, 1824 + 32 * i: 1856 + 32 * i] for i in range(2)]
        CONSTS = T8[:, 1888:1896]
        stg = T8[:, 1904:2032]
        T2half = T2[:, 12288:16384]
        qslab = T2[:, 8192:12288]
        wq_sb = T2[:, 0:8192]
        T3f = T3[:].bitcast(F32)
        T7b = T7[:].bitcast(BF16)
        u_lo = T4[:].bitcast(F32)

        def u_sl(tt, c0, c1):
            if tt < 8:
                return u_lo[:, tt * 1024 + c0: tt * 1024 + c1]
            return T5[:, (tt - 8) * 1024 + c0: (tt - 8) * 1024 + c1]

        r_pb = [Res() for _ in range(6)]
        r_pbE = Res()
        r_pbO = Res()
        r_pt = [r_pb[4], r_pb[5]]
        r_pring = [Res() for _ in range(4)]
        r_xr = [Res(), Res()]
        r_afm = [Res() for _ in range(NQS)]
        r_kfm = [Res() for _ in range(NPAIR)]
        r_vsb = [Res() for _ in range(NT)]
        r_qsl = Res()
        r_ctx = [Res() for _ in range(NPAIR)]
        r_wq = Res()
        r_whalfA = Res()
        r_whalfB = Res()
        r_wo = Res()
        r_sums = Res()
        r_scrY = Res()
        r_sscr = [Res(), Res()]
        r_bvb = Res()
        r_bob = Res()
        r_w2 = [Res(), Res(), Res()]
        r_h2 = Res()
        r_gmh = [Res() for _ in range(16)]
        r_b2r = [Res(), Res()]
        r_b1p = [Res(), Res()]
        r_stg = Res()
        r_zr = [Res(), Res()]
        r_w1r = [Res(), Res()]
        r_rstg = Res()

        dkeys = {}
        anon_ctr = [0]

        def dma(e, dst, src, res=None, waits=(), dkey=None):
            for w in waits:
                kb.wt(e, w)
            if res is not None:
                kb.reuse(e, res)
            if dkey is None:
                if res is not None:
                    dkey = dkeys.setdefault(id(res), "d%d" % len(dkeys))
                else:
                    anon_ctr[0] += 1
                    dkey = "a%d" % anon_ctr[0]
            tok = kb.emd(e, lambda E, d=dst, s=src: E.dma_start(out=d, in_=s), dkey)
            if res is not None:
                kb.produce(res, tok)
            return tok

        ln_prev_s = [None]

        def ln_stats(src1024, waits=()):
            for w in waits:
                kb.wt("v", w)
            kb.wt("v", ln_prev_s[0])
            kb.em("v", lambda E, s=src1024: E.bn_stats(out=bnsc[:, 0:6], in_=s[:, 0:512]))
            kb.em("v", lambda E, s=src1024: E.bn_stats(out=bnsc[:, 6:12], in_=s[:, 512:1024]))
            tv = kb.em("v", lambda E: E.bn_aggr(out=bn2[:], in_=bnsc[:]), inc=1)
            kb.wt("s", tv)
            kb.em("s", lambda E: E.activation(out=tstat[:, 0:1], in_=bn2[:, 1:2], func=AF.Ln, bias=CONSTS[:, 0:1]))
            ts_ = kb.em("s", lambda E: E.activation(out=tstat[:, 1:2], in_=tstat[:, 0:1], func=AF.Exp, scale=-0.5), inc=1)
            ln_prev_s[0] = ts_
            return bn2[:, 0:1], tstat[:, 1:2], ts_

        def transpose8(src1024, src_tok, evac, src_res=None):
            kb.wt("t", tok_idf)
            kb.wt("t", src_tok)
            for bank in range(2):
                kb.reuse("t", r_pt[bank])
                tk = None
                for j in range(4):
                    c = bank * 4 + j
                    tk = kb.em("t", lambda E, c=c, j=j, b=bank: E.transpose(
                        out=ptf[b][:, j * 128:(j + 1) * 128],
                        in_=src1024[:, c * 128:(c + 1) * 128],
                        identity=identf), inc=(1 if j == 3 else 0))
                kb.produce(r_pt[bank], tk)
                if src_res is not None:
                    kb.used(src_res, tk)
                eng = "v" if bank == 0 else "s"
                kb.wt(eng, tk)
                for j in range(4):
                    c = bank * 4 + j
                    tv = evac(c, ptf[bank][:, j * 128:(j + 1) * 128], eng)
                    kb.used(r_pt[bank], tv)

        def scrz_write(fn):
            for rp in r_pring:
                kb.reuse("v", rp)
            tok = kb.em("v", fn, inc=1)
            for rp in r_pring:
                kb.produce(rp, tok)
            return tok

        def scrz_used(tok):
            for rp in r_pring:
                kb.used(rp, tok)

        tok_idb = dma("q", identb, identb_h)
        tok_idf = dma("q", identf, identf_h)
        ones_tok = kb.em("v", lambda E: E.memset(ones_b[:], 1.0), inc=1)
        ones_tok = kb.em("v", lambda E: E.memset(ones_f[:], 1.0), inc=1)
        kb.em("v", lambda E: E.memset(CONSTS[:, 0:1], LN_EPS))
        consts_tok = kb.em("v", lambda E: E.memset(CONSTS[:, 1:2], 1.0), inc=1)
        vaug4 = T4[:, 0:16512].rearrange("p (t pr c) -> p t pr c", pr=8, c=129)
        vones_tok = kb.em("v", lambda E: E.memset(vaug4[:, :, :, 64:65], 1.0), inc=1)

        pp_loads = [
            (ab["q"], PPB[:, 0:8]), (ab["k"], PPB[:, 8:16]),
            (fb["q"], PPB[:, 16:24]), (fb["k"], PPB[:, 24:32]),
            (g1_h, PPB[:, 32:40]), (b1_h, PPB[:, 40:48]),
            (g2_h, PPB[:, 48:56]), (b2_h, PPB[:, 56:64]),
        ]
        for src, dstcols in pp_loads:
            tokd = dma("q", stg[0:8, :], src.rearrange("(a p) -> a p", p=128), res=r_stg)
            kb.wt("t", tokd)
            kb.wt("t", tok_idf)
            kb.reuse("t", r_pb[5])
            tokt = kb.em("t", lambda E: E.transpose(
                out=pb[5][:, 0:8], in_=stg[0:8, :], identity=identf[0:8, 0:8]), inc=1)
            kb.produce(r_pb[5], tokt)
            kb.wt("v", tokt)
            tokv = kb.em("v", lambda E, dc=dstcols: E.tensor_copy(out=dc, in_=pb[5][:, 0:8]), inc=1)
            kb.used(r_pb[5], tokv)
            kb.used(r_stg, tokv)

        tok_gw = dma("g", GWSB[:, 0:64].rearrange("p (kc j) -> p kc j", j=8),
                     gw_h.rearrange("(kc p) j -> p kc j", p=128))
        tokd = dma("q", RSTG[0:1, 0:8], gb_h.rearrange("(a j) -> a j", a=1), res=r_rstg)
        kb.wt("t", tokd)
        kb.wt("t", ones_tok)
        kb.reuse("t", r_pb[5])
        tokt = kb.em("t", lambda E: E.matmul(out=pb[5][:, 0:8], lhsT=ones_f[0:1, 0:128],
                                             rhs=RSTG[0:1, 0:8], start=True, stop=True), inc=1)
        kb.produce(r_pb[5], tokt)
        kb.wt("v", tokt)
        tokv = kb.em("v", lambda E: E.tensor_copy(out=gbb, in_=pb[5][:, 0:8]), inc=1)
        kb.used(r_pb[5], tokv)
        kb.used(r_rstg, tokv)

        def load_bias_bcast(src_h, dst, res):
            tokd = dma("q", RSTG[0:1, :], src_h.rearrange("(a j) -> a j", a=1), res=r_rstg)
            kb.wt("t", tokd)
            kb.wt("t", ones_tok)
            toks = []
            for half in range(2):
                kb.reuse("t", r_pb[4 + half])
                tk = kb.em("t", lambda E, h=half: E.matmul(
                    out=pb[4 + h][:], lhsT=ones_f[0:1, 0:128],
                    rhs=RSTG[0:1, h * 512:(h + 1) * 512], start=True, stop=True), inc=1)
                kb.produce(r_pb[4 + half], tk)
                toks.append(tk)
            kb.reuse("v", res)
            vt = None
            for half in range(2):
                kb.wt("v", toks[half])
                vt = kb.em("v", lambda E, h=half, d=dst: E.tensor_copy(
                    out=d[:, h * 512:(h + 1) * 512], in_=pb[4 + h][:]), inc=1)
                kb.used(r_pb[4 + half], vt)
                kb.used(r_rstg, vt)
            kb.produce(res, vt)

        kb.em("v", lambda E: E.tensor_scalar(
            out=PPB[:, 64:72], in0=PPB[:, 0:8], scalar1=0.125, scalar2=None, op0=OP.mult))
        tok_bq8 = kb.em("v", lambda E: E.tensor_scalar(
            out=PPB[:, 72:80], in0=PPB[:, 16:24], scalar1=0.125, scalar2=None, op0=OP.mult), inc=1)

        load_bias_bcast(ab["v"], bvb, r_bvb)
        load_bias_bcast(ab["o"], bob, r_bob)

        for tt in range(NT):
            sl = tt % 2
            xr = T7[:, sl * 1024:(sl + 1) * 1024]
            tokd = dma("q", xr, x_r[tt], res=r_xr[sl])
            m, s, tok_s = ln_stats(xr, waits=(tokd,))
            kb.wt("v", tok_s)
            kb.reuse("v", r_scrY)
            ta = kb.em("v", lambda E, xr=xr: E.tensor_scalar(
                out=scrY, in0=xr, scalar1=bn2[:, 0:1], scalar2=tstat[:, 1:2],
                op0=OP.subtract, op1=OP.mult), inc=1)
            kb.produce(r_scrY, ta)
            kb.used(r_xr[sl], ta)

            qs = tt // 4

            def evac_a(c, pap, eng, tt=tt):
                if eng == "v":
                    return kb.em("v", lambda E, c=c, pap=pap: E.tensor_copy(
                        out=T1[:, c * 2048 + tt * 128:(c * 2048 + tt * 128) + 128], in_=pap),
                        inc=1, serial=False)
                return kb.em("s", lambda E, c=c, pap=pap: E.activation(
                    out=T1[:, c * 2048 + tt * 128:(c * 2048 + tt * 128) + 128], in_=pap,
                    func=AF.Identity), inc=1, serial=False)

            transpose8(scrY, ta, evac_a, src_res=r_scrY)
            if tt % 4 == 3:
                kb.produce(r_afm[qs], [("v", kb.cnt["v"]), ("s", kb.cnt["s"])])

        spill_toks = []
        def emit_attention(wq_h, wk_h, wv_h, wout_h, bq8c, bkc, epi, first):
            wsrc = lambda w: w.rearrange("(kc p) d -> p kc d", p=128)
            tok_wq = dma("g", wq_sb.rearrange("p (kc d) -> p kc d", d=1024), wsrc(wq_h), res=r_wq)
            for rr, half_res in ((0, r_whalfA), (1, r_whalfB)):
                dstA = (T2half if rr == 0 else ctxsb).rearrange("p (kc d) -> p kc d", d=1024)
                if rr == 1:
                    for rc in r_ctx:
                        kb.reuse("g", rc)
                dma("g", dstA, wsrc(wk_h)[:, rr * 4:(rr + 1) * 4, :], res=half_res)
            tok_wo = dma("g", wo_sb.rearrange("p (kc d) -> p kc d", d=1024), wsrc(wout_h), res=r_wo)

            for c in range(NPAIR):
                kb.reuse("v", r_kfm[c])
                for tsl in range(4):
                    bank = (c * 4 + tsl) % 4
                    kb.reuse("t", r_pb[bank])
                    kb.consume("t", r_whalfA)
                    kb.consume("t", r_whalfB)
                    kb.consume("t", r_afm[tsl])
                    tk = None
                    for kc in range(NKC):
                        slot = T2half if kc < 4 else ctxsb
                        tk = kb.em("t", lambda E, c=c, tsl=tsl, kc=kc, slot=slot, bank=bank: E.matmul(
                            out=pb[bank][:],
                            lhsT=slot[:, (kc % 4) * 1024 + c * 128:(kc % 4) * 1024 + c * 128 + 128],
                            rhs=T1[:, kc * 2048 + tsl * 512:kc * 2048 + tsl * 512 + 512],
                            start=(kc == 0), stop=(kc == NKC - 1)), inc=(1 if kc == NKC - 1 else 0))
                    kb.used(r_whalfA, tk)
                    kb.used(r_whalfB, tk)
                    kb.used(r_afm[tsl], tk)
                    kb.produce(r_pb[bank], tk)
                    kb.wt("v", tk)
                    tv = kb.em("v", lambda E, c=c, tsl=tsl, bank=bank: E.tensor_scalar(
                        out=T3[:, c * 2048 + tsl * 512:c * 2048 + tsl * 512 + 512],
                        in0=pb[bank][:], scalar1=bkc[:, c:c + 1], scalar2=None, op0=OP.add), inc=1, serial=False)
                    kb.used(r_pb[bank], tv)
                kb.produce(r_kfm[c], ("v", kb.cnt["v"]))

            for rr, half_res in ((0, r_whalfA), (1, r_whalfB)):
                dstA = (T2half if rr == 0 else ctxsb).rearrange("p (kc d) -> p kc d", d=1024)
                dma("g", dstA, wsrc(wv_h)[:, rr * 4:(rr + 1) * 4, :], res=half_res)
            kb.consume("v", r_bvb)
            for tt in range(NT):
                kb.reuse("v", r_vsb[tt])
                last_v = None
                for dsh in range(2):
                    bank = (tt * 2 + dsh) % 4
                    kb.reuse("t", r_pb[bank])
                    tk = None
                    for kc in range(NKC):
                        slot = T2half if kc < 4 else ctxsb
                        res_w = r_whalfA if kc < 4 else r_whalfB
                        kb.consume("t", res_w)
                        kb.consume("t", r_afm[tt // 4])
                        tk = kb.em("t", lambda E, tt=tt, dsh=dsh, kc=kc, slot=slot, bank=bank: E.matmul(
                            out=pb[bank][:],
                            lhsT=T1[:, kc * 2048 + tt * 128:kc * 2048 + tt * 128 + 128],
                            rhs=slot[:, (kc % 4) * 1024 + dsh * 512:(kc % 4) * 1024 + dsh * 512 + 512],
                            start=(kc == 0), stop=(kc == NKC - 1)), inc=(1 if kc == NKC - 1 else 0))
                        if kc == NKC - 1:
                            kb.used(r_whalfA, tk)
                            kb.used(r_whalfB, tk)
                            kb.used(r_afm[tt // 4], tk)
                    kb.produce(r_pb[bank], tk)
                    kb.wt("v", tk)
                    kb.wt("v", vones_tok)
                    base = tt * 1032 + dsh * 4 * 129
                    vdst = T4[:, base: base + 516].rearrange("p (pr c) -> p pr c", c=129)
                    psrc = pb[bank][:].rearrange("p (pr c) -> p pr c", c=128)
                    bsrc = bvb[:, dsh * 512:(dsh + 1) * 512].rearrange("p (pr c) -> p pr c", c=128)
                    for half in range(2):
                        tv = kb.em("v", lambda E, vdst=vdst, psrc=psrc, bsrc=bsrc, h=half: E.scalar_tensor_tensor(
                            out=vdst[:, :, h * 65:h * 65 + 64],
                            in0=psrc[:, :, 0:64] if h == 0 else psrc[:, :, 64:128],
                            scalar=1.0,
                            in1=bsrc[:, :, 0:64] if h == 0 else bsrc[:, :, 64:128],
                            op0=OP.mult, op1=OP.add), inc=1, serial=False)
                        kb.used(r_pb[bank], tv)
                        kb.used(r_bvb, tv)
                        last_v = tv
                kb.produce(r_vsb[tt], last_v)

            kb.reuse("q", r_whalfA)
            tok_masks = dma("q", masksA, masks_h, dkey="dmask")

            for qs in range(NQS):
                kb.reuse("v", r_qsl)
                for c in range(NPAIR):
                    bank = c % 4
                    kb.reuse("t", r_pb[bank])
                    kb.consume("t", r_wq)
                    kb.consume("t", r_afm[qs])
                    tk = None
                    for kc in range(NKC):
                        tk = kb.em("t", lambda E, c=c, kc=kc, qs=qs, bank=bank: E.matmul(
                            out=pb[bank][:],
                            lhsT=wq_sb[:, kc * 1024 + c * 128:kc * 1024 + c * 128 + 128],
                            rhs=T1[:, kc * 2048 + qs * 512:kc * 2048 + qs * 512 + 512],
                            start=(kc == 0), stop=(kc == NKC - 1)), inc=(1 if kc == NKC - 1 else 0))
                    kb.used(r_wq, tk)
                    kb.used(r_afm[qs], tk)
                    kb.produce(r_pb[bank], tk)
                    kb.wt("s", tk)
                    kb.wt("s", tok_bq8)
                    tv = kb.em("s", lambda E, c=c, bank=bank: E.activation(
                        out=qslab[:, c * 512:(c + 1) * 512], in_=pb[bank][:],
                        func=AF.Identity, bias=bq8c[:, c:c + 1], scale=0.125), inc=1, serial=False)
                    kb.used(r_pb[bank], tv)
                kb.produce(r_qsl, ("s", kb.cnt["s"]))

                nkt = 4 * qs + 4
                def emit_pair_epilogue(p, rbT, rbb):
                        kb.wt("v", r_pbE.ready)
                        kb.wt("v", r_pbO.ready)
                        kb.reuse("v", r_sums)
                        kb.em("v", lambda E: E.tensor_copy(out=sums_sb[64:65, :], in_=pbE[64:65, :]))
                        tv = kb.em("v", lambda E: E.tensor_copy(out=sums_sb[96:97, :], in_=pbE[96:97, :]), inc=1)
                        kb.produce(r_sums, tv)
                        kb.used(r_pbE, tv)
                        kb.used(r_pbO, tv)
                        kb.wt("t", tv)
                        kb.reuse("t", r_pb[2 * rbT])
                        kb.reuse("t", r_pb[2 * rbT + 1])
                        kb.em("t", lambda E, rbb=rbb: E.matmul(
                            out=rbb[0:64, :], lhsT=ones_f[64:65, 0:64], rhs=sums_sb[64:65, :],
                            start=True, stop=True, tile_position=(64, 0)))
                        tk = kb.em("t", lambda E, rbb=rbb: E.matmul(
                            out=rbb[64:128, :], lhsT=ones_f[96:97, 64:128], rhs=sums_sb[96:97, :],
                            start=True, stop=True, tile_position=(96, 64), skip_group_check=True), inc=1)
                        kb.produce(r_pb[2 * rbT], tk)
                        kb.used(r_sums, tk)
                        kb.wt("s", tk)
                        kb.em("s", lambda E, rbb=rbb: E.activation(out=lnscr, in_=rbb, func=AF.Ln))
                        ts_ = kb.em("s", lambda E: E.activation(out=rb_sb, in_=lnscr, func=AF.Exp, scale=-1.0), inc=1)
                        kb.used(r_pb[2 * rbT], ts_)
                        kb.wt("v", ts_)
                        kb.reuse("v", r_ctx[p])
                        kb.em("v", lambda E, p=p: E.tensor_tensor(
                            out=ctxsb[0:64, p * 512:(p + 1) * 512], in0=pbE[0:64, :],
                            in1=rb_sb[0:64, :], op=OP.mult), serial=False)
                        tv = kb.em("v", lambda E, p=p: E.tensor_tensor(
                            out=ctxsb[64:128, p * 512:(p + 1) * 512], in0=pbO[64:128, :],
                            in1=rb_sb[64:128, :], op=OP.mult), inc=1, serial=False)
                        kb.produce(r_ctx[p], tv)
                        kb.used(r_pbE, tv)
                        kb.used(r_pbO, tv)

                pend_epi = [None]

                for p in range(NPAIR):
                    nb = nkt // 2
                    exp_toks = {}

                    def emit_scores_and_exp(b, p=p, qs=qs, nb=nb):
                        diag = 2 * b >= 4 * qs
                        tsrs = [(p * 2 * nb + 2 * b) % 3, (p * 2 * nb + 2 * b + 1) % 3]
                        for head in range(2):
                            kb.reuse("t", r_pb[2 * tsrs[head]])
                            kb.reuse("t", r_pb[2 * tsrs[head] + 1])
                        kb.consume("t", r_kfm[p])
                        kb.consume("t", r_qsl)
                        tks = [None, None]
                        for j01 in range(2):
                            kt = 2 * b + j01
                            for head in range(2):
                                tsr = tsrs[head]
                                base = head * 64
                                tks[head] = kb.em("t", lambda E, p=p, kt=kt, tsr=tsr, j01=j01, base=base, diag=diag: E.matmul(
                                    out=pw[tsr][:, j01 * 512:(j01 + 1) * 512],
                                    lhsT=T3[base:base + 64, p * 2048 + kt * 128:p * 2048 + kt * 128 + 128],
                                    rhs=qslab[base:base + 64, p * 512:(p + 1) * 512],
                                    start=True, stop=not diag), inc=(0 if diag else 1))
                            if diag:
                                jv = kt - 4 * qs
                                kb.wt("t", tok_idb)
                                kb.wt("t", tok_masks)
                                for head in range(2):
                                    tsr = tsrs[head]
                                    tks[head] = kb.em("t", lambda E, tsr=tsr, j01=j01, jv=jv: E.matmul(
                                        out=pw[tsr][:, j01 * 512:(j01 + 1) * 512],
                                        lhsT=identb,
                                        rhs=masksA[:, jv * 512:(jv + 1) * 512],
                                        start=False, stop=True), inc=1)
                                    kb.used(r_whalfA, tks[head])
                        for head in range(2):
                            tsr = tsrs[head]
                            tk = tks[head]
                            kb.produce(r_pb[2 * tsr], tk)
                            kb.produce(r_pb[2 * tsr + 1], tk)
                            kb.used(r_kfm[p], tk)
                            kb.used(r_qsl, tk)
                            slot = (head * 2) + b % 2
                            kb.wt("s", tk)
                            kb.reuse("s", r_pring[slot])
                            te = kb.em("s", lambda E, slot=slot, tsr=tsr: E.activation(
                                out=pring[slot], in_=pw[tsr][:], func=AF.Exp), inc=1, serial=False)
                            kb.used(r_pb[2 * tsr], te)
                            kb.used(r_pb[2 * tsr + 1], te)
                            kb.produce(r_pring[slot], te)
                            exp_toks[2 * b + head] = (slot, te)

                    def emit_ctx(b, p=p, nb=nb, nkt=nkt):
                        slot_e = exp_toks[2 * b][0]
                        slot_o = exp_toks[2 * b + 1][0]
                        if b == 0:
                            kb.reuse("t", r_pbE)
                            kb.reuse("t", r_pbO)
                        kb.consume("t", r_pring[slot_e])
                        kb.consume("t", r_pring[slot_o])
                        for j01 in range(2):
                            kt = 2 * b + j01
                            laste = (kt == nkt - 1)
                            kb.consume("t", r_vsb[kt])
                            pr_e = pring[slot_e][:, j01 * 512:(j01 + 1) * 512]
                            pr_o = pring[slot_o][:, j01 * 512:(j01 + 1) * 512]
                            tk_e = kb.em("t", lambda E, p=p, kt=kt, pr=pr_e, nkt=nkt: E.matmul(
                                out=pbE[0:65, :],
                                lhsT=T4[:, kt * 1032 + p * 129: kt * 1032 + p * 129 + 65],
                                rhs=pr, start=(kt == 0), stop=(kt == nkt - 1),
                                skip_group_check=True), inc=1)
                            kb.used(r_pring[slot_e], tk_e)
                            tk_s = kb.em("t", lambda E, kt=kt, pr=pr_o, nkt=nkt: E.matmul(
                                out=pbE[96:97, :], lhsT=ones_b[:, 0:1], rhs=pr,
                                start=(kt == 0), stop=(kt == nkt - 1),
                                tile_position=(0, 96), skip_group_check=True), inc=1)
                            tk_o = kb.em("t", lambda E, p=p, kt=kt, pr=pr_o, nkt=nkt: E.matmul(
                                out=pbO[64:128, :],
                                lhsT=T4[:, kt * 1032 + p * 129 + 65: kt * 1032 + p * 129 + 129],
                                rhs=pr, start=(kt == 0), stop=(kt == nkt - 1),
                                tile_position=(0, 64), skip_group_check=True), inc=1)
                            kb.used(r_pring[slot_o], tk_o)
                            kb.used(r_vsb[kt], tk_o)
                            if laste:
                                kb.produce(r_pbE, tk_s)
                                kb.produce(r_pbO, tk_o)

                    for b in range(nb):
                        if b == 1 and pend_epi[0] is not None:
                            pend_epi[0]()
                            pend_epi[0] = None
                        emit_scores_and_exp(b)
                        if b >= 1:
                            emit_ctx(b - 1)
                    emit_ctx(nb - 1)
                    rbT = (p * 2 * nb + 2 * nb + 2) % 3
                    pend_epi[0] = (lambda p=p, rbT=rbT: emit_pair_epilogue(p, rbT, pw[rbT][:, 0:512]))
                if pend_epi[0] is not None:
                    pend_epi[0]()
                    pend_epi[0] = None
                for i in range(4):
                    tt = qs * 4 + i
                    banks = (2 * (i % 2), 2 * (i % 2) + 1)
                    toks = []
                    for ds in range(2):
                        bank = banks[ds]
                        kb.reuse("t", r_pb[bank])
                        kb.consume("t", r_wo)
                        tk = None
                        for c in range(NPAIR):
                            kb.consume("t", r_ctx[c])
                            tk = kb.em("t", lambda E, c=c, i=i, ds=ds, bank=bank: E.matmul(
                                out=pb[bank][:],
                                lhsT=ctxsb[:, c * 512 + i * 128:c * 512 + i * 128 + 128],
                                rhs=wo_sb[:, c * 1024 + ds * 512:c * 1024 + ds * 512 + 512],
                                start=(c == 0), stop=(c == NPAIR - 1)), inc=(1 if c == NPAIR - 1 else 0))
                        for c in range(NPAIR):
                            kb.used(r_ctx[c], tk)
                        kb.used(r_wo, tk)
                        kb.produce(r_pb[bank], tk)
                        toks.append(tk)
                    epi(tt, banks, toks)

        def epi1(tt, banks, toks):
            qs = tt // 4
            kb.consume("v", r_bob)
            kb.reuse("v", r_scrY)
            ty = None
            for ds in range(2):
                kb.wt("v", toks[ds])
                ty = kb.em("v", lambda E, ds=ds, b=banks[ds]: E.scalar_tensor_tensor(
                    out=scrY[:, ds * 512:(ds + 1) * 512], in0=pb[b][:], scalar=1.0,
                    in1=bob[:, ds * 512:(ds + 1) * 512], op0=OP.mult, op1=OP.add), inc=1)
                kb.used(r_pb[banks[ds]], ty)
                kb.used(r_bob, ty)
            kb.produce(r_scrY, ty)
            m, s, tok_s = ln_stats(scrY)
            sl = tt % 2
            xr = T7[:, sl * 1024:(sl + 1) * 1024]
            tokd = dma("q", xr, x_r[tt], res=r_xr[sl])
            kb.wt("v", tok_s)
            tz = scrz_write(lambda E: E.tensor_scalar(
                out=scrZ, in0=scrY, scalar1=bn2[:, 0:1], scalar2=tstat[:, 1:2],
                op0=OP.subtract, op1=OP.mult))
            kb.used(r_scrY, tz)
            kb.wt("v", tokd)
            tx1 = kb.em("v", lambda E, xr=xr: E.tensor_tensor(
                out=xr, in0=scrZ, in1=xr, op=OP.add), inc=1)
            scrz_used(tx1)
            tok_sp = dma("q", x1s_r[tt], xr, waits=(tx1,))
            spill_toks.append(tok_sp)
            kb.used(r_xr[sl], tok_sp)
            m, s, tok_s = ln_stats(xr, waits=(tx1,))
            kb.wt("v", tok_s)
            kb.em("v", lambda E, tt=tt: E.tensor_copy(out=m1s1[:, tt:tt + 1], in_=bn2[:, 0:1]))
            tms = kb.em("v", lambda E, tt=tt: E.tensor_copy(out=m1s1[:, 16 + tt:17 + tt], in_=tstat[:, 1:2]), inc=1)
            kb.reuse("v", r_scrY)
            t1 = kb.em("v", lambda E, xr=xr, tt=tt: E.tensor_scalar(
                out=scrY, in0=xr, scalar1=m1s1[:, tt:tt + 1], scalar2=m1s1[:, 16 + tt:17 + tt],
                op0=OP.subtract, op1=OP.mult), inc=1)
            kb.produce(r_scrY, t1)
            kb.used(r_xr[sl], t1)

            def evac_h1(c, pap, eng, tt=tt):
                if eng == "v":
                    return kb.em("v", lambda E, c=c, pap=pap: E.tensor_scalar(
                        out=T1[:, c * 2048 + tt * 128:c * 2048 + tt * 128 + 128], in0=pap,
                        scalar1=PPB[:, 32 + c:33 + c], scalar2=PPB[:, 40 + c:41 + c],
                        op0=OP.mult, op1=OP.add), inc=1, serial=False)
                return kb.em("s", lambda E, c=c, pap=pap: E.activation(
                    out=T1[:, c * 2048 + tt * 128:c * 2048 + tt * 128 + 128], in_=pap,
                    func=AF.Identity, scale=PPB[:, 32 + c:33 + c],
                    bias=PPB[:, 40 + c:41 + c]), inc=1, serial=False)

            transpose8(scrY, t1, evac_h1, src_res=r_scrY)
            if tt % 4 == 3:
                kb.produce(r_afm[qs], [("v", kb.cnt["v"]), ("s", kb.cnt["s"])])

        def epi2(tt, banks, toks):
            kb.consume("v", r_bob)
            if tt % 2 == 0:
                kb.reuse("v", r_scrY)
                dstw = scrY
            else:
                dstw = scrZ
            ta = None
            for ds in range(2):
                kb.wt("v", toks[ds])
                if tt % 2 == 1 and ds == 0:
                    for rp in r_pring:
                        kb.reuse("v", rp)
                ta = kb.em("v", lambda E, ds=ds, b=banks[ds], dstw=dstw: E.scalar_tensor_tensor(
                    out=dstw[:, ds * 512:(ds + 1) * 512], in0=pb[b][:], scalar=1.0,
                    in1=bob[:, ds * 512:(ds + 1) * 512], op0=OP.mult, op1=OP.add), inc=1)
                kb.used(r_pb[banks[ds]], ta)
                kb.used(r_bob, ta)
            if tt % 2 == 0:
                kb.produce(r_scrY, ta)
            else:
                for rp in r_pring:
                    kb.produce(rp, ta)
            tok_sp = dma("q", atts_r[tt], dstw, waits=(ta,))
            spill_toks.append(tok_sp)
            if tt % 2 == 0:
                kb.used(r_scrY, tok_sp)
            else:
                scrz_used(tok_sp)

        emit_attention(aW["q"], aW["k"], aW["v"], aW["o"], PPB[:, 64:72], PPB[:, 8:16], epi1, True)
        load_bias_bcast(fb["v"], bvb, r_bvb)
        load_bias_bcast(fb["p"], bob, r_bob)
        emit_attention(fW["q"], fW["k"], fW["v"], fW["p"], PPB[:, 72:80], PPB[:, 24:32], epi2, False)

        for rs in r_vsb + [r_wo, r_bvb, r_bob, r_scrY, r_sscr[0], r_sscr[1]]:
            kb.reuse("v", rs)
        for rs in r_ctx + r_pring:
            kb.reuse("v", rs)
        att_slots = [ctxZf[:, 0:1024], ctxZf[:, 1024:2048]]
        r_att = [Res(), Res()]
        for _tok in spill_toks:
            kb.wt("q", _tok)
        u_toks = [None] * NT

        w2buf = [T2[:, 0:8192], T2[:, 8192:16384], T6[:, 0:8192]]
        h2f = T1
        gmv = T3
        w1r = [T7b[:, 0:2048], T7b[:, 2048:4096]]

        def w2_load(idx):
            e, hq = divmod(idx, 4)
            buf = w2buf[idx % 3]
            res = r_w2[idx % 3]
            if idx == 0:
                kb.reuse("g", r_wq)
            if idx == 1:
                kb.reuse("g", r_qsl)
                kb.reuse("g", r_whalfA)
            if idx == 2:
                for rs in r_pring + r_ctx + [r_att[0], r_att[1]]:
                    kb.reuse("g", rs)
            src = w2_h[e, hq * 1024:(hq + 1) * 1024, :].rearrange("(hc p) d -> p hc d", p=128)
            return dma("g", buf.rearrange("p (hc d) -> p hc d", d=1024), src, res=res)

        w2_load(0)
        w2_load(1)

        def eb_load(e):
            sl = e % 2
            dma("q", B2R[0:1, sl * 1024:(sl + 1) * 1024], eb2_h[e:e + 1, :], res=r_b2r[sl])
            tokb = dma("q", stg[0:32, :], eb1_h[e].rearrange("(a p) -> a p", p=128), res=r_stg)
            kb.wt("t", tokb)
            kb.reuse("t", r_pb[5])
            tokt = kb.em("t", lambda E: E.transpose(
                out=pt0f[:, 0:32], in_=stg[0:32, :], identity=identf[0:32, 0:32]), inc=1)
            kb.produce(r_pb[5], tokt)
            kb.wt("v", tokt)
            kb.reuse("v", r_b1p[sl])
            tokv = kb.em("v", lambda E, sl=sl: E.tensor_copy(out=b1pp[sl], in_=pt0f[:, 0:32]), inc=1)
            kb.used(r_pb[5], tokv)
            kb.used(r_stg, tokv)
            kb.produce(r_b1p[sl], tokv)

        r_t2s = Res()
        kb.reuse("v", r_h2)
        for rs in r_vsb + [r_wo, r_bvb, r_bob, r_scrY, r_sscr[0], r_sscr[1]]:
            kb.reuse("v", rs)
        for rs in r_pring + r_ctx + r_kfm + [r_att[0], r_att[1]]:
            kb.reuse("v", rs)
        kb.consume("t", r_h2)
        kb.wt("t", tok_gw)
        gate_prev_s = None
        gbank = [pbE, pbO]
        gres = [r_pbE, r_pbO]
        for tt in range(NT):
            sl = tt % 2
            xr = T7[:, sl * 1024:(sl + 1) * 1024]
            tokx = dma("q", xr, x1s_r[tt], res=r_xr[sl])
            if tt < 2:
                for rc in r_ctx:
                    kb.reuse("q", rc)
            toka = dma("q", att_slots[sl], atts_r[tt], res=r_att[sl])
            kb.wt("v", tokx)
            th = kb.em("v", lambda E, xr=xr, tt=tt: E.tensor_scalar(
                out=u_sl(tt, 0, 1024), in0=xr, scalar1=m1s1[:, tt:tt + 1],
                scalar2=m1s1[:, 16 + tt:17 + tt], op0=OP.subtract, op1=OP.mult), inc=1)
            kb.used(r_xr[sl], th)
            kb.wt("v", toka)
            tu = kb.em("v", lambda E, tt=tt, sl=sl: E.tensor_tensor(
                out=u_sl(tt, 0, 1024), in0=att_slots[sl], in1=u_sl(tt, 0, 1024), op=OP.add), inc=1)
            kb.used(r_att[sl], tu)
            u_toks[tt] = tu
            m, s, tok_s = ln_stats(u_sl(tt, 0, 1024), waits=(tu,))
            kb.wt("v", tok_s)
            if tt % 4 == 0:
                kb.reuse("v", r_afm[tt // 4])
            t2stage = T3f[:, 0:1024]
            kb.reuse("v", r_t2s)
            t2tok = kb.em("v", lambda E, tt=tt: E.tensor_scalar(
                out=t2stage, in0=u_sl(tt, 0, 1024), scalar1=bn2[:, 0:1], scalar2=tstat[:, 1:2],
                op0=OP.subtract, op1=OP.mult), inc=1)
            kb.produce(r_t2s, t2tok)

            def evac_h2(c, pap, eng, tt=tt):
                if eng == "v":
                    return kb.em("v", lambda E, c=c, pap=pap, tt=tt: E.tensor_scalar(
                        out=h2f[:, c * 2048 + tt * 128:c * 2048 + tt * 128 + 128], in0=pap,
                        scalar1=PPB[:, 48 + c:49 + c], scalar2=PPB[:, 56 + c:57 + c],
                        op0=OP.mult, op1=OP.add), inc=1, serial=False)
                return kb.em("s", lambda E, c=c, pap=pap, tt=tt: E.activation(
                    out=h2f[:, c * 2048 + tt * 128:c * 2048 + tt * 128 + 128], in_=pap,
                    func=AF.Identity, scale=PPB[:, 48 + c:49 + c],
                    bias=PPB[:, 56 + c:57 + c]), inc=1, serial=False)

            transpose8(t2stage, t2tok, evac_h2, src_res=r_t2s)
            bank = tt % 2
            kb.reuse("t", gres[bank])
            kb.wt("t", ("v", kb.cnt["v"]))
            kb.wt("t", ("s", kb.cnt["s"]))
            tk = None
            for kc in range(NKC):
                tk = kb.em("t", lambda E, tt=tt, kc=kc, bank=bank: E.matmul(
                    out=gbank[bank][:, 0:8],
                    lhsT=h2f[:, kc * 2048 + tt * 128:kc * 2048 + tt * 128 + 128],
                    rhs=GWSB[:, kc * 8:(kc + 1) * 8],
                    start=(kc == 0), stop=(kc == NKC - 1)), inc=(1 if kc == NKC - 1 else 0))
            kb.used(r_h2, tk)
            kb.produce(gres[bank], tk)
            kb.wt("v", tk)
            kb.wt("v", gate_prev_s)
            kb.em("v", lambda E, bank=bank: E.tensor_tensor(
                out=gsc[:, 0:8], in0=gbank[bank][:, 0:8], in1=gbb, op=OP.add))
            tg = kb.em("v", lambda E: E.reduce_max(out=gsc[:, 8:9], in_=gsc[:, 0:8], axis=AX.X), inc=1)
            kb.used(gres[bank], tg)
            kb.em("v", lambda E: E.tensor_scalar(
                out=gsc[:, 9:17], in0=gsc[:, 0:8], scalar1=gsc[:, 8:9], scalar2=None, op0=OP.is_equal))
            kb.em("v", lambda E: E.scalar_tensor_tensor(
                out=gsc[:, 17:25], in0=gsc[:, 9:17], scalar=-1.0e30, in1=gsc[:, 0:8],
                op0=OP.mult, op1=OP.add))
            kb.em("v", lambda E: E.reduce_max(out=gsc[:, 25:26], in_=gsc[:, 17:25], axis=AX.X))
            tg = kb.em("v", lambda E: E.tensor_tensor(
                out=gsc[:, 26:27], in0=gsc[:, 25:26], in1=gsc[:, 8:9], op=OP.subtract), inc=1)
            kb.wt("s", tg)
            kb.em("s", lambda E: E.activation(out=gsc[:, 27:28], in_=gsc[:, 26:27], func=AF.Exp))
            kb.em("s", lambda E: E.activation(out=gsc[:, 28:29], in_=gsc[:, 27:28], func=AF.Ln, bias=CONSTS[:, 1:2]))
            tw1 = kb.em("s", lambda E: E.activation(out=gsc[:, 29:30], in_=gsc[:, 28:29], func=AF.Exp, scale=-1.0), inc=1)
            gate_prev_s = tw1
            kb.wt("v", tw1)
            kb.em("v", lambda E: E.tensor_scalar(
                out=gsc[:, 30:31], in0=gsc[:, 29:30], scalar1=-1.0, scalar2=1.0,
                op0=OP.mult, op1=OP.add))
            kb.em("v", lambda E: E.tensor_scalar(
                out=gsc[:, 32:40], in0=gsc[:, 17:25], scalar1=gsc[:, 25:26], scalar2=None, op0=OP.is_equal))
            kb.em("v", lambda E: E.tensor_scalar(
                out=gsc[:, 40:48], in0=gsc[:, 32:40], scalar1=gsc[:, 30:31], scalar2=None, op0=OP.mult))
            kb.em("v", lambda E, tt=tt: E.scalar_tensor_tensor(
                out=wfull[:, tt * 8:(tt + 1) * 8], in0=gsc[:, 9:17], scalar=gsc[:, 29:30],
                in1=gsc[:, 40:48], op0=OP.mult, op1=OP.add), inc=1)
        kb.produce(r_h2, [("v", kb.cnt["v"]), ("s", kb.cnt["s"])])

        T1f = T1[:].bitcast(F32)
        out_toks = []

        def emit_final(tt):
            sl = tt % 2
            xr = T7[:, sl * 1024:(sl + 1) * 1024]
            kb.reuse("q", r_w1r[sl])
            tokx = dma("q", xr, x1s_r[tt], res=r_xr[sl], waits=(u_toks[tt],))
            m, s, tok_s = ln_stats(u_sl(tt, 0, 1024), waits=(u_toks[tt],))
            zsl = T1f[:, sl * 1024:(sl + 1) * 1024]
            kb.wt("v", tok_s)
            kb.reuse("v", r_zr[sl])
            if tt < 2:
                kb.reuse("v", r_h2)
            tz = kb.em("v", lambda E, tt=tt, zsl=zsl: E.tensor_scalar(
                out=zsl, in0=u_sl(tt, 0, 1024), scalar1=bn2[:, 0:1], scalar2=tstat[:, 1:2],
                op0=OP.subtract, op1=OP.mult), inc=1)
            kb.wt("v", tokx)
            to = kb.em("v", lambda E, xr=xr, zsl=zsl: E.tensor_tensor(
                out=zsl, in0=xr, in1=zsl, op=OP.add), inc=1)
            kb.used(r_xr[sl], to)
            tokout = dma("q", out_r[tt], zsl, waits=(to,))
            out_toks.append(tokout)
            kb.produce(r_zr[sl], tokout)

        g4 = [pbE[:], pbO[:], pb[4], pb[5]]
        gres4 = [r_pbE, r_pbO, r_pb[4], r_pb[5]]
        for rk in r_kfm:
            kb.reuse("s", rk)
        eb_load(0)
        for e in range(NE):
            if e + 1 < NE:
                eb_load(e + 1)
            for hq in range(4):
                idx = e * 4 + hq
                for hcp in range(4):
                    slw = hcp % 2
                    if idx == 0 and hcp < 2:
                        kb.reuse("g", r_xr[slw])
                    src = w1_h[e, :, hq * 1024 + hcp * 256: hq * 1024 + (hcp + 1) * 256]
                    tokw = dma("g", w1r[slw].rearrange("p (kc j) -> p kc j", j=256),
                               src.rearrange("(kc p) j -> p kc j", p=128), res=r_w1r[slw])
                    for sub in range(2):
                        hc = hcp * 2 + sub
                        for ts_ in range(4):
                            bank = (hc * 4 + ts_) % 4
                            kb.reuse("t", r_pb[bank])
                            kb.wt("t", tokw)
                            kb.consume("t", r_h2)
                            tk = None
                            for kc in range(NKC):
                                tk = kb.em("t", lambda E, sub=sub, ts_=ts_, kc=kc, slw=slw, bank=bank: E.matmul(
                                    out=pb[bank][:],
                                    lhsT=w1r[slw][:, kc * 256 + sub * 128:kc * 256 + sub * 128 + 128],
                                    rhs=h2f[:, kc * 2048 + ts_ * 512:kc * 2048 + ts_ * 512 + 512],
                                    start=(kc == 0), stop=(kc == NKC - 1)), inc=(1 if kc == NKC - 1 else 0))
                            kb.used(r_w1r[slw], tk)
                            kb.used(r_h2, tk)
                            kb.produce(r_pb[bank], tk)
                            kb.wt("s", tk)
                            kb.consume("s", r_b1p[e % 2])
                            if ts_ == 0:
                                kb.reuse("s", r_gmh[hc])
                                kb.reuse("s", r_t2s)
                            tg = kb.em("s", lambda E, hc=hc, ts_=ts_, e=e, hq=hq, bank=bank: E.activation(
                                out=gmv[:, hc * 2048 + ts_ * 512:hc * 2048 + ts_ * 512 + 512],
                                in_=pb[bank][:], func=AF.Gelu,
                                bias=b1pp[e % 2][:, hq * 8 + hc:hq * 8 + hc + 1]), inc=1, serial=False)
                            kb.used(r_pb[bank], tg)
                            kb.used(r_b1p[e % 2], tg)
                            if ts_ == 3:
                                kb.produce(r_gmh[hc], tg)
                if idx + 2 < 32:
                    w2_load(idx + 2)
                kb.consume("t", r_w2[idx % 3])
                for tt in range(NT):
                    for ds in range(2):
                        bank = (tt * 2 + ds) % 4
                        kb.reuse("t", gres4[bank])
                        tk = None
                        for hc in range(8):
                            kb.consume("t", r_gmh[hc])
                            tk = kb.em("t", lambda E, tt=tt, ds=ds, hc=hc, bank=bank, idx=idx, hq=hq: E.matmul(
                                out=g4[bank],
                                lhsT=gmv[:, hc * 2048 + tt * 128:hc * 2048 + tt * 128 + 128],
                                rhs=w2buf[idx % 3][:, hc * 1024 + ds * 512:hc * 1024 + ds * 512 + 512],
                                start=(hc == 0), stop=(hc == 7 and hq != 0)), inc=(1 if (hc == 7 and hq != 0) else 0))
                        if hq == 0:
                            kb.consume("t", r_b2r[e % 2])
                            tk = kb.em("t", lambda E, ds=ds, e=e, bank=bank: E.matmul(
                                out=g4[bank], lhsT=ones_f[0:1, 0:128],
                                rhs=B2R[0:1, (e % 2) * 1024 + ds * 512:(e % 2) * 1024 + ds * 512 + 512],
                                start=False, stop=True), inc=1)
                            kb.used(r_b2r[e % 2], tk)
                        for hc in range(8):
                            kb.used(r_gmh[hc], tk)
                        kb.used(r_w2[idx % 3], tk)
                        kb.produce(gres4[bank], tk)
                        kb.wt("v", tk)
                        tv = kb.em("v", lambda E, tt=tt, ds=ds, e=e, bank=bank: E.scalar_tensor_tensor(
                            out=u_sl(tt, ds * 512, ds * 512 + 512), in0=g4[bank],
                            scalar=wfull[:, tt * 8 + e:tt * 8 + e + 1],
                            in1=u_sl(tt, ds * 512, ds * 512 + 512),
                            op0=OP.mult, op1=OP.add), inc=1, serial=False)
                        kb.used(gres4[bank], tv)
                        u_toks[tt] = tv
                        if e == NE - 1 and hq == 3 and ds == 1:
                            emit_final(tt)


        for _tok in out_toks:
            kb.ops["q"].append((("WAIT", _tok[0], _tok[1]), 0))

        for _k in kb.dcnt:
            sems[_k] = ctx.enter_context(nc.semaphore("sem_" + _k))
        block = ctx.enter_context(nc.Block())

        def runner(eng, e):
            for fn, inc in kb.ops[e]:
                if isinstance(fn, tuple):
                    if fn[0] == "WAIT":
                        eng.wait_ge(sems[fn[1]], fn[2])
                    elif fn[0] == "DMA":
                        instr = fn[1](eng)
                        instr.then_inc(sems[fn[2]], 16)
                    continue
                instr = fn(eng)
                if inc:
                    instr.then_inc(sems[e], inc)

        @block.tensor
        def _(eng):
            runner(eng, "t")

        @block.vector
        def _(eng):
            runner(eng, "v")

        @block.scalar
        def _(eng):
            runner(eng, "s")

        @block.gpsimd
        def _(eng):
            runner(eng, "g")

        @block.sync
        def _(eng):
            runner(eng, "q")

    return nc


_NC_CACHE = None


def kernel(**inputs):
    global _NC_CACHE
    if _NC_CACHE is None:
        _NC_CACHE = build_nc()
    nc = _NC_CACHE

    x = np.ascontiguousarray(np.asarray(inputs["L0_attn_x"], dtype=np.float32))
    consts = {
        "cc_identb": np.eye(128, dtype=np.float32).astype(ml_dtypes.bfloat16),
        "cc_identf": np.eye(128, dtype=np.float32),
    }
    mvs = []
    r = np.arange(128)[:, None]
    c = np.arange(512)[None, :]
    for j in range(4):
        mvs.append(np.where(r > c - 128 * j, np.float32(NEG), np.float32(0.0)))
    consts["cc_masks"] = np.concatenate(mvs, axis=1).astype(ml_dtypes.bfloat16)

    shared = {}
    for k, v in inputs.items():
        if k == "L0_attn_x":
            continue
        shared[k] = np.ascontiguousarray(np.asarray(v, dtype=np.float32))

    in_maps = []
    for b in range(8):
        m = {"xcore": np.ascontiguousarray(x[b])}
        m.update(shared)
        m.update(consts)
        in_maps.append(m)

    res = bass_utils.run_bass_kernel_spmd(nc, in_maps, core_ids=list(range(8)))
    out = np.stack([np.asarray(r["out"], dtype=np.float32) for r in res.results], axis=0)
    return out


# version 16
# speedup vs baseline: 1.1423x; 1.0141x over previous
import numpy as np
import ml_dtypes
from contextlib import ExitStack

import concourse.bass as bass
import concourse.mybir as mybir
from concourse.alu_op_type import AluOpType as OP
import concourse.bass_utils as bass_utils

F32 = mybir.dt.float32
BF16 = mybir.dt.bfloat16
AF = mybir.ActivationFunctionType
AX = mybir.AxisListType

D = 1024
T = 2048
NT = 16
NQS = 4
NKC = 8
NPAIR = 8
NE = 8
HID = 4096
LN_EPS = 1e-5
NEG = -1.0e9


class Res:
    __slots__ = ("ready", "users")

    def __init__(self):
        self.ready = None
        self.users = []


class KB:
    ENGS = ("t", "v", "s", "g", "q")

    def __init__(self):
        self.ops = {e: [] for e in self.ENGS}
        self.cnt = {e: 0 for e in self.ENGS}
        self.waited = {e: {} for e in self.ENGS}
        self.dcnt = {}

    def em(self, e, fn, inc=0, serial=True):
        if e in ("v", "s"):
            if serial and self.cnt[e] > 0:
                self.ops[e].append((("WAIT", e, self.cnt[e]), 0))
            inc = max(inc, 1)
        self.ops[e].append((fn, inc))
        if inc:
            self.cnt[e] += inc
            return (e, self.cnt[e])
        return None

    def emd(self, e, fn, dkey):
        self.dcnt[dkey] = self.dcnt.get(dkey, 0) + 16
        self.ops[e].append((("DMA", fn, dkey), 0))
        return (dkey, self.dcnt[dkey])

    def wt(self, e, tok):
        if tok is None:
            return
        te, val = tok
        if te == e:
            return
        if self.waited[e].get(te, -1) >= val:
            return
        self.waited[e][te] = val
        self.ops[e].append((("WAIT", te, val), 0))

    def consume(self, e, res):
        r = res.ready
        for t in (r if isinstance(r, list) else [r]):
            self.wt(e, t)

    def used(self, res, tok):
        if tok is not None:
            res.users.append(tok)

    def produce(self, res, tok):
        res.ready = tok
        res.users = []

    def reuse(self, e, res):
        for u in res.users:
            self.wt(e, u)
        if not res.users:
            r = res.ready
            for t in (r if isinstance(r, list) else [r]):
                self.wt(e, t)


def build_nc():
    nc = bass.Bass("TRN2", target_bir_lowering=False)

    def din(name, shape, dt=F32):
        return nc.dram_tensor(name, shape, dt, kind="ExternalInput").ap()

    x_h = din("xcore", [T, D])
    aW = {w: din("L0_attn_W" + w, [D, D]) for w in "qkvo"}
    ab = {w: din("L0_attn_b" + w, [D]) for w in "qkvo"}
    fW = {w: din("L0_ffn_W" + w, [D, D]) for w in "qkvp"}
    fb = {w: din("L0_ffn_b" + w, [D]) for w in "qkvp"}
    g1_h = din("L0_ffn_ln1_g", [D])
    b1_h = din("L0_ffn_ln1_b", [D])
    g2_h = din("L0_ffn_ln2_g", [D])
    b2_h = din("L0_ffn_ln2_b", [D])
    gw_h = din("L0_ffn_gate_W", [D, NE])
    gb_h = din("L0_ffn_gate_b", [NE])
    w1_h = din("L0_ffn_exp_W1", [NE, D, HID])
    eb1_h = din("L0_ffn_exp_b1", [NE, HID])
    w2_h = din("L0_ffn_exp_W2", [NE, HID, D])
    eb2_h = din("L0_ffn_exp_b2", [NE, D])
    identb_h = din("cc_identb", [128, 128], BF16)
    identf_h = din("cc_identf", [128, 128])
    masks_h = din("cc_masks", [128, 2048], BF16)

    out_h = nc.dram_tensor("out", [T, D], F32, kind="ExternalOutput").ap()
    x1s_h = nc.dram_tensor("x1s", [T, D], F32, kind="Internal").ap()
    atts_h = nc.dram_tensor("atts", [T, D], F32, kind="Internal").ap()

    x_r = x_h.rearrange("(t p) d -> t p d", p=128)
    out_r = out_h.rearrange("(t p) d -> t p d", p=128)
    x1s_r = x1s_h.rearrange("(t p) d -> t p d", p=128)
    atts_r = atts_h.rearrange("(t p) d -> t p d", p=128)

    with ExitStack() as ctx:
        def sbt(name, shape, dt):
            return ctx.enter_context(nc.sbuf_tensor(name, shape, dt))

        def psb(name, shape, dt):
            return ctx.enter_context(nc.psum_tensor(name, shape, dt))

        T1 = sbt("T1", [128, 16384], BF16)
        T2 = sbt("T2", [128, 16384], BF16)
        T3 = sbt("T3", [128, 16384], BF16)
        T4 = sbt("T4", [128, 16512], BF16)
        T5 = sbt("T5", [128, 8192], F32)
        T6 = sbt("T6", [128, 8192], BF16)
        T7 = sbt("T7", [128, 2048], F32)
        T8 = sbt("T8", [128, 2048], F32)
        T9 = sbt("T9", [128, 256], BF16)
        T10 = sbt("T10", [128, 256], F32)
        PPB = sbt("PPB", [128, 96], F32)
        GWSB = sbt("GWSB", [128, 64], BF16)
        B2R = sbt("B2R", [1, 2048], F32)
        RSTG = sbt("RSTG", [1, 1024], F32)

        pw = [psb("pw%d" % i, [128, 1024], F32) for i in range(3)]
        pb = [pw[0][:, 0:512], pw[0][:, 512:1024], pw[1][:, 0:512], pw[1][:, 512:1024],
              pw[2][:, 0:512], pw[2][:, 512:1024]]
        pbE = psb("pbE", [128, 512], F32)
        pbO = psb("pbO", [128, 512], F32)
        pt0f = pb[5]
        ptf = [pb[4], pb[5]]

        sems = {e: ctx.enter_context(nc.semaphore("sem_" + e)) for e in KB.ENGS}
        kb = KB()

        identb = T9[:, 0:128]
        ones_b = T9[:, 128:256]
        identf = T10[:, 0:128]
        ones_f = T10[:, 128:256]
        T5b = T5[:].bitcast(BF16)
        wo_sb = T5b[:, 0:8192]
        sscr = [T5[:, 4096:4608], T5[:, 4608:5120]]
        scrY = T5[:, 5120:6144]
        bvb = T5[:, 6144:7168]
        bob = T5[:, 7168:8192]
        pring = [T6[:, i * 1024:(i + 1) * 1024] for i in range(4)]
        ctxsb = T6[:, 4096:8192]
        scrZ = T6[:, 0:2048].bitcast(F32)
        ctxZf = T6[:, 4096:8192].bitcast(F32)
        masksA = T2[:, 12288:14336]
        sscrW = T5[:, 4096:5120]
        sums_sb = T8[:, 0:512]
        lnscr = T8[:, 512:1024]
        rb_sb = T8[:, 1024:1536]
        m1s1 = T8[:, 1536:1568]
        wfull = T8[:, 1600:1728]
        gbb = T8[:, 1728:1736]
        gsc = T8[:, 1736:1800]
        bnsc = T8[:, 1800:1812]
        bn2 = T8[:, 1812:1814]
        tstat = T8[:, 1814:1824]
        b1pp = [T8[:, 1824 + 32 * i: 1856 + 32 * i] for i in range(2)]
        CONSTS = T8[:, 1888:1896]
        stg = T8[:, 1904:2032]
        T2half = T2[:, 12288:16384]
        qslab = T2[:, 8192:12288]
        wq_sb = T2[:, 0:8192]
        T3f = T3[:].bitcast(F32)
        T7b = T7[:].bitcast(BF16)
        u_lo = T4[:].bitcast(F32)

        def u_sl(tt, c0, c1):
            if tt < 8:
                return u_lo[:, tt * 1024 + c0: tt * 1024 + c1]
            return T5[:, (tt - 8) * 1024 + c0: (tt - 8) * 1024 + c1]

        r_pb = [Res() for _ in range(6)]
        r_pbE = Res()
        r_pbO = Res()
        r_pt = [r_pb[4], r_pb[5]]
        r_pring = [Res() for _ in range(4)]
        r_xr = [Res(), Res()]
        r_afm = [Res() for _ in range(NQS)]
        r_kfm = [Res() for _ in range(NPAIR)]
        r_vsb = [Res() for _ in range(NT)]
        r_qsl = Res()
        r_ctx = [Res() for _ in range(NPAIR)]
        r_wq = Res()
        r_whalfA = Res()
        r_whalfB = Res()
        r_wo = Res()
        r_sums = Res()
        r_scrY = Res()
        r_sscr = [Res(), Res()]
        r_bvb = Res()
        r_bob = Res()
        r_w2 = [Res(), Res(), Res()]
        r_h2 = Res()
        r_gmh = [Res() for _ in range(16)]
        r_b2r = [Res(), Res()]
        r_b1p = [Res(), Res()]
        r_stg = Res()
        r_zr = [Res(), Res()]
        r_w1r = [Res(), Res()]
        r_rstg = Res()

        dkeys = {}
        anon_ctr = [0]

        def dma(e, dst, src, res=None, waits=(), dkey=None):
            for w in waits:
                kb.wt(e, w)
            if res is not None:
                kb.reuse(e, res)
            if dkey is None:
                if res is not None:
                    dkey = dkeys.setdefault(id(res), "d%d" % len(dkeys))
                else:
                    anon_ctr[0] += 1
                    dkey = "a%d" % anon_ctr[0]
            tok = kb.emd(e, lambda E, d=dst, s=src: E.dma_start(out=d, in_=s), dkey)
            if res is not None:
                kb.produce(res, tok)
            return tok

        ln_prev_s = [None]

        def ln_stats(src1024, waits=()):
            for w in waits:
                kb.wt("v", w)
            kb.wt("v", ln_prev_s[0])
            kb.em("v", lambda E, s=src1024: E.bn_stats(out=bnsc[:, 0:6], in_=s[:, 0:512]))
            kb.em("v", lambda E, s=src1024: E.bn_stats(out=bnsc[:, 6:12], in_=s[:, 512:1024]))
            tv = kb.em("v", lambda E: E.bn_aggr(out=bn2[:], in_=bnsc[:]), inc=1)
            kb.wt("s", tv)
            kb.em("s", lambda E: E.activation(out=tstat[:, 0:1], in_=bn2[:, 1:2], func=AF.Ln, bias=CONSTS[:, 0:1]))
            ts_ = kb.em("s", lambda E: E.activation(out=tstat[:, 1:2], in_=tstat[:, 0:1], func=AF.Exp, scale=-0.5), inc=1)
            ln_prev_s[0] = ts_
            return bn2[:, 0:1], tstat[:, 1:2], ts_

        def transpose8(src1024, src_tok, evac, src_res=None, engines=("v", "s")):
            kb.wt("t", tok_idf)
            kb.wt("t", src_tok)
            for bank in range(2):
                kb.reuse("t", r_pt[bank])
                tk = None
                for j in range(4):
                    c = bank * 4 + j
                    tk = kb.em("t", lambda E, c=c, j=j, b=bank: E.transpose(
                        out=ptf[b][:, j * 128:(j + 1) * 128],
                        in_=src1024[:, c * 128:(c + 1) * 128],
                        identity=identf), inc=(1 if j == 3 else 0))
                kb.produce(r_pt[bank], tk)
                if src_res is not None:
                    kb.used(src_res, tk)
                eng = engines[bank]
                kb.wt(eng, tk)
                for j in range(4):
                    c = bank * 4 + j
                    tv = evac(c, ptf[bank][:, j * 128:(j + 1) * 128], eng)
                    kb.used(r_pt[bank], tv)

        def scrz_write(fn):
            for rp in r_pring:
                kb.reuse("v", rp)
            tok = kb.em("v", fn, inc=1)
            for rp in r_pring:
                kb.produce(rp, tok)
            return tok

        def scrz_used(tok):
            for rp in r_pring:
                kb.used(rp, tok)

        tok_idb = dma("q", identb, identb_h)
        tok_idf = dma("q", identf, identf_h)
        ones_tok = kb.em("v", lambda E: E.memset(ones_b[:], 1.0), inc=1)
        ones_tok = kb.em("v", lambda E: E.memset(ones_f[:], 1.0), inc=1)
        kb.em("v", lambda E: E.memset(CONSTS[:, 0:1], LN_EPS))
        consts_tok = kb.em("v", lambda E: E.memset(CONSTS[:, 1:2], 1.0), inc=1)
        vaug4 = T4[:, 0:16512].rearrange("p (t pr c) -> p t pr c", pr=8, c=129)
        vones_tok = kb.em("v", lambda E: E.memset(vaug4[:, :, :, 64:65], 1.0), inc=1)

        pp_loads = [
            (ab["q"], PPB[:, 0:8]), (ab["k"], PPB[:, 8:16]),
            (fb["q"], PPB[:, 16:24]), (fb["k"], PPB[:, 24:32]),
            (g1_h, PPB[:, 32:40]), (b1_h, PPB[:, 40:48]),
            (g2_h, PPB[:, 48:56]), (b2_h, PPB[:, 56:64]),
        ]
        for src, dstcols in pp_loads:
            tokd = dma("q", stg[0:8, :], src.rearrange("(a p) -> a p", p=128), res=r_stg)
            kb.wt("t", tokd)
            kb.wt("t", tok_idf)
            kb.reuse("t", r_pb[5])
            tokt = kb.em("t", lambda E: E.transpose(
                out=pb[5][:, 0:8], in_=stg[0:8, :], identity=identf[0:8, 0:8]), inc=1)
            kb.produce(r_pb[5], tokt)
            kb.wt("v", tokt)
            tokv = kb.em("v", lambda E, dc=dstcols: E.tensor_copy(out=dc, in_=pb[5][:, 0:8]), inc=1)
            kb.used(r_pb[5], tokv)
            kb.used(r_stg, tokv)

        tok_gw = dma("g", GWSB[:, 0:64].rearrange("p (kc j) -> p kc j", j=8),
                     gw_h.rearrange("(kc p) j -> p kc j", p=128))
        tokd = dma("q", RSTG[0:1, 0:8], gb_h.rearrange("(a j) -> a j", a=1), res=r_rstg)
        kb.wt("t", tokd)
        kb.wt("t", ones_tok)
        kb.reuse("t", r_pb[5])
        tokt = kb.em("t", lambda E: E.matmul(out=pb[5][:, 0:8], lhsT=ones_f[0:1, 0:128],
                                             rhs=RSTG[0:1, 0:8], start=True, stop=True), inc=1)
        kb.produce(r_pb[5], tokt)
        kb.wt("v", tokt)
        tokv = kb.em("v", lambda E: E.tensor_copy(out=gbb, in_=pb[5][:, 0:8]), inc=1)
        kb.used(r_pb[5], tokv)
        kb.used(r_rstg, tokv)

        def load_bias_bcast(src_h, dst, res):
            tokd = dma("q", RSTG[0:1, :], src_h.rearrange("(a j) -> a j", a=1), res=r_rstg)
            kb.wt("t", tokd)
            kb.wt("t", ones_tok)
            toks = []
            for half in range(2):
                kb.reuse("t", r_pb[4 + half])
                tk = kb.em("t", lambda E, h=half: E.matmul(
                    out=pb[4 + h][:], lhsT=ones_f[0:1, 0:128],
                    rhs=RSTG[0:1, h * 512:(h + 1) * 512], start=True, stop=True), inc=1)
                kb.produce(r_pb[4 + half], tk)
                toks.append(tk)
            kb.reuse("v", res)
            vt = None
            for half in range(2):
                kb.wt("v", toks[half])
                vt = kb.em("v", lambda E, h=half, d=dst: E.tensor_copy(
                    out=d[:, h * 512:(h + 1) * 512], in_=pb[4 + h][:]), inc=1)
                kb.used(r_pb[4 + half], vt)
                kb.used(r_rstg, vt)
            kb.produce(res, vt)

        kb.em("v", lambda E: E.tensor_scalar(
            out=PPB[:, 64:72], in0=PPB[:, 0:8], scalar1=0.125, scalar2=None, op0=OP.mult))
        tok_bq8 = kb.em("v", lambda E: E.tensor_scalar(
            out=PPB[:, 72:80], in0=PPB[:, 16:24], scalar1=0.125, scalar2=None, op0=OP.mult), inc=1)

        load_bias_bcast(ab["v"], bvb, r_bvb)
        load_bias_bcast(ab["o"], bob, r_bob)

        for tt in range(NT):
            sl = tt % 2
            xr = T7[:, sl * 1024:(sl + 1) * 1024]
            tokd = dma("q", xr, x_r[tt], res=r_xr[sl])
            m, s, tok_s = ln_stats(xr, waits=(tokd,))
            kb.wt("v", tok_s)
            kb.reuse("v", r_scrY)
            ta = kb.em("v", lambda E, xr=xr: E.tensor_scalar(
                out=scrY, in0=xr, scalar1=bn2[:, 0:1], scalar2=tstat[:, 1:2],
                op0=OP.subtract, op1=OP.mult), inc=1)
            kb.produce(r_scrY, ta)
            kb.used(r_xr[sl], ta)

            qs = tt // 4

            def evac_a(c, pap, eng, tt=tt):
                if eng == "v":
                    return kb.em("v", lambda E, c=c, pap=pap: E.tensor_copy(
                        out=T1[:, c * 2048 + tt * 128:(c * 2048 + tt * 128) + 128], in_=pap),
                        inc=1, serial=False)
                return kb.em("s", lambda E, c=c, pap=pap: E.activation(
                    out=T1[:, c * 2048 + tt * 128:(c * 2048 + tt * 128) + 128], in_=pap,
                    func=AF.Identity), inc=1, serial=False)

            transpose8(scrY, ta, evac_a, src_res=r_scrY)
            if tt % 4 == 3:
                kb.produce(r_afm[qs], [("v", kb.cnt["v"]), ("s", kb.cnt["s"])])

        spill_toks = []
        def emit_attention(wq_h, wk_h, wv_h, wout_h, bq8c, bkc, epi, first):
            wsrc = lambda w: w.rearrange("(kc p) d -> p kc d", p=128)
            tok_wq = dma("g", wq_sb.rearrange("p (kc d) -> p kc d", d=1024), wsrc(wq_h), res=r_wq)
            for rr, half_res in ((0, r_whalfA), (1, r_whalfB)):
                dstA = (T2half if rr == 0 else ctxsb).rearrange("p (kc d) -> p kc d", d=1024)
                if rr == 1:
                    for rc in r_ctx:
                        kb.reuse("g", rc)
                dma("g", dstA, wsrc(wk_h)[:, rr * 4:(rr + 1) * 4, :], res=half_res)
            tok_wo = dma("g", wo_sb.rearrange("p (kc d) -> p kc d", d=1024), wsrc(wout_h), res=r_wo)

            for c in range(NPAIR):
                kb.reuse("v", r_kfm[c])
                for tsl in range(4):
                    bank = (c * 4 + tsl) % 4
                    kb.reuse("t", r_pb[bank])
                    kb.consume("t", r_whalfA)
                    kb.consume("t", r_whalfB)
                    kb.consume("t", r_afm[tsl])
                    tk = None
                    for kc in range(NKC):
                        slot = T2half if kc < 4 else ctxsb
                        tk = kb.em("t", lambda E, c=c, tsl=tsl, kc=kc, slot=slot, bank=bank: E.matmul(
                            out=pb[bank][:],
                            lhsT=slot[:, (kc % 4) * 1024 + c * 128:(kc % 4) * 1024 + c * 128 + 128],
                            rhs=T1[:, kc * 2048 + tsl * 512:kc * 2048 + tsl * 512 + 512],
                            start=(kc == 0), stop=(kc == NKC - 1)), inc=(1 if kc == NKC - 1 else 0))
                    kb.used(r_whalfA, tk)
                    kb.used(r_whalfB, tk)
                    kb.used(r_afm[tsl], tk)
                    kb.produce(r_pb[bank], tk)
                    kb.wt("v", tk)
                    tv = kb.em("v", lambda E, c=c, tsl=tsl, bank=bank: E.tensor_scalar(
                        out=T3[:, c * 2048 + tsl * 512:c * 2048 + tsl * 512 + 512],
                        in0=pb[bank][:], scalar1=bkc[:, c:c + 1], scalar2=None, op0=OP.add), inc=1, serial=False)
                    kb.used(r_pb[bank], tv)
                kb.produce(r_kfm[c], ("v", kb.cnt["v"]))

            for rr, half_res in ((0, r_whalfA), (1, r_whalfB)):
                dstA = (T2half if rr == 0 else ctxsb).rearrange("p (kc d) -> p kc d", d=1024)
                dma("g", dstA, wsrc(wv_h)[:, rr * 4:(rr + 1) * 4, :], res=half_res)
            kb.consume("v", r_bvb)
            for tt in range(NT):
                kb.reuse("v", r_vsb[tt])
                last_v = None
                for dsh in range(2):
                    bank = (tt * 2 + dsh) % 4
                    kb.reuse("t", r_pb[bank])
                    tk = None
                    for kc in range(NKC):
                        slot = T2half if kc < 4 else ctxsb
                        res_w = r_whalfA if kc < 4 else r_whalfB
                        kb.consume("t", res_w)
                        kb.consume("t", r_afm[tt // 4])
                        tk = kb.em("t", lambda E, tt=tt, dsh=dsh, kc=kc, slot=slot, bank=bank: E.matmul(
                            out=pb[bank][:],
                            lhsT=T1[:, kc * 2048 + tt * 128:kc * 2048 + tt * 128 + 128],
                            rhs=slot[:, (kc % 4) * 1024 + dsh * 512:(kc % 4) * 1024 + dsh * 512 + 512],
                            start=(kc == 0), stop=(kc == NKC - 1)), inc=(1 if kc == NKC - 1 else 0))
                        if kc == NKC - 1:
                            kb.used(r_whalfA, tk)
                            kb.used(r_whalfB, tk)
                            kb.used(r_afm[tt // 4], tk)
                    kb.produce(r_pb[bank], tk)
                    kb.wt("v", tk)
                    kb.wt("v", vones_tok)
                    base = tt * 1032 + dsh * 4 * 129
                    vdst = T4[:, base: base + 516].rearrange("p (pr c) -> p pr c", c=129)
                    psrc = pb[bank][:].rearrange("p (pr c) -> p pr c", c=128)
                    bsrc = bvb[:, dsh * 512:(dsh + 1) * 512].rearrange("p (pr c) -> p pr c", c=128)
                    for half in range(2):
                        tv = kb.em("v", lambda E, vdst=vdst, psrc=psrc, bsrc=bsrc, h=half: E.scalar_tensor_tensor(
                            out=vdst[:, :, h * 65:h * 65 + 64],
                            in0=psrc[:, :, 0:64] if h == 0 else psrc[:, :, 64:128],
                            scalar=1.0,
                            in1=bsrc[:, :, 0:64] if h == 0 else bsrc[:, :, 64:128],
                            op0=OP.mult, op1=OP.add), inc=1, serial=False)
                        kb.used(r_pb[bank], tv)
                        kb.used(r_bvb, tv)
                        last_v = tv
                kb.produce(r_vsb[tt], last_v)

            kb.reuse("q", r_whalfA)
            tok_masks = dma("q", masksA, masks_h, dkey="dmask")

            for qs in range(NQS):
                kb.reuse("v", r_qsl)
                for c in range(NPAIR):
                    bank = c % 4
                    kb.reuse("t", r_pb[bank])
                    kb.consume("t", r_wq)
                    kb.consume("t", r_afm[qs])
                    tk = None
                    for kc in range(NKC):
                        tk = kb.em("t", lambda E, c=c, kc=kc, qs=qs, bank=bank: E.matmul(
                            out=pb[bank][:],
                            lhsT=wq_sb[:, kc * 1024 + c * 128:kc * 1024 + c * 128 + 128],
                            rhs=T1[:, kc * 2048 + qs * 512:kc * 2048 + qs * 512 + 512],
                            start=(kc == 0), stop=(kc == NKC - 1)), inc=(1 if kc == NKC - 1 else 0))
                    kb.used(r_wq, tk)
                    kb.used(r_afm[qs], tk)
                    kb.produce(r_pb[bank], tk)
                    kb.wt("s", tk)
                    kb.wt("s", tok_bq8)
                    tv = kb.em("s", lambda E, c=c, bank=bank: E.activation(
                        out=qslab[:, c * 512:(c + 1) * 512], in_=pb[bank][:],
                        func=AF.Identity, bias=bq8c[:, c:c + 1], scale=0.125), inc=1, serial=False)
                    kb.used(r_pb[bank], tv)
                kb.produce(r_qsl, ("s", kb.cnt["s"]))

                nkt = 4 * qs + 4
                def emit_pair_epilogue(p, rbT, rbb):
                        kb.wt("v", r_pbE.ready)
                        kb.wt("v", r_pbO.ready)
                        kb.reuse("v", r_sums)
                        kb.em("v", lambda E: E.tensor_copy(out=sums_sb[64:65, :], in_=pbE[64:65, :]))
                        tv = kb.em("v", lambda E: E.tensor_copy(out=sums_sb[96:97, :], in_=pbE[96:97, :]), inc=1)
                        kb.produce(r_sums, tv)
                        kb.used(r_pbE, tv)
                        kb.used(r_pbO, tv)
                        kb.wt("t", tv)
                        kb.reuse("t", r_pb[2 * rbT])
                        kb.reuse("t", r_pb[2 * rbT + 1])
                        kb.em("t", lambda E, rbb=rbb: E.matmul(
                            out=rbb[0:64, :], lhsT=ones_f[64:65, 0:64], rhs=sums_sb[64:65, :],
                            start=True, stop=True, tile_position=(64, 0)))
                        tk = kb.em("t", lambda E, rbb=rbb: E.matmul(
                            out=rbb[64:128, :], lhsT=ones_f[96:97, 64:128], rhs=sums_sb[96:97, :],
                            start=True, stop=True, tile_position=(96, 64), skip_group_check=True), inc=1)
                        kb.produce(r_pb[2 * rbT], tk)
                        kb.used(r_sums, tk)
                        kb.wt("s", tk)
                        kb.em("s", lambda E, rbb=rbb: E.activation(out=lnscr, in_=rbb, func=AF.Ln))
                        ts_ = kb.em("s", lambda E: E.activation(out=rb_sb, in_=lnscr, func=AF.Exp, scale=-1.0), inc=1)
                        kb.used(r_pb[2 * rbT], ts_)
                        kb.wt("v", ts_)
                        kb.reuse("v", r_ctx[p])
                        kb.em("v", lambda E, p=p: E.tensor_tensor(
                            out=ctxsb[0:64, p * 512:(p + 1) * 512], in0=pbE[0:64, :],
                            in1=rb_sb[0:64, :], op=OP.mult), serial=False)
                        tv = kb.em("v", lambda E, p=p: E.tensor_tensor(
                            out=ctxsb[64:128, p * 512:(p + 1) * 512], in0=pbO[64:128, :],
                            in1=rb_sb[64:128, :], op=OP.mult), inc=1, serial=False)
                        kb.produce(r_ctx[p], tv)
                        kb.used(r_pbE, tv)
                        kb.used(r_pbO, tv)

                pend_epi = [None]

                for p in range(NPAIR):
                    nb = nkt // 2
                    exp_toks = {}

                    def emit_scores_and_exp(b, p=p, qs=qs, nb=nb):
                        diag = 2 * b >= 4 * qs
                        tsrs = [(p * 2 * nb + 2 * b) % 3, (p * 2 * nb + 2 * b + 1) % 3]
                        for head in range(2):
                            kb.reuse("t", r_pb[2 * tsrs[head]])
                            kb.reuse("t", r_pb[2 * tsrs[head] + 1])
                        kb.consume("t", r_kfm[p])
                        kb.consume("t", r_qsl)
                        tks = [None, None]
                        for j01 in range(2):
                            kt = 2 * b + j01
                            for head in range(2):
                                tsr = tsrs[head]
                                base = head * 64
                                tks[head] = kb.em("t", lambda E, p=p, kt=kt, tsr=tsr, j01=j01, base=base, diag=diag: E.matmul(
                                    out=pw[tsr][:, j01 * 512:(j01 + 1) * 512],
                                    lhsT=T3[base:base + 64, p * 2048 + kt * 128:p * 2048 + kt * 128 + 128],
                                    rhs=qslab[base:base + 64, p * 512:(p + 1) * 512],
                                    start=True, stop=not diag), inc=(0 if diag else 1))
                            if diag:
                                jv = kt - 4 * qs
                                kb.wt("t", tok_idb)
                                kb.wt("t", tok_masks)
                                for head in range(2):
                                    tsr = tsrs[head]
                                    tks[head] = kb.em("t", lambda E, tsr=tsr, j01=j01, jv=jv: E.matmul(
                                        out=pw[tsr][:, j01 * 512:(j01 + 1) * 512],
                                        lhsT=identb,
                                        rhs=masksA[:, jv * 512:(jv + 1) * 512],
                                        start=False, stop=True), inc=1)
                                    kb.used(r_whalfA, tks[head])
                        for head in range(2):
                            tsr = tsrs[head]
                            tk = tks[head]
                            kb.produce(r_pb[2 * tsr], tk)
                            kb.produce(r_pb[2 * tsr + 1], tk)
                            kb.used(r_kfm[p], tk)
                            kb.used(r_qsl, tk)
                            slot = (head * 2) + b % 2
                            kb.wt("s", tk)
                            kb.reuse("s", r_pring[slot])
                            te = kb.em("s", lambda E, slot=slot, tsr=tsr: E.activation(
                                out=pring[slot], in_=pw[tsr][:], func=AF.Exp), inc=1, serial=False)
                            kb.used(r_pb[2 * tsr], te)
                            kb.used(r_pb[2 * tsr + 1], te)
                            kb.produce(r_pring[slot], te)
                            exp_toks[2 * b + head] = (slot, te)

                    def emit_ctx(b, p=p, nb=nb, nkt=nkt):
                        slot_e = exp_toks[2 * b][0]
                        slot_o = exp_toks[2 * b + 1][0]
                        if b == 0:
                            kb.reuse("t", r_pbE)
                            kb.reuse("t", r_pbO)
                        kb.consume("t", r_pring[slot_e])
                        kb.consume("t", r_pring[slot_o])
                        for j01 in range(2):
                            kt = 2 * b + j01
                            laste = (kt == nkt - 1)
                            kb.consume("t", r_vsb[kt])
                            pr_e = pring[slot_e][:, j01 * 512:(j01 + 1) * 512]
                            pr_o = pring[slot_o][:, j01 * 512:(j01 + 1) * 512]
                            tk_e = kb.em("t", lambda E, p=p, kt=kt, pr=pr_e, nkt=nkt: E.matmul(
                                out=pbE[0:65, :],
                                lhsT=T4[:, kt * 1032 + p * 129: kt * 1032 + p * 129 + 65],
                                rhs=pr, start=(kt == 0), stop=(kt == nkt - 1),
                                skip_group_check=True), inc=1)
                            kb.used(r_pring[slot_e], tk_e)
                            tk_s = kb.em("t", lambda E, kt=kt, pr=pr_o, nkt=nkt: E.matmul(
                                out=pbE[96:97, :], lhsT=ones_b[:, 0:1], rhs=pr,
                                start=(kt == 0), stop=(kt == nkt - 1),
                                tile_position=(0, 96), skip_group_check=True), inc=1)
                            tk_o = kb.em("t", lambda E, p=p, kt=kt, pr=pr_o, nkt=nkt: E.matmul(
                                out=pbO[64:128, :],
                                lhsT=T4[:, kt * 1032 + p * 129 + 65: kt * 1032 + p * 129 + 129],
                                rhs=pr, start=(kt == 0), stop=(kt == nkt - 1),
                                tile_position=(0, 64), skip_group_check=True), inc=1)
                            kb.used(r_pring[slot_o], tk_o)
                            kb.used(r_vsb[kt], tk_o)
                            if laste:
                                kb.produce(r_pbE, tk_s)
                                kb.produce(r_pbO, tk_o)

                    for b in range(nb):
                        if b == 1 and pend_epi[0] is not None:
                            pend_epi[0]()
                            pend_epi[0] = None
                        emit_scores_and_exp(b)
                        if b >= 1:
                            emit_ctx(b - 1)
                    emit_ctx(nb - 1)
                    rbT = (p * 2 * nb + 2 * nb + 2) % 3
                    pend_epi[0] = (lambda p=p, rbT=rbT: emit_pair_epilogue(p, rbT, pw[rbT][:, 0:512]))
                if pend_epi[0] is not None:
                    pend_epi[0]()
                    pend_epi[0] = None
                for i in range(4):
                    tt = qs * 4 + i
                    banks = (2 * (i % 2), 2 * (i % 2) + 1)
                    toks = []
                    for ds in range(2):
                        bank = banks[ds]
                        kb.reuse("t", r_pb[bank])
                        kb.consume("t", r_wo)
                        tk = None
                        for c in range(NPAIR):
                            kb.consume("t", r_ctx[c])
                            tk = kb.em("t", lambda E, c=c, i=i, ds=ds, bank=bank: E.matmul(
                                out=pb[bank][:],
                                lhsT=ctxsb[:, c * 512 + i * 128:c * 512 + i * 128 + 128],
                                rhs=wo_sb[:, c * 1024 + ds * 512:c * 1024 + ds * 512 + 512],
                                start=(c == 0), stop=(c == NPAIR - 1)), inc=(1 if c == NPAIR - 1 else 0))
                        for c in range(NPAIR):
                            kb.used(r_ctx[c], tk)
                        kb.used(r_wo, tk)
                        kb.produce(r_pb[bank], tk)
                        toks.append(tk)
                    epi(tt, banks, toks)

        def epi1(tt, banks, toks):
            qs = tt // 4
            kb.consume("v", r_bob)
            kb.reuse("v", r_scrY)
            ty = None
            for ds in range(2):
                kb.wt("v", toks[ds])
                ty = kb.em("v", lambda E, ds=ds, b=banks[ds]: E.scalar_tensor_tensor(
                    out=scrY[:, ds * 512:(ds + 1) * 512], in0=pb[b][:], scalar=1.0,
                    in1=bob[:, ds * 512:(ds + 1) * 512], op0=OP.mult, op1=OP.add), inc=1)
                kb.used(r_pb[banks[ds]], ty)
                kb.used(r_bob, ty)
            kb.produce(r_scrY, ty)
            m, s, tok_s = ln_stats(scrY)
            sl = tt % 2
            xr = T7[:, sl * 1024:(sl + 1) * 1024]
            tokd = dma("q", xr, x_r[tt], res=r_xr[sl])
            kb.wt("v", tok_s)
            tz = scrz_write(lambda E: E.tensor_scalar(
                out=scrZ, in0=scrY, scalar1=bn2[:, 0:1], scalar2=tstat[:, 1:2],
                op0=OP.subtract, op1=OP.mult))
            kb.used(r_scrY, tz)
            kb.wt("v", tokd)
            tx1 = kb.em("v", lambda E, xr=xr: E.tensor_tensor(
                out=xr, in0=scrZ, in1=xr, op=OP.add), inc=1)
            scrz_used(tx1)
            tok_sp = dma("q", x1s_r[tt], xr, waits=(tx1,))
            spill_toks.append(tok_sp)
            kb.used(r_xr[sl], tok_sp)
            m, s, tok_s = ln_stats(xr, waits=(tx1,))
            kb.wt("v", tok_s)
            kb.em("v", lambda E, tt=tt: E.tensor_copy(out=m1s1[:, tt:tt + 1], in_=bn2[:, 0:1]))
            tms = kb.em("v", lambda E, tt=tt: E.tensor_copy(out=m1s1[:, 16 + tt:17 + tt], in_=tstat[:, 1:2]), inc=1)
            kb.reuse("v", r_scrY)
            t1 = kb.em("v", lambda E, xr=xr, tt=tt: E.tensor_scalar(
                out=scrY, in0=xr, scalar1=m1s1[:, tt:tt + 1], scalar2=m1s1[:, 16 + tt:17 + tt],
                op0=OP.subtract, op1=OP.mult), inc=1)
            kb.produce(r_scrY, t1)
            kb.used(r_xr[sl], t1)

            def evac_h1(c, pap, eng, tt=tt):
                if eng == "v":
                    return kb.em("v", lambda E, c=c, pap=pap: E.tensor_scalar(
                        out=T1[:, c * 2048 + tt * 128:c * 2048 + tt * 128 + 128], in0=pap,
                        scalar1=PPB[:, 32 + c:33 + c], scalar2=PPB[:, 40 + c:41 + c],
                        op0=OP.mult, op1=OP.add), inc=1, serial=False)
                return kb.em("s", lambda E, c=c, pap=pap: E.activation(
                    out=T1[:, c * 2048 + tt * 128:c * 2048 + tt * 128 + 128], in_=pap,
                    func=AF.Identity, scale=PPB[:, 32 + c:33 + c],
                    bias=PPB[:, 40 + c:41 + c]), inc=1, serial=False)

            transpose8(scrY, t1, evac_h1, src_res=r_scrY, engines=("s", "s"))
            if tt % 4 == 3:
                kb.produce(r_afm[qs], [("v", kb.cnt["v"]), ("s", kb.cnt["s"])])

        def epi2(tt, banks, toks):
            kb.consume("v", r_bob)
            if tt % 2 == 0:
                kb.reuse("v", r_scrY)
                dstw = scrY
            else:
                dstw = scrZ
            ta = None
            for ds in range(2):
                kb.wt("v", toks[ds])
                if tt % 2 == 1 and ds == 0:
                    for rp in r_pring:
                        kb.reuse("v", rp)
                ta = kb.em("v", lambda E, ds=ds, b=banks[ds], dstw=dstw: E.scalar_tensor_tensor(
                    out=dstw[:, ds * 512:(ds + 1) * 512], in0=pb[b][:], scalar=1.0,
                    in1=bob[:, ds * 512:(ds + 1) * 512], op0=OP.mult, op1=OP.add), inc=1)
                kb.used(r_pb[banks[ds]], ta)
                kb.used(r_bob, ta)
            if tt % 2 == 0:
                kb.produce(r_scrY, ta)
            else:
                for rp in r_pring:
                    kb.produce(rp, ta)
            tok_sp = dma("q", atts_r[tt], dstw, waits=(ta,))
            spill_toks.append(tok_sp)
            if tt % 2 == 0:
                kb.used(r_scrY, tok_sp)
            else:
                scrz_used(tok_sp)

        emit_attention(aW["q"], aW["k"], aW["v"], aW["o"], PPB[:, 64:72], PPB[:, 8:16], epi1, True)
        load_bias_bcast(fb["v"], bvb, r_bvb)
        load_bias_bcast(fb["p"], bob, r_bob)
        emit_attention(fW["q"], fW["k"], fW["v"], fW["p"], PPB[:, 72:80], PPB[:, 24:32], epi2, False)

        for rs in r_vsb + [r_wo, r_bvb, r_bob, r_scrY, r_sscr[0], r_sscr[1]]:
            kb.reuse("v", rs)
        for rs in r_ctx + r_pring:
            kb.reuse("v", rs)
        att_slots = [ctxZf[:, 0:1024], ctxZf[:, 1024:2048]]
        r_att = [Res(), Res()]
        for _tok in spill_toks:
            kb.wt("q", _tok)
        u_toks = [None] * NT

        w2buf = [T2[:, 0:8192], T2[:, 8192:16384], T6[:, 0:8192]]
        h2f = T1
        gmv = T3
        w1r = [T7b[:, 0:2048], T7b[:, 2048:4096]]

        def w2_load(idx):
            e, hq = divmod(idx, 4)
            buf = w2buf[idx % 3]
            res = r_w2[idx % 3]
            if idx == 0:
                kb.reuse("g", r_wq)
            if idx == 1:
                kb.reuse("g", r_qsl)
                kb.reuse("g", r_whalfA)
            if idx == 2:
                for rs in r_pring + r_ctx + [r_att[0], r_att[1]]:
                    kb.reuse("g", rs)
            src = w2_h[e, hq * 1024:(hq + 1) * 1024, :].rearrange("(hc p) d -> p hc d", p=128)
            return dma("g", buf.rearrange("p (hc d) -> p hc d", d=1024), src, res=res)

        w2_load(0)
        w2_load(1)

        def eb_load(e):
            sl = e % 2
            dma("q", B2R[0:1, sl * 1024:(sl + 1) * 1024], eb2_h[e:e + 1, :], res=r_b2r[sl])
            tokb = dma("q", stg[0:32, :], eb1_h[e].rearrange("(a p) -> a p", p=128), res=r_stg)
            kb.wt("t", tokb)
            kb.reuse("t", r_pb[5])
            tokt = kb.em("t", lambda E: E.transpose(
                out=pt0f[:, 0:32], in_=stg[0:32, :], identity=identf[0:32, 0:32]), inc=1)
            kb.produce(r_pb[5], tokt)
            kb.wt("v", tokt)
            kb.reuse("v", r_b1p[sl])
            tokv = kb.em("v", lambda E, sl=sl: E.tensor_copy(out=b1pp[sl], in_=pt0f[:, 0:32]), inc=1)
            kb.used(r_pb[5], tokv)
            kb.used(r_stg, tokv)
            kb.produce(r_b1p[sl], tokv)

        r_t2s = Res()
        kb.reuse("v", r_h2)
        for rs in r_vsb + [r_wo, r_bvb, r_bob, r_scrY, r_sscr[0], r_sscr[1]]:
            kb.reuse("v", rs)
        for rs in r_pring + r_ctx + r_kfm + [r_att[0], r_att[1]]:
            kb.reuse("v", rs)
        kb.consume("t", r_h2)
        kb.wt("t", tok_gw)
        gate_prev_s = None
        gbank = [pbE, pbO]
        gres = [r_pbE, r_pbO]
        pend_gm = [None]
        for tt in range(NT):
            sl = tt % 2
            xr = T7[:, sl * 1024:(sl + 1) * 1024]
            tokx = dma("q", xr, x1s_r[tt], res=r_xr[sl])
            if tt < 2:
                for rc in r_ctx:
                    kb.reuse("q", rc)
            toka = dma("q", att_slots[sl], atts_r[tt], res=r_att[sl])
            kb.wt("v", tokx)
            th = kb.em("v", lambda E, xr=xr, tt=tt: E.tensor_scalar(
                out=u_sl(tt, 0, 1024), in0=xr, scalar1=m1s1[:, tt:tt + 1],
                scalar2=m1s1[:, 16 + tt:17 + tt], op0=OP.subtract, op1=OP.mult), inc=1)
            kb.used(r_xr[sl], th)
            kb.wt("v", toka)
            tu = kb.em("v", lambda E, tt=tt, sl=sl: E.tensor_tensor(
                out=u_sl(tt, 0, 1024), in0=att_slots[sl], in1=u_sl(tt, 0, 1024), op=OP.add), inc=1)
            kb.used(r_att[sl], tu)
            u_toks[tt] = tu
            m, s, tok_s = ln_stats(u_sl(tt, 0, 1024), waits=(tu,))
            kb.wt("v", tok_s)
            if tt % 4 == 0:
                kb.reuse("v", r_afm[tt // 4])
            t2stage = T3f[:, 0:1024]
            kb.reuse("v", r_t2s)
            t2tok = kb.em("v", lambda E, tt=tt: E.tensor_scalar(
                out=t2stage, in0=u_sl(tt, 0, 1024), scalar1=bn2[:, 0:1], scalar2=tstat[:, 1:2],
                op0=OP.subtract, op1=OP.mult), inc=1)
            kb.produce(r_t2s, t2tok)

            def evac_h2(c, pap, eng, tt=tt):
                if eng == "v":
                    return kb.em("v", lambda E, c=c, pap=pap, tt=tt: E.tensor_scalar(
                        out=h2f[:, c * 2048 + tt * 128:c * 2048 + tt * 128 + 128], in0=pap,
                        scalar1=PPB[:, 48 + c:49 + c], scalar2=PPB[:, 56 + c:57 + c],
                        op0=OP.mult, op1=OP.add), inc=1, serial=False)
                return kb.em("s", lambda E, c=c, pap=pap, tt=tt: E.activation(
                    out=h2f[:, c * 2048 + tt * 128:c * 2048 + tt * 128 + 128], in_=pap,
                    func=AF.Identity, scale=PPB[:, 48 + c:49 + c],
                    bias=PPB[:, 56 + c:57 + c]), inc=1, serial=False)

            if pend_gm[0] is not None:
                pend_gm[0]()
                pend_gm[0] = None
            ev_toks = []
            transpose8(t2stage, t2tok, lambda c, pap, eng: ev_toks.append(evac_h2(c, pap, eng)) or ev_toks[-1],
                       src_res=r_t2s, engines=("s", "s"))
            bank = tt % 2
            kb.reuse("t", gres[bank])
            for et in ev_toks[3:4] + ev_toks[7:8]:
                kb.wt("t", et)
            kb.wt("t", t2tok)
            tk = None
            for kc in range(NKC):
                tk = kb.em("t", lambda E, tt=tt, kc=kc, bank=bank: E.matmul(
                    out=gbank[bank][:, 0:8],
                    lhsT=h2f[:, kc * 2048 + tt * 128:kc * 2048 + tt * 128 + 128],
                    rhs=GWSB[:, kc * 8:(kc + 1) * 8],
                    start=(kc == 0), stop=(kc == NKC - 1)), inc=(1 if kc == NKC - 1 else 0))
            kb.used(r_h2, tk)
            kb.produce(gres[bank], tk)

            def gmath(tt=tt, bank=bank, tk=tk):
                nonlocal gate_prev_s
                kb.wt("v", tk)
                kb.wt("v", gate_prev_s)
                kb.em("v", lambda E, bank=bank: E.tensor_tensor(
                    out=gsc[:, 0:8], in0=gbank[bank][:, 0:8], in1=gbb, op=OP.add))
                tg = kb.em("v", lambda E: E.reduce_max(out=gsc[:, 8:9], in_=gsc[:, 0:8], axis=AX.X), inc=1)
                kb.used(gres[bank], tg)
                kb.em("v", lambda E: E.tensor_scalar(
                    out=gsc[:, 9:17], in0=gsc[:, 0:8], scalar1=gsc[:, 8:9], scalar2=None, op0=OP.is_equal))
                kb.em("v", lambda E: E.scalar_tensor_tensor(
                    out=gsc[:, 17:25], in0=gsc[:, 9:17], scalar=-1.0e30, in1=gsc[:, 0:8],
                    op0=OP.mult, op1=OP.add))
                kb.em("v", lambda E: E.reduce_max(out=gsc[:, 25:26], in_=gsc[:, 17:25], axis=AX.X))
                tg = kb.em("v", lambda E: E.tensor_tensor(
                    out=gsc[:, 26:27], in0=gsc[:, 25:26], in1=gsc[:, 8:9], op=OP.subtract), inc=1)
                kb.wt("s", tg)
                kb.em("s", lambda E: E.activation(out=gsc[:, 27:28], in_=gsc[:, 26:27], func=AF.Exp))
                kb.em("s", lambda E: E.activation(out=gsc[:, 28:29], in_=gsc[:, 27:28], func=AF.Ln, bias=CONSTS[:, 1:2]))
                tw1 = kb.em("s", lambda E: E.activation(out=gsc[:, 29:30], in_=gsc[:, 28:29], func=AF.Exp, scale=-1.0), inc=1)
                gate_prev_s = tw1
                kb.wt("v", tw1)
                kb.em("v", lambda E: E.tensor_scalar(
                    out=gsc[:, 30:31], in0=gsc[:, 29:30], scalar1=-1.0, scalar2=1.0,
                    op0=OP.mult, op1=OP.add))
                kb.em("v", lambda E: E.tensor_scalar(
                    out=gsc[:, 32:40], in0=gsc[:, 17:25], scalar1=gsc[:, 25:26], scalar2=None, op0=OP.is_equal))
                kb.em("v", lambda E: E.tensor_scalar(
                    out=gsc[:, 40:48], in0=gsc[:, 32:40], scalar1=gsc[:, 30:31], scalar2=None, op0=OP.mult))
                kb.em("v", lambda E, tt=tt: E.scalar_tensor_tensor(
                    out=wfull[:, tt * 8:(tt + 1) * 8], in0=gsc[:, 9:17], scalar=gsc[:, 29:30],
                    in1=gsc[:, 40:48], op0=OP.mult, op1=OP.add), inc=1)

            pend_gm[0] = gmath
        if pend_gm[0] is not None:
            pend_gm[0]()
            pend_gm[0] = None
        kb.produce(r_h2, [("v", kb.cnt["v"]), ("s", kb.cnt["s"])])

        T1f = T1[:].bitcast(F32)
        out_toks = []

        def emit_final(tt):
            sl = tt % 2
            xr = T7[:, sl * 1024:(sl + 1) * 1024]
            kb.reuse("q", r_w1r[sl])
            tokx = dma("q", xr, x1s_r[tt], res=r_xr[sl], waits=(u_toks[tt],))
            m, s, tok_s = ln_stats(u_sl(tt, 0, 1024), waits=(u_toks[tt],))
            zsl = T1f[:, sl * 1024:(sl + 1) * 1024]
            kb.wt("v", tok_s)
            kb.reuse("v", r_zr[sl])
            if tt < 2:
                kb.reuse("v", r_h2)
            tz = kb.em("v", lambda E, tt=tt, zsl=zsl: E.tensor_scalar(
                out=zsl, in0=u_sl(tt, 0, 1024), scalar1=bn2[:, 0:1], scalar2=tstat[:, 1:2],
                op0=OP.subtract, op1=OP.mult), inc=1)
            kb.wt("v", tokx)
            to = kb.em("v", lambda E, xr=xr, zsl=zsl: E.tensor_tensor(
                out=zsl, in0=xr, in1=zsl, op=OP.add), inc=1)
            kb.used(r_xr[sl], to)
            tokout = dma("q", out_r[tt], zsl, waits=(to,))
            out_toks.append(tokout)
            kb.produce(r_zr[sl], tokout)

        g4 = [pbE[:], pbO[:], pb[4], pb[5]]
        gres4 = [r_pbE, r_pbO, r_pb[4], r_pb[5]]
        for rk in r_kfm:
            kb.reuse("s", rk)
        eb_load(0)
        for e in range(NE):
            if e + 1 < NE:
                eb_load(e + 1)
            for hq in range(4):
                idx = e * 4 + hq
                for hcp in range(4):
                    slw = hcp % 2
                    if idx == 0 and hcp < 2:
                        kb.reuse("g", r_xr[slw])
                    src = w1_h[e, :, hq * 1024 + hcp * 256: hq * 1024 + (hcp + 1) * 256]
                    tokw = dma("g", w1r[slw].rearrange("p (kc j) -> p kc j", j=256),
                               src.rearrange("(kc p) j -> p kc j", p=128), res=r_w1r[slw])
                    for sub in range(2):
                        hc = hcp * 2 + sub
                        for ts_ in range(4):
                            bank = (hc * 4 + ts_) % 4
                            kb.reuse("t", r_pb[bank])
                            kb.wt("t", tokw)
                            kb.consume("t", r_h2)
                            tk = None
                            for kc in range(NKC):
                                tk = kb.em("t", lambda E, sub=sub, ts_=ts_, kc=kc, slw=slw, bank=bank: E.matmul(
                                    out=pb[bank][:],
                                    lhsT=w1r[slw][:, kc * 256 + sub * 128:kc * 256 + sub * 128 + 128],
                                    rhs=h2f[:, kc * 2048 + ts_ * 512:kc * 2048 + ts_ * 512 + 512],
                                    start=(kc == 0), stop=(kc == NKC - 1)), inc=(1 if kc == NKC - 1 else 0))
                            kb.used(r_w1r[slw], tk)
                            kb.used(r_h2, tk)
                            kb.produce(r_pb[bank], tk)
                            kb.wt("s", tk)
                            kb.consume("s", r_b1p[e % 2])
                            if ts_ == 0:
                                kb.reuse("s", r_gmh[hc])
                                kb.reuse("s", r_t2s)
                            tg = kb.em("s", lambda E, hc=hc, ts_=ts_, e=e, hq=hq, bank=bank: E.activation(
                                out=gmv[:, hc * 2048 + ts_ * 512:hc * 2048 + ts_ * 512 + 512],
                                in_=pb[bank][:], func=AF.Gelu,
                                bias=b1pp[e % 2][:, hq * 8 + hc:hq * 8 + hc + 1]), inc=1, serial=False)
                            kb.used(r_pb[bank], tg)
                            kb.used(r_b1p[e % 2], tg)
                            if ts_ == 3:
                                kb.produce(r_gmh[hc], tg)
                if idx + 2 < 32:
                    w2_load(idx + 2)
                kb.consume("t", r_w2[idx % 3])
                for tt in range(NT):
                    for ds in range(2):
                        bank = (tt * 2 + ds) % 4
                        kb.reuse("t", gres4[bank])
                        tk = None
                        for hc in range(8):
                            kb.consume("t", r_gmh[hc])
                            tk = kb.em("t", lambda E, tt=tt, ds=ds, hc=hc, bank=bank, idx=idx, hq=hq: E.matmul(
                                out=g4[bank],
                                lhsT=gmv[:, hc * 2048 + tt * 128:hc * 2048 + tt * 128 + 128],
                                rhs=w2buf[idx % 3][:, hc * 1024 + ds * 512:hc * 1024 + ds * 512 + 512],
                                start=(hc == 0), stop=(hc == 7 and hq != 0)), inc=(1 if (hc == 7 and hq != 0) else 0))
                        if hq == 0:
                            kb.consume("t", r_b2r[e % 2])
                            tk = kb.em("t", lambda E, ds=ds, e=e, bank=bank: E.matmul(
                                out=g4[bank], lhsT=ones_f[0:1, 0:128],
                                rhs=B2R[0:1, (e % 2) * 1024 + ds * 512:(e % 2) * 1024 + ds * 512 + 512],
                                start=False, stop=True), inc=1)
                            kb.used(r_b2r[e % 2], tk)
                        for hc in range(8):
                            kb.used(r_gmh[hc], tk)
                        kb.used(r_w2[idx % 3], tk)
                        kb.produce(gres4[bank], tk)
                        kb.wt("v", tk)
                        tv = kb.em("v", lambda E, tt=tt, ds=ds, e=e, bank=bank: E.scalar_tensor_tensor(
                            out=u_sl(tt, ds * 512, ds * 512 + 512), in0=g4[bank],
                            scalar=wfull[:, tt * 8 + e:tt * 8 + e + 1],
                            in1=u_sl(tt, ds * 512, ds * 512 + 512),
                            op0=OP.mult, op1=OP.add), inc=1, serial=False)
                        kb.used(gres4[bank], tv)
                        u_toks[tt] = tv
                        if e == NE - 1 and hq == 3 and ds == 1:
                            emit_final(tt)


        for _tok in out_toks:
            kb.ops["q"].append((("WAIT", _tok[0], _tok[1]), 0))

        for _k in kb.dcnt:
            sems[_k] = ctx.enter_context(nc.semaphore("sem_" + _k))
        block = ctx.enter_context(nc.Block())

        def runner(eng, e):
            for fn, inc in kb.ops[e]:
                if isinstance(fn, tuple):
                    if fn[0] == "WAIT":
                        eng.wait_ge(sems[fn[1]], fn[2])
                    elif fn[0] == "DMA":
                        instr = fn[1](eng)
                        instr.then_inc(sems[fn[2]], 16)
                    continue
                instr = fn(eng)
                if inc:
                    instr.then_inc(sems[e], inc)

        @block.tensor
        def _(eng):
            runner(eng, "t")

        @block.vector
        def _(eng):
            runner(eng, "v")

        @block.scalar
        def _(eng):
            runner(eng, "s")

        @block.gpsimd
        def _(eng):
            runner(eng, "g")

        @block.sync
        def _(eng):
            runner(eng, "q")

    return nc


_NC_CACHE = None


def kernel(**inputs):
    global _NC_CACHE
    if _NC_CACHE is None:
        _NC_CACHE = build_nc()
    nc = _NC_CACHE

    x = np.ascontiguousarray(np.asarray(inputs["L0_attn_x"], dtype=np.float32))
    consts = {
        "cc_identb": np.eye(128, dtype=np.float32).astype(ml_dtypes.bfloat16),
        "cc_identf": np.eye(128, dtype=np.float32),
    }
    mvs = []
    r = np.arange(128)[:, None]
    c = np.arange(512)[None, :]
    for j in range(4):
        mvs.append(np.where(r > c - 128 * j, np.float32(NEG), np.float32(0.0)))
    consts["cc_masks"] = np.concatenate(mvs, axis=1).astype(ml_dtypes.bfloat16)

    shared = {}
    for k, v in inputs.items():
        if k == "L0_attn_x":
            continue
        shared[k] = np.ascontiguousarray(np.asarray(v, dtype=np.float32))

    in_maps = []
    for b in range(8):
        m = {"xcore": np.ascontiguousarray(x[b])}
        m.update(shared)
        m.update(consts)
        in_maps.append(m)

    res = bass_utils.run_bass_kernel_spmd(nc, in_maps, core_ids=list(range(8)))
    out = np.stack([np.asarray(r["out"], dtype=np.float32) for r in res.results], axis=0)
    return out


# version 17
# speedup vs baseline: 1.1554x; 1.0115x over previous
import numpy as np
import ml_dtypes
from contextlib import ExitStack

import concourse.bass as bass
import concourse.mybir as mybir
from concourse.alu_op_type import AluOpType as OP
import concourse.bass_utils as bass_utils

F32 = mybir.dt.float32
BF16 = mybir.dt.bfloat16
AF = mybir.ActivationFunctionType
AX = mybir.AxisListType

D = 1024
T = 2048
NT = 16
NQS = 4
NKC = 8
NPAIR = 8
NE = 8
HID = 4096
LN_EPS = 1e-5
NEG = -1.0e9


class Res:
    __slots__ = ("ready", "users")

    def __init__(self):
        self.ready = None
        self.users = []


class KB:
    ENGS = ("t", "v", "s", "g", "q")

    def __init__(self):
        self.ops = {e: [] for e in self.ENGS}
        self.cnt = {e: 0 for e in self.ENGS}
        self.waited = {e: {} for e in self.ENGS}
        self.dcnt = {}

    def em(self, e, fn, inc=0, serial=True):
        if e in ("v", "s"):
            if serial and self.cnt[e] > 0:
                self.ops[e].append((("WAIT", e, self.cnt[e]), 0))
            inc = max(inc, 1)
        self.ops[e].append((fn, inc))
        if inc:
            self.cnt[e] += inc
            return (e, self.cnt[e])
        return None

    def emd(self, e, fn, dkey):
        self.dcnt[dkey] = self.dcnt.get(dkey, 0) + 16
        self.ops[e].append((("DMA", fn, dkey), 0))
        return (dkey, self.dcnt[dkey])

    def wt(self, e, tok):
        if tok is None:
            return
        te, val = tok
        if te == e:
            return
        if self.waited[e].get(te, -1) >= val:
            return
        self.waited[e][te] = val
        self.ops[e].append((("WAIT", te, val), 0))

    def consume(self, e, res):
        r = res.ready
        for t in (r if isinstance(r, list) else [r]):
            self.wt(e, t)

    def used(self, res, tok):
        if tok is not None:
            res.users.append(tok)

    def produce(self, res, tok):
        res.ready = tok
        res.users = []

    def reuse(self, e, res):
        for u in res.users:
            self.wt(e, u)
        if not res.users:
            r = res.ready
            for t in (r if isinstance(r, list) else [r]):
                self.wt(e, t)


def build_nc():
    nc = bass.Bass("TRN2", target_bir_lowering=False)

    def din(name, shape, dt=F32):
        return nc.dram_tensor(name, shape, dt, kind="ExternalInput").ap()

    x_h = din("xcore", [T, D])
    aW = {w: din("L0_attn_W" + w, [D, D]) for w in "qkvo"}
    ab = {w: din("L0_attn_b" + w, [D]) for w in "qkvo"}
    fW = {w: din("L0_ffn_W" + w, [D, D]) for w in "qkvp"}
    fb = {w: din("L0_ffn_b" + w, [D]) for w in "qkvp"}
    g1_h = din("L0_ffn_ln1_g", [D])
    b1_h = din("L0_ffn_ln1_b", [D])
    g2_h = din("L0_ffn_ln2_g", [D])
    b2_h = din("L0_ffn_ln2_b", [D])
    gw_h = din("L0_ffn_gate_W", [D, NE])
    gb_h = din("L0_ffn_gate_b", [NE])
    w1_h = din("L0_ffn_exp_W1", [NE, D, HID])
    eb1_h = din("L0_ffn_exp_b1", [NE, HID])
    w2_h = din("L0_ffn_exp_W2", [NE, HID, D])
    eb2_h = din("L0_ffn_exp_b2", [NE, D])
    identb_h = din("cc_identb", [128, 128], BF16)
    identf_h = din("cc_identf", [128, 128])
    masks_h = din("cc_masks", [128, 2048], BF16)

    out_h = nc.dram_tensor("out", [T, D], F32, kind="ExternalOutput").ap()
    x1s_h = nc.dram_tensor("x1s", [T, D], F32, kind="Internal").ap()
    atts_h = nc.dram_tensor("atts", [T, D], F32, kind="Internal").ap()

    x_r = x_h.rearrange("(t p) d -> t p d", p=128)
    out_r = out_h.rearrange("(t p) d -> t p d", p=128)
    x1s_r = x1s_h.rearrange("(t p) d -> t p d", p=128)
    atts_r = atts_h.rearrange("(t p) d -> t p d", p=128)

    with ExitStack() as ctx:
        def sbt(name, shape, dt):
            return ctx.enter_context(nc.sbuf_tensor(name, shape, dt))

        def psb(name, shape, dt):
            return ctx.enter_context(nc.psum_tensor(name, shape, dt))

        T1 = sbt("T1", [128, 16384], BF16)
        T2 = sbt("T2", [128, 16384], BF16)
        T3 = sbt("T3", [128, 16384], BF16)
        T4 = sbt("T4", [128, 16512], BF16)
        T5 = sbt("T5", [128, 8192], F32)
        T6 = sbt("T6", [128, 8192], BF16)
        T7 = sbt("T7", [128, 2048], F32)
        T8 = sbt("T8", [128, 2048], F32)
        T9 = sbt("T9", [128, 256], BF16)
        T10 = sbt("T10", [128, 256], F32)
        PPB = sbt("PPB", [128, 96], F32)
        GWSB = sbt("GWSB", [128, 64], BF16)
        B2R = sbt("B2R", [1, 2048], F32)
        RSTG = sbt("RSTG", [1, 1024], F32)

        pw = [psb("pw%d" % i, [128, 1024], F32) for i in range(3)]
        pb = [pw[0][:, 0:512], pw[0][:, 512:1024], pw[1][:, 0:512], pw[1][:, 512:1024],
              pw[2][:, 0:512], pw[2][:, 512:1024]]
        pbE = psb("pbE", [128, 512], F32)
        pbO = psb("pbO", [128, 512], F32)
        pt0f = pb[5]
        ptf = [pb[4], pb[5]]

        sems = {e: ctx.enter_context(nc.semaphore("sem_" + e)) for e in KB.ENGS}
        kb = KB()

        identb = T9[:, 0:128]
        ones_b = T9[:, 128:256]
        identf = T10[:, 0:128]
        ones_f = T10[:, 128:256]
        T5b = T5[:].bitcast(BF16)
        wo_sb = T5b[:, 0:8192]
        sscr = [T5[:, 4096:4608], T5[:, 4608:5120]]
        scrY = T5[:, 5120:6144]
        bvb = T5[:, 6144:7168]
        bob = T5[:, 7168:8192]
        pring = [T6[:, i * 1024:(i + 1) * 1024] for i in range(4)]
        ctxsb = T6[:, 4096:8192]
        scrZ = T6[:, 0:2048].bitcast(F32)
        ctxZf = T6[:, 4096:8192].bitcast(F32)
        masksA = T2[:, 12288:14336]
        sscrW = T5[:, 4096:5120]
        sums_sb = T8[:, 0:512]
        lnscr = T8[:, 512:1024]
        rb_sb = T8[:, 1024:1536]
        m1s1 = T8[:, 1536:1568]
        wfull = T8[:, 1600:1728]
        gbb = T8[:, 1728:1736]
        gsc = T8[:, 1736:1800]
        bnsc = T8[:, 1800:1812]
        bn2 = T8[:, 1812:1814]
        tstat = T8[:, 1814:1824]
        b1pp = [T8[:, 1824 + 32 * i: 1856 + 32 * i] for i in range(2)]
        CONSTS = T8[:, 1888:1896]
        stg = T8[:, 1904:2032]
        T2half = T2[:, 12288:16384]
        qslab = T2[:, 8192:12288]
        wq_sb = T2[:, 0:8192]
        T3f = T3[:].bitcast(F32)
        T7b = T7[:].bitcast(BF16)
        u_lo = T4[:].bitcast(F32)

        def u_sl(tt, c0, c1):
            if tt < 8:
                return u_lo[:, tt * 1024 + c0: tt * 1024 + c1]
            return T5[:, (tt - 8) * 1024 + c0: (tt - 8) * 1024 + c1]

        r_pb = [Res() for _ in range(6)]
        r_pbE = Res()
        r_pbO = Res()
        r_pt = [r_pb[4], r_pb[5]]
        r_pring = [Res() for _ in range(4)]
        r_xr = [Res(), Res()]
        r_afm = [Res() for _ in range(NQS)]
        r_kfm = [Res() for _ in range(NPAIR)]
        r_vsb = [Res() for _ in range(NT)]
        r_qsl = Res()
        r_ctx = [Res() for _ in range(NPAIR)]
        r_wq = Res()
        r_whalfA = Res()
        r_whalfB = Res()
        r_wo = Res()
        r_sums = Res()
        r_scrY = Res()
        r_sscr = [Res(), Res()]
        r_bvb = Res()
        r_bob = Res()
        r_w2 = [Res(), Res(), Res()]
        r_h2 = Res()
        r_gmh = [Res() for _ in range(16)]
        r_b2r = [Res(), Res()]
        r_b1p = [Res(), Res()]
        r_stg = Res()
        r_zr = [Res(), Res()]
        r_w1r = [Res(), Res()]
        r_rstg = Res()

        dkeys = {}
        anon_ctr = [0]

        def dma(e, dst, src, res=None, waits=(), dkey=None):
            for w in waits:
                kb.wt(e, w)
            if res is not None:
                kb.reuse(e, res)
            if dkey is None:
                if res is not None:
                    dkey = dkeys.setdefault(id(res), "d%d" % len(dkeys))
                else:
                    anon_ctr[0] += 1
                    dkey = "a%d" % anon_ctr[0]
            tok = kb.emd(e, lambda E, d=dst, s=src: E.dma_start(out=d, in_=s), dkey)
            if res is not None:
                kb.produce(res, tok)
            return tok

        ln_prev_s = [None]

        def ln_stats(src1024, waits=()):
            for w in waits:
                kb.wt("v", w)
            kb.wt("v", ln_prev_s[0])
            kb.em("v", lambda E, s=src1024: E.bn_stats(out=bnsc[:, 0:6], in_=s[:, 0:512]))
            kb.em("v", lambda E, s=src1024: E.bn_stats(out=bnsc[:, 6:12], in_=s[:, 512:1024]))
            tv = kb.em("v", lambda E: E.bn_aggr(out=bn2[:], in_=bnsc[:]), inc=1)
            kb.wt("s", tv)
            kb.em("s", lambda E: E.activation(out=tstat[:, 0:1], in_=bn2[:, 1:2], func=AF.Ln, bias=CONSTS[:, 0:1]))
            ts_ = kb.em("s", lambda E: E.activation(out=tstat[:, 1:2], in_=tstat[:, 0:1], func=AF.Exp, scale=-0.5), inc=1)
            ln_prev_s[0] = ts_
            return bn2[:, 0:1], tstat[:, 1:2], ts_

        def transpose8(src1024, src_tok, evac, src_res=None, engines=("v", "s")):
            kb.wt("t", tok_idf)
            kb.wt("t", src_tok)
            for bank in range(2):
                kb.reuse("t", r_pt[bank])
                tk = None
                for j in range(4):
                    c = bank * 4 + j
                    tk = kb.em("t", lambda E, c=c, j=j, b=bank: E.transpose(
                        out=ptf[b][:, j * 128:(j + 1) * 128],
                        in_=src1024[:, c * 128:(c + 1) * 128],
                        identity=identf), inc=(1 if j == 3 else 0))
                kb.produce(r_pt[bank], tk)
                if src_res is not None:
                    kb.used(src_res, tk)
                eng = engines[bank]
                kb.wt(eng, tk)
                for j in range(4):
                    c = bank * 4 + j
                    tv = evac(c, ptf[bank][:, j * 128:(j + 1) * 128], eng)
                    kb.used(r_pt[bank], tv)

        def scrz_write(fn):
            for rp in r_pring:
                kb.reuse("v", rp)
            tok = kb.em("v", fn, inc=1)
            for rp in r_pring:
                kb.produce(rp, tok)
            return tok

        def scrz_used(tok):
            for rp in r_pring:
                kb.used(rp, tok)

        tok_idb = dma("q", identb, identb_h)
        tok_idf = dma("q", identf, identf_h)
        ones_tok = kb.em("v", lambda E: E.memset(ones_b[:], 1.0), inc=1)
        ones_tok = kb.em("v", lambda E: E.memset(ones_f[:], 1.0), inc=1)
        kb.em("v", lambda E: E.memset(CONSTS[:, 0:1], LN_EPS))
        consts_tok = kb.em("v", lambda E: E.memset(CONSTS[:, 1:2], 1.0), inc=1)
        vaug4 = T4[:, 0:16512].rearrange("p (t pr c) -> p t pr c", pr=8, c=129)
        vones_tok = kb.em("v", lambda E: E.memset(vaug4[:, :, :, 64:65], 1.0), inc=1)

        pp_loads = [
            (ab["q"], PPB[:, 0:8]), (ab["k"], PPB[:, 8:16]),
            (fb["q"], PPB[:, 16:24]), (fb["k"], PPB[:, 24:32]),
            (g1_h, PPB[:, 32:40]), (b1_h, PPB[:, 40:48]),
            (g2_h, PPB[:, 48:56]), (b2_h, PPB[:, 56:64]),
        ]
        for src, dstcols in pp_loads:
            tokd = dma("q", stg[0:8, :], src.rearrange("(a p) -> a p", p=128), res=r_stg)
            kb.wt("t", tokd)
            kb.wt("t", tok_idf)
            kb.reuse("t", r_pb[5])
            tokt = kb.em("t", lambda E: E.transpose(
                out=pb[5][:, 0:8], in_=stg[0:8, :], identity=identf[0:8, 0:8]), inc=1)
            kb.produce(r_pb[5], tokt)
            kb.wt("v", tokt)
            tokv = kb.em("v", lambda E, dc=dstcols: E.tensor_copy(out=dc, in_=pb[5][:, 0:8]), inc=1)
            kb.used(r_pb[5], tokv)
            kb.used(r_stg, tokv)

        tok_gw = dma("g", GWSB[:, 0:64].rearrange("p (kc j) -> p kc j", j=8),
                     gw_h.rearrange("(kc p) j -> p kc j", p=128))
        tokd = dma("q", RSTG[0:1, 0:8], gb_h.rearrange("(a j) -> a j", a=1), res=r_rstg)
        kb.wt("t", tokd)
        kb.wt("t", ones_tok)
        kb.reuse("t", r_pb[5])
        tokt = kb.em("t", lambda E: E.matmul(out=pb[5][:, 0:8], lhsT=ones_f[0:1, 0:128],
                                             rhs=RSTG[0:1, 0:8], start=True, stop=True), inc=1)
        kb.produce(r_pb[5], tokt)
        kb.wt("v", tokt)
        tokv = kb.em("v", lambda E: E.tensor_copy(out=gbb, in_=pb[5][:, 0:8]), inc=1)
        kb.used(r_pb[5], tokv)
        kb.used(r_rstg, tokv)

        def load_bias_bcast(src_h, dst, res):
            tokd = dma("q", RSTG[0:1, :], src_h.rearrange("(a j) -> a j", a=1), res=r_rstg)
            kb.wt("t", tokd)
            kb.wt("t", ones_tok)
            toks = []
            for half in range(2):
                kb.reuse("t", r_pb[4 + half])
                tk = kb.em("t", lambda E, h=half: E.matmul(
                    out=pb[4 + h][:], lhsT=ones_f[0:1, 0:128],
                    rhs=RSTG[0:1, h * 512:(h + 1) * 512], start=True, stop=True), inc=1)
                kb.produce(r_pb[4 + half], tk)
                toks.append(tk)
            kb.reuse("v", res)
            vt = None
            for half in range(2):
                kb.wt("v", toks[half])
                vt = kb.em("v", lambda E, h=half, d=dst: E.tensor_copy(
                    out=d[:, h * 512:(h + 1) * 512], in_=pb[4 + h][:]), inc=1)
                kb.used(r_pb[4 + half], vt)
                kb.used(r_rstg, vt)
            kb.produce(res, vt)

        kb.em("v", lambda E: E.tensor_scalar(
            out=PPB[:, 64:72], in0=PPB[:, 0:8], scalar1=0.125, scalar2=None, op0=OP.mult))
        tok_bq8 = kb.em("v", lambda E: E.tensor_scalar(
            out=PPB[:, 72:80], in0=PPB[:, 16:24], scalar1=0.125, scalar2=None, op0=OP.mult), inc=1)

        load_bias_bcast(ab["v"], bvb, r_bvb)
        load_bias_bcast(ab["o"], bob, r_bob)

        for tt in range(NT):
            sl = tt % 2
            xr = T7[:, sl * 1024:(sl + 1) * 1024]
            tokd = dma("q", xr, x_r[tt], res=r_xr[sl])
            m, s, tok_s = ln_stats(xr, waits=(tokd,))
            kb.wt("v", tok_s)
            kb.reuse("v", r_scrY)
            ta = kb.em("v", lambda E, xr=xr: E.tensor_scalar(
                out=scrY, in0=xr, scalar1=bn2[:, 0:1], scalar2=tstat[:, 1:2],
                op0=OP.subtract, op1=OP.mult), inc=1)
            kb.produce(r_scrY, ta)
            kb.used(r_xr[sl], ta)

            qs = tt // 4

            def evac_a(c, pap, eng, tt=tt):
                if eng == "v":
                    return kb.em("v", lambda E, c=c, pap=pap: E.tensor_copy(
                        out=T1[:, c * 2048 + tt * 128:(c * 2048 + tt * 128) + 128], in_=pap),
                        inc=1, serial=False)
                return kb.em("s", lambda E, c=c, pap=pap: E.activation(
                    out=T1[:, c * 2048 + tt * 128:(c * 2048 + tt * 128) + 128], in_=pap,
                    func=AF.Identity), inc=1, serial=False)

            transpose8(scrY, ta, evac_a, src_res=r_scrY)
            if tt % 4 == 3:
                kb.produce(r_afm[qs], [("v", kb.cnt["v"]), ("s", kb.cnt["s"])])

        spill_toks = []
        pend_t8 = [None]

        def flush_t8():
            if pend_t8[0] is not None:
                pend_t8[0]()
                pend_t8[0] = None
        def emit_attention(wq_h, wk_h, wv_h, wout_h, bq8c, bkc, epi, first):
            wsrc = lambda w: w.rearrange("(kc p) d -> p kc d", p=128)
            tok_wq = dma("g", wq_sb.rearrange("p (kc d) -> p kc d", d=1024), wsrc(wq_h), res=r_wq)
            for rr, half_res in ((0, r_whalfA), (1, r_whalfB)):
                dstA = (T2half if rr == 0 else ctxsb).rearrange("p (kc d) -> p kc d", d=1024)
                if rr == 1:
                    for rc in r_ctx:
                        kb.reuse("g", rc)
                dma("g", dstA, wsrc(wk_h)[:, rr * 4:(rr + 1) * 4, :], res=half_res)
            tok_wo = dma("g", wo_sb.rearrange("p (kc d) -> p kc d", d=1024), wsrc(wout_h), res=r_wo)
            flush_t8()

            for c in range(NPAIR):
                kb.reuse("v", r_kfm[c])
                for tsl in range(4):
                    bank = (c * 4 + tsl) % 4
                    kb.reuse("t", r_pb[bank])
                    kb.consume("t", r_whalfA)
                    kb.consume("t", r_whalfB)
                    kb.consume("t", r_afm[tsl])
                    tk = None
                    for kc in range(NKC):
                        slot = T2half if kc < 4 else ctxsb
                        tk = kb.em("t", lambda E, c=c, tsl=tsl, kc=kc, slot=slot, bank=bank: E.matmul(
                            out=pb[bank][:],
                            lhsT=slot[:, (kc % 4) * 1024 + c * 128:(kc % 4) * 1024 + c * 128 + 128],
                            rhs=T1[:, kc * 2048 + tsl * 512:kc * 2048 + tsl * 512 + 512],
                            start=(kc == 0), stop=(kc == NKC - 1)), inc=(1 if kc == NKC - 1 else 0))
                    kb.used(r_whalfA, tk)
                    kb.used(r_whalfB, tk)
                    kb.used(r_afm[tsl], tk)
                    kb.produce(r_pb[bank], tk)
                    kb.wt("v", tk)
                    tv = kb.em("v", lambda E, c=c, tsl=tsl, bank=bank: E.tensor_scalar(
                        out=T3[:, c * 2048 + tsl * 512:c * 2048 + tsl * 512 + 512],
                        in0=pb[bank][:], scalar1=bkc[:, c:c + 1], scalar2=None, op0=OP.add), inc=1, serial=False)
                    kb.used(r_pb[bank], tv)
                kb.produce(r_kfm[c], ("v", kb.cnt["v"]))

            for rr, half_res in ((0, r_whalfA), (1, r_whalfB)):
                dstA = (T2half if rr == 0 else ctxsb).rearrange("p (kc d) -> p kc d", d=1024)
                dma("g", dstA, wsrc(wv_h)[:, rr * 4:(rr + 1) * 4, :], res=half_res)
            kb.consume("v", r_bvb)
            for tt in range(NT):
                kb.reuse("v", r_vsb[tt])
                last_v = None
                for dsh in range(2):
                    bank = (tt * 2 + dsh) % 4
                    kb.reuse("t", r_pb[bank])
                    tk = None
                    for kc in range(NKC):
                        slot = T2half if kc < 4 else ctxsb
                        res_w = r_whalfA if kc < 4 else r_whalfB
                        kb.consume("t", res_w)
                        kb.consume("t", r_afm[tt // 4])
                        tk = kb.em("t", lambda E, tt=tt, dsh=dsh, kc=kc, slot=slot, bank=bank: E.matmul(
                            out=pb[bank][:],
                            lhsT=T1[:, kc * 2048 + tt * 128:kc * 2048 + tt * 128 + 128],
                            rhs=slot[:, (kc % 4) * 1024 + dsh * 512:(kc % 4) * 1024 + dsh * 512 + 512],
                            start=(kc == 0), stop=(kc == NKC - 1)), inc=(1 if kc == NKC - 1 else 0))
                        if kc == NKC - 1:
                            kb.used(r_whalfA, tk)
                            kb.used(r_whalfB, tk)
                            kb.used(r_afm[tt // 4], tk)
                    kb.produce(r_pb[bank], tk)
                    kb.wt("v", tk)
                    kb.wt("v", vones_tok)
                    base = tt * 1032 + dsh * 4 * 129
                    vdst = T4[:, base: base + 516].rearrange("p (pr c) -> p pr c", c=129)
                    psrc = pb[bank][:].rearrange("p (pr c) -> p pr c", c=128)
                    bsrc = bvb[:, dsh * 512:(dsh + 1) * 512].rearrange("p (pr c) -> p pr c", c=128)
                    for half in range(2):
                        tv = kb.em("v", lambda E, vdst=vdst, psrc=psrc, bsrc=bsrc, h=half: E.scalar_tensor_tensor(
                            out=vdst[:, :, h * 65:h * 65 + 64],
                            in0=psrc[:, :, 0:64] if h == 0 else psrc[:, :, 64:128],
                            scalar=1.0,
                            in1=bsrc[:, :, 0:64] if h == 0 else bsrc[:, :, 64:128],
                            op0=OP.mult, op1=OP.add), inc=1, serial=False)
                        kb.used(r_pb[bank], tv)
                        kb.used(r_bvb, tv)
                        last_v = tv
                kb.produce(r_vsb[tt], last_v)

            kb.reuse("q", r_whalfA)
            tok_masks = dma("q", masksA, masks_h, dkey="dmask")

            for qs in range(NQS):
                kb.reuse("v", r_qsl)
                for c in range(NPAIR):
                    bank = c % 4
                    kb.reuse("t", r_pb[bank])
                    kb.consume("t", r_wq)
                    kb.consume("t", r_afm[qs])
                    tk = None
                    for kc in range(NKC):
                        tk = kb.em("t", lambda E, c=c, kc=kc, qs=qs, bank=bank: E.matmul(
                            out=pb[bank][:],
                            lhsT=wq_sb[:, kc * 1024 + c * 128:kc * 1024 + c * 128 + 128],
                            rhs=T1[:, kc * 2048 + qs * 512:kc * 2048 + qs * 512 + 512],
                            start=(kc == 0), stop=(kc == NKC - 1)), inc=(1 if kc == NKC - 1 else 0))
                    kb.used(r_wq, tk)
                    kb.used(r_afm[qs], tk)
                    kb.produce(r_pb[bank], tk)
                    kb.wt("s", tk)
                    kb.wt("s", tok_bq8)
                    tv = kb.em("s", lambda E, c=c, bank=bank: E.activation(
                        out=qslab[:, c * 512:(c + 1) * 512], in_=pb[bank][:],
                        func=AF.Identity, bias=bq8c[:, c:c + 1], scale=0.125), inc=1, serial=False)
                    kb.used(r_pb[bank], tv)
                kb.produce(r_qsl, ("s", kb.cnt["s"]))
                flush_t8()

                nkt = 4 * qs + 4
                def emit_pair_epilogue(p, rbT, rbb):
                        kb.wt("v", r_pbE.ready)
                        kb.wt("v", r_pbO.ready)
                        kb.reuse("v", r_sums)
                        kb.em("v", lambda E: E.tensor_copy(out=sums_sb[64:65, :], in_=pbE[64:65, :]))
                        tv = kb.em("v", lambda E: E.tensor_copy(out=sums_sb[96:97, :], in_=pbE[96:97, :]), inc=1)
                        kb.produce(r_sums, tv)
                        kb.used(r_pbE, tv)
                        kb.used(r_pbO, tv)
                        kb.wt("t", tv)
                        kb.reuse("t", r_pb[2 * rbT])
                        kb.reuse("t", r_pb[2 * rbT + 1])
                        kb.em("t", lambda E, rbb=rbb: E.matmul(
                            out=rbb[0:64, :], lhsT=ones_f[64:65, 0:64], rhs=sums_sb[64:65, :],
                            start=True, stop=True, tile_position=(64, 0)))
                        tk = kb.em("t", lambda E, rbb=rbb: E.matmul(
                            out=rbb[64:128, :], lhsT=ones_f[96:97, 64:128], rhs=sums_sb[96:97, :],
                            start=True, stop=True, tile_position=(96, 64), skip_group_check=True), inc=1)
                        kb.produce(r_pb[2 * rbT], tk)
                        kb.used(r_sums, tk)
                        kb.wt("s", tk)
                        kb.em("s", lambda E, rbb=rbb: E.activation(out=lnscr, in_=rbb, func=AF.Ln))
                        ts_ = kb.em("s", lambda E: E.activation(out=rb_sb, in_=lnscr, func=AF.Exp, scale=-1.0), inc=1)
                        kb.used(r_pb[2 * rbT], ts_)
                        kb.wt("v", ts_)
                        kb.reuse("v", r_ctx[p])
                        kb.em("v", lambda E, p=p: E.tensor_tensor(
                            out=ctxsb[0:64, p * 512:(p + 1) * 512], in0=pbE[0:64, :],
                            in1=rb_sb[0:64, :], op=OP.mult), serial=False)
                        tv = kb.em("v", lambda E, p=p: E.tensor_tensor(
                            out=ctxsb[64:128, p * 512:(p + 1) * 512], in0=pbO[64:128, :],
                            in1=rb_sb[64:128, :], op=OP.mult), inc=1, serial=False)
                        kb.produce(r_ctx[p], tv)
                        kb.used(r_pbE, tv)
                        kb.used(r_pbO, tv)

                pend_epi = [None]

                for p in range(NPAIR):
                    nb = nkt // 2
                    exp_toks = {}

                    def emit_scores_and_exp(b, p=p, qs=qs, nb=nb):
                        diag = 2 * b >= 4 * qs
                        tsrs = [(p * 2 * nb + 2 * b) % 3, (p * 2 * nb + 2 * b + 1) % 3]
                        for head in range(2):
                            kb.reuse("t", r_pb[2 * tsrs[head]])
                            kb.reuse("t", r_pb[2 * tsrs[head] + 1])
                        kb.consume("t", r_kfm[p])
                        kb.consume("t", r_qsl)
                        tks = [None, None]
                        for j01 in range(2):
                            kt = 2 * b + j01
                            for head in range(2):
                                tsr = tsrs[head]
                                base = head * 64
                                tks[head] = kb.em("t", lambda E, p=p, kt=kt, tsr=tsr, j01=j01, base=base, diag=diag: E.matmul(
                                    out=pw[tsr][:, j01 * 512:(j01 + 1) * 512],
                                    lhsT=T3[base:base + 64, p * 2048 + kt * 128:p * 2048 + kt * 128 + 128],
                                    rhs=qslab[base:base + 64, p * 512:(p + 1) * 512],
                                    start=True, stop=not diag), inc=(0 if diag else 1))
                            if diag:
                                jv = kt - 4 * qs
                                kb.wt("t", tok_idb)
                                kb.wt("t", tok_masks)
                                for head in range(2):
                                    tsr = tsrs[head]
                                    tks[head] = kb.em("t", lambda E, tsr=tsr, j01=j01, jv=jv: E.matmul(
                                        out=pw[tsr][:, j01 * 512:(j01 + 1) * 512],
                                        lhsT=identb,
                                        rhs=masksA[:, jv * 512:(jv + 1) * 512],
                                        start=False, stop=True), inc=1)
                                    kb.used(r_whalfA, tks[head])
                        for head in range(2):
                            tsr = tsrs[head]
                            tk = tks[head]
                            kb.produce(r_pb[2 * tsr], tk)
                            kb.produce(r_pb[2 * tsr + 1], tk)
                            kb.used(r_kfm[p], tk)
                            kb.used(r_qsl, tk)
                            slot = (head * 2) + b % 2
                            kb.wt("s", tk)
                            kb.reuse("s", r_pring[slot])
                            te = kb.em("s", lambda E, slot=slot, tsr=tsr: E.activation(
                                out=pring[slot], in_=pw[tsr][:], func=AF.Exp), inc=1, serial=False)
                            kb.used(r_pb[2 * tsr], te)
                            kb.used(r_pb[2 * tsr + 1], te)
                            kb.produce(r_pring[slot], te)
                            exp_toks[2 * b + head] = (slot, te)

                    def emit_ctx(b, p=p, nb=nb, nkt=nkt):
                        slot_e = exp_toks[2 * b][0]
                        slot_o = exp_toks[2 * b + 1][0]
                        if b == 0:
                            kb.reuse("t", r_pbE)
                            kb.reuse("t", r_pbO)
                        kb.consume("t", r_pring[slot_e])
                        kb.consume("t", r_pring[slot_o])
                        for j01 in range(2):
                            kt = 2 * b + j01
                            laste = (kt == nkt - 1)
                            kb.consume("t", r_vsb[kt])
                            pr_e = pring[slot_e][:, j01 * 512:(j01 + 1) * 512]
                            pr_o = pring[slot_o][:, j01 * 512:(j01 + 1) * 512]
                            tk_e = kb.em("t", lambda E, p=p, kt=kt, pr=pr_e, nkt=nkt: E.matmul(
                                out=pbE[0:65, :],
                                lhsT=T4[:, kt * 1032 + p * 129: kt * 1032 + p * 129 + 65],
                                rhs=pr, start=(kt == 0), stop=(kt == nkt - 1),
                                skip_group_check=True), inc=1)
                            kb.used(r_pring[slot_e], tk_e)
                            tk_s = kb.em("t", lambda E, kt=kt, pr=pr_o, nkt=nkt: E.matmul(
                                out=pbE[96:97, :], lhsT=ones_b[:, 0:1], rhs=pr,
                                start=(kt == 0), stop=(kt == nkt - 1),
                                tile_position=(0, 96), skip_group_check=True), inc=1)
                            tk_o = kb.em("t", lambda E, p=p, kt=kt, pr=pr_o, nkt=nkt: E.matmul(
                                out=pbO[64:128, :],
                                lhsT=T4[:, kt * 1032 + p * 129 + 65: kt * 1032 + p * 129 + 129],
                                rhs=pr, start=(kt == 0), stop=(kt == nkt - 1),
                                tile_position=(0, 64), skip_group_check=True), inc=1)
                            kb.used(r_pring[slot_o], tk_o)
                            kb.used(r_vsb[kt], tk_o)
                            if laste:
                                kb.produce(r_pbE, tk_s)
                                kb.produce(r_pbO, tk_o)

                    for b in range(nb):
                        if b == 1 and pend_epi[0] is not None:
                            pend_epi[0]()
                            pend_epi[0] = None
                        emit_scores_and_exp(b)
                        if b >= 1:
                            emit_ctx(b - 1)
                    emit_ctx(nb - 1)
                    rbT = (p * 2 * nb + 2 * nb + 2) % 3
                    pend_epi[0] = (lambda p=p, rbT=rbT: emit_pair_epilogue(p, rbT, pw[rbT][:, 0:512]))
                if pend_epi[0] is not None:
                    pend_epi[0]()
                    pend_epi[0] = None
                for i in range(4):
                    tt = qs * 4 + i
                    banks = (2 * (i % 2), 2 * (i % 2) + 1)
                    toks = []
                    for ds in range(2):
                        bank = banks[ds]
                        kb.reuse("t", r_pb[bank])
                        kb.consume("t", r_wo)
                        tk = None
                        for c in range(NPAIR):
                            kb.consume("t", r_ctx[c])
                            tk = kb.em("t", lambda E, c=c, i=i, ds=ds, bank=bank: E.matmul(
                                out=pb[bank][:],
                                lhsT=ctxsb[:, c * 512 + i * 128:c * 512 + i * 128 + 128],
                                rhs=wo_sb[:, c * 1024 + ds * 512:c * 1024 + ds * 512 + 512],
                                start=(c == 0), stop=(c == NPAIR - 1)), inc=(1 if c == NPAIR - 1 else 0))
                        for c in range(NPAIR):
                            kb.used(r_ctx[c], tk)
                        kb.used(r_wo, tk)
                        kb.produce(r_pb[bank], tk)
                        toks.append(tk)
                    flush_t8()
                    epi(tt, banks, toks)

        def epi1(tt, banks, toks):
            qs = tt // 4
            kb.consume("v", r_bob)
            kb.reuse("v", r_scrY)
            ty = None
            for ds in range(2):
                kb.wt("v", toks[ds])
                ty = kb.em("v", lambda E, ds=ds, b=banks[ds]: E.scalar_tensor_tensor(
                    out=scrY[:, ds * 512:(ds + 1) * 512], in0=pb[b][:], scalar=1.0,
                    in1=bob[:, ds * 512:(ds + 1) * 512], op0=OP.mult, op1=OP.add), inc=1)
                kb.used(r_pb[banks[ds]], ty)
                kb.used(r_bob, ty)
            kb.produce(r_scrY, ty)
            m, s, tok_s = ln_stats(scrY)
            sl = tt % 2
            xr = T7[:, sl * 1024:(sl + 1) * 1024]
            tokd = dma("q", xr, x_r[tt], res=r_xr[sl])
            kb.wt("v", tok_s)
            tz = scrz_write(lambda E: E.tensor_scalar(
                out=scrZ, in0=scrY, scalar1=bn2[:, 0:1], scalar2=tstat[:, 1:2],
                op0=OP.subtract, op1=OP.mult))
            kb.used(r_scrY, tz)
            kb.wt("v", tokd)
            tx1 = kb.em("v", lambda E, xr=xr: E.tensor_tensor(
                out=xr, in0=scrZ, in1=xr, op=OP.add), inc=1)
            scrz_used(tx1)
            tok_sp = dma("q", x1s_r[tt], xr, waits=(tx1,))
            spill_toks.append(tok_sp)
            kb.used(r_xr[sl], tok_sp)
            m, s, tok_s = ln_stats(xr, waits=(tx1,))
            kb.wt("v", tok_s)
            kb.em("v", lambda E, tt=tt: E.tensor_copy(out=m1s1[:, tt:tt + 1], in_=bn2[:, 0:1]))
            tms = kb.em("v", lambda E, tt=tt: E.tensor_copy(out=m1s1[:, 16 + tt:17 + tt], in_=tstat[:, 1:2]), inc=1)
            kb.reuse("v", r_scrY)
            t1 = kb.em("v", lambda E, xr=xr, tt=tt: E.tensor_scalar(
                out=scrY, in0=xr, scalar1=m1s1[:, tt:tt + 1], scalar2=m1s1[:, 16 + tt:17 + tt],
                op0=OP.subtract, op1=OP.mult), inc=1)
            kb.produce(r_scrY, t1)
            kb.used(r_xr[sl], t1)

            def evac_h1(c, pap, eng, tt=tt):
                if eng == "v":
                    return kb.em("v", lambda E, c=c, pap=pap: E.tensor_scalar(
                        out=T1[:, c * 2048 + tt * 128:c * 2048 + tt * 128 + 128], in0=pap,
                        scalar1=PPB[:, 32 + c:33 + c], scalar2=PPB[:, 40 + c:41 + c],
                        op0=OP.mult, op1=OP.add), inc=1, serial=False)
                return kb.em("s", lambda E, c=c, pap=pap: E.activation(
                    out=T1[:, c * 2048 + tt * 128:c * 2048 + tt * 128 + 128], in_=pap,
                    func=AF.Identity, scale=PPB[:, 32 + c:33 + c],
                    bias=PPB[:, 40 + c:41 + c]), inc=1, serial=False)

            def t8(tt=tt, qs=qs, t1=t1, evac_h1=evac_h1):
                transpose8(scrY, t1, evac_h1, src_res=r_scrY, engines=("s", "s"))
                if tt % 4 == 3:
                    kb.produce(r_afm[qs], [("v", kb.cnt["v"]), ("s", kb.cnt["s"])])

            pend_t8[0] = t8

        def epi2(tt, banks, toks):
            kb.consume("v", r_bob)
            if tt % 2 == 0:
                kb.reuse("v", r_scrY)
                dstw = scrY
            else:
                dstw = scrZ
            ta = None
            for ds in range(2):
                kb.wt("v", toks[ds])
                if tt % 2 == 1 and ds == 0:
                    for rp in r_pring:
                        kb.reuse("v", rp)
                ta = kb.em("v", lambda E, ds=ds, b=banks[ds], dstw=dstw: E.scalar_tensor_tensor(
                    out=dstw[:, ds * 512:(ds + 1) * 512], in0=pb[b][:], scalar=1.0,
                    in1=bob[:, ds * 512:(ds + 1) * 512], op0=OP.mult, op1=OP.add), inc=1)
                kb.used(r_pb[banks[ds]], ta)
                kb.used(r_bob, ta)
            if tt % 2 == 0:
                kb.produce(r_scrY, ta)
            else:
                for rp in r_pring:
                    kb.produce(rp, ta)
            tok_sp = dma("q", atts_r[tt], dstw, waits=(ta,))
            spill_toks.append(tok_sp)
            if tt % 2 == 0:
                kb.used(r_scrY, tok_sp)
            else:
                scrz_used(tok_sp)

        emit_attention(aW["q"], aW["k"], aW["v"], aW["o"], PPB[:, 64:72], PPB[:, 8:16], epi1, True)
        load_bias_bcast(fb["v"], bvb, r_bvb)
        load_bias_bcast(fb["p"], bob, r_bob)
        emit_attention(fW["q"], fW["k"], fW["v"], fW["p"], PPB[:, 72:80], PPB[:, 24:32], epi2, False)

        for rs in r_vsb + [r_wo, r_bvb, r_bob, r_scrY, r_sscr[0], r_sscr[1]]:
            kb.reuse("v", rs)
        for rs in r_ctx + r_pring:
            kb.reuse("v", rs)
        att_slots = [ctxZf[:, 0:1024], ctxZf[:, 1024:2048]]
        r_att = [Res(), Res()]
        for _tok in spill_toks:
            kb.wt("q", _tok)
        u_toks = [None] * NT

        w2buf = [T2[:, 0:8192], T2[:, 8192:16384], T6[:, 0:8192]]
        h2f = T1
        gmv = T3
        w1r = [T7b[:, 0:2048], T7b[:, 2048:4096]]

        def w2_load(idx):
            e, hq = divmod(idx, 4)
            buf = w2buf[idx % 3]
            res = r_w2[idx % 3]
            if idx == 0:
                kb.reuse("g", r_wq)
            if idx == 1:
                kb.reuse("g", r_qsl)
                kb.reuse("g", r_whalfA)
            if idx == 2:
                for rs in r_pring + r_ctx + [r_att[0], r_att[1]]:
                    kb.reuse("g", rs)
            src = w2_h[e, hq * 1024:(hq + 1) * 1024, :].rearrange("(hc p) d -> p hc d", p=128)
            return dma("g", buf.rearrange("p (hc d) -> p hc d", d=1024), src, res=res)

        w2_load(0)
        w2_load(1)

        def eb_load(e):
            sl = e % 2
            dma("q", B2R[0:1, sl * 1024:(sl + 1) * 1024], eb2_h[e:e + 1, :], res=r_b2r[sl])
            tokb = dma("q", stg[0:32, :], eb1_h[e].rearrange("(a p) -> a p", p=128), res=r_stg)
            kb.wt("t", tokb)
            kb.reuse("t", r_pb[5])
            tokt = kb.em("t", lambda E: E.transpose(
                out=pt0f[:, 0:32], in_=stg[0:32, :], identity=identf[0:32, 0:32]), inc=1)
            kb.produce(r_pb[5], tokt)
            kb.wt("v", tokt)
            kb.reuse("v", r_b1p[sl])
            tokv = kb.em("v", lambda E, sl=sl: E.tensor_copy(out=b1pp[sl], in_=pt0f[:, 0:32]), inc=1)
            kb.used(r_pb[5], tokv)
            kb.used(r_stg, tokv)
            kb.produce(r_b1p[sl], tokv)

        r_t2s = Res()
        kb.reuse("v", r_h2)
        for rs in r_vsb + [r_wo, r_bvb, r_bob, r_scrY, r_sscr[0], r_sscr[1]]:
            kb.reuse("v", rs)
        for rs in r_pring + r_ctx + r_kfm + [r_att[0], r_att[1]]:
            kb.reuse("v", rs)
        kb.consume("t", r_h2)
        kb.wt("t", tok_gw)
        gate_prev_s = None
        gbank = [pbE, pbO]
        gres = [r_pbE, r_pbO]
        pend_gm = [None]
        for tt in range(NT):
            sl = tt % 2
            xr = T7[:, sl * 1024:(sl + 1) * 1024]
            tokx = dma("q", xr, x1s_r[tt], res=r_xr[sl])
            if tt < 2:
                for rc in r_ctx:
                    kb.reuse("q", rc)
            toka = dma("q", att_slots[sl], atts_r[tt], res=r_att[sl])
            kb.wt("v", tokx)
            th = kb.em("v", lambda E, xr=xr, tt=tt: E.tensor_scalar(
                out=u_sl(tt, 0, 1024), in0=xr, scalar1=m1s1[:, tt:tt + 1],
                scalar2=m1s1[:, 16 + tt:17 + tt], op0=OP.subtract, op1=OP.mult), inc=1)
            kb.used(r_xr[sl], th)
            kb.wt("v", toka)
            tu = kb.em("v", lambda E, tt=tt, sl=sl: E.tensor_tensor(
                out=u_sl(tt, 0, 1024), in0=att_slots[sl], in1=u_sl(tt, 0, 1024), op=OP.add), inc=1)
            kb.used(r_att[sl], tu)
            u_toks[tt] = tu
            m, s, tok_s = ln_stats(u_sl(tt, 0, 1024), waits=(tu,))
            kb.wt("v", tok_s)
            if tt % 4 == 0:
                kb.reuse("v", r_afm[tt // 4])
            t2stage = T3f[:, 0:1024]
            kb.reuse("v", r_t2s)
            t2tok = kb.em("v", lambda E, tt=tt: E.tensor_scalar(
                out=t2stage, in0=u_sl(tt, 0, 1024), scalar1=bn2[:, 0:1], scalar2=tstat[:, 1:2],
                op0=OP.subtract, op1=OP.mult), inc=1)
            kb.produce(r_t2s, t2tok)

            def evac_h2(c, pap, eng, tt=tt):
                if eng == "v":
                    return kb.em("v", lambda E, c=c, pap=pap, tt=tt: E.tensor_scalar(
                        out=h2f[:, c * 2048 + tt * 128:c * 2048 + tt * 128 + 128], in0=pap,
                        scalar1=PPB[:, 48 + c:49 + c], scalar2=PPB[:, 56 + c:57 + c],
                        op0=OP.mult, op1=OP.add), inc=1, serial=False)
                return kb.em("s", lambda E, c=c, pap=pap, tt=tt: E.activation(
                    out=h2f[:, c * 2048 + tt * 128:c * 2048 + tt * 128 + 128], in_=pap,
                    func=AF.Identity, scale=PPB[:, 48 + c:49 + c],
                    bias=PPB[:, 56 + c:57 + c]), inc=1, serial=False)

            if pend_gm[0] is not None:
                pend_gm[0]()
                pend_gm[0] = None
            ev_toks = []
            transpose8(t2stage, t2tok, lambda c, pap, eng: ev_toks.append(evac_h2(c, pap, eng)) or ev_toks[-1],
                       src_res=r_t2s, engines=("s", "s"))
            bank = tt % 2
            kb.reuse("t", gres[bank])
            for et in ev_toks[3:4] + ev_toks[7:8]:
                kb.wt("t", et)
            kb.wt("t", t2tok)
            tk = None
            for kc in range(NKC):
                tk = kb.em("t", lambda E, tt=tt, kc=kc, bank=bank: E.matmul(
                    out=gbank[bank][:, 0:8],
                    lhsT=h2f[:, kc * 2048 + tt * 128:kc * 2048 + tt * 128 + 128],
                    rhs=GWSB[:, kc * 8:(kc + 1) * 8],
                    start=(kc == 0), stop=(kc == NKC - 1)), inc=(1 if kc == NKC - 1 else 0))
            kb.used(r_h2, tk)
            kb.produce(gres[bank], tk)

            def gmath(tt=tt, bank=bank, tk=tk):
                nonlocal gate_prev_s
                kb.wt("v", tk)
                kb.wt("v", gate_prev_s)
                kb.em("v", lambda E, bank=bank: E.tensor_tensor(
                    out=gsc[:, 0:8], in0=gbank[bank][:, 0:8], in1=gbb, op=OP.add))
                tg = kb.em("v", lambda E: E.reduce_max(out=gsc[:, 8:9], in_=gsc[:, 0:8], axis=AX.X), inc=1)
                kb.used(gres[bank], tg)
                kb.em("v", lambda E: E.tensor_scalar(
                    out=gsc[:, 9:17], in0=gsc[:, 0:8], scalar1=gsc[:, 8:9], scalar2=None, op0=OP.is_equal))
                kb.em("v", lambda E: E.scalar_tensor_tensor(
                    out=gsc[:, 17:25], in0=gsc[:, 9:17], scalar=-1.0e30, in1=gsc[:, 0:8],
                    op0=OP.mult, op1=OP.add))
                kb.em("v", lambda E: E.reduce_max(out=gsc[:, 25:26], in_=gsc[:, 17:25], axis=AX.X))
                tg = kb.em("v", lambda E: E.tensor_tensor(
                    out=gsc[:, 26:27], in0=gsc[:, 25:26], in1=gsc[:, 8:9], op=OP.subtract), inc=1)
                kb.wt("s", tg)
                kb.em("s", lambda E: E.activation(out=gsc[:, 27:28], in_=gsc[:, 26:27], func=AF.Exp))
                kb.em("s", lambda E: E.activation(out=gsc[:, 28:29], in_=gsc[:, 27:28], func=AF.Ln, bias=CONSTS[:, 1:2]))
                tw1 = kb.em("s", lambda E: E.activation(out=gsc[:, 29:30], in_=gsc[:, 28:29], func=AF.Exp, scale=-1.0), inc=1)
                gate_prev_s = tw1
                kb.wt("v", tw1)
                kb.em("v", lambda E: E.tensor_scalar(
                    out=gsc[:, 30:31], in0=gsc[:, 29:30], scalar1=-1.0, scalar2=1.0,
                    op0=OP.mult, op1=OP.add))
                kb.em("v", lambda E: E.tensor_scalar(
                    out=gsc[:, 32:40], in0=gsc[:, 17:25], scalar1=gsc[:, 25:26], scalar2=None, op0=OP.is_equal))
                kb.em("v", lambda E: E.tensor_scalar(
                    out=gsc[:, 40:48], in0=gsc[:, 32:40], scalar1=gsc[:, 30:31], scalar2=None, op0=OP.mult))
                kb.em("v", lambda E, tt=tt: E.scalar_tensor_tensor(
                    out=wfull[:, tt * 8:(tt + 1) * 8], in0=gsc[:, 9:17], scalar=gsc[:, 29:30],
                    in1=gsc[:, 40:48], op0=OP.mult, op1=OP.add), inc=1)

            pend_gm[0] = gmath
        if pend_gm[0] is not None:
            pend_gm[0]()
            pend_gm[0] = None
        kb.produce(r_h2, [("v", kb.cnt["v"]), ("s", kb.cnt["s"])])

        T1f = T1[:].bitcast(F32)
        out_toks = []

        def emit_final(tt):
            sl = tt % 2
            xr = T7[:, sl * 1024:(sl + 1) * 1024]
            kb.reuse("q", r_w1r[sl])
            tokx = dma("q", xr, x1s_r[tt], res=r_xr[sl], waits=(u_toks[tt],))
            m, s, tok_s = ln_stats(u_sl(tt, 0, 1024), waits=(u_toks[tt],))
            zsl = T1f[:, sl * 1024:(sl + 1) * 1024]
            kb.wt("v", tok_s)
            kb.reuse("v", r_zr[sl])
            if tt < 2:
                kb.reuse("v", r_h2)
            tz = kb.em("v", lambda E, tt=tt, zsl=zsl: E.tensor_scalar(
                out=zsl, in0=u_sl(tt, 0, 1024), scalar1=bn2[:, 0:1], scalar2=tstat[:, 1:2],
                op0=OP.subtract, op1=OP.mult), inc=1)
            kb.wt("v", tokx)
            to = kb.em("v", lambda E, xr=xr, zsl=zsl: E.tensor_tensor(
                out=zsl, in0=xr, in1=zsl, op=OP.add), inc=1)
            kb.used(r_xr[sl], to)
            tokout = dma("q", out_r[tt], zsl, waits=(to,))
            out_toks.append(tokout)
            kb.produce(r_zr[sl], tokout)

        g4 = [pbE[:], pbO[:], pb[4], pb[5]]
        gres4 = [r_pbE, r_pbO, r_pb[4], r_pb[5]]
        for rk in r_kfm:
            kb.reuse("s", rk)
        eb_load(0)
        for e in range(NE):
            if e + 1 < NE:
                eb_load(e + 1)
            for hq in range(4):
                idx = e * 4 + hq
                for hcp in range(4):
                    slw = hcp % 2
                    if idx == 0 and hcp < 2:
                        kb.reuse("g", r_xr[slw])
                    src = w1_h[e, :, hq * 1024 + hcp * 256: hq * 1024 + (hcp + 1) * 256]
                    tokw = dma("g", w1r[slw].rearrange("p (kc j) -> p kc j", j=256),
                               src.rearrange("(kc p) j -> p kc j", p=128), res=r_w1r[slw])
                    for sub in range(2):
                        hc = hcp * 2 + sub
                        for ts_ in range(4):
                            bank = (hc * 4 + ts_) % 4
                            kb.reuse("t", r_pb[bank])
                            kb.wt("t", tokw)
                            kb.consume("t", r_h2)
                            tk = None
                            for kc in range(NKC):
                                tk = kb.em("t", lambda E, sub=sub, ts_=ts_, kc=kc, slw=slw, bank=bank: E.matmul(
                                    out=pb[bank][:],
                                    lhsT=w1r[slw][:, kc * 256 + sub * 128:kc * 256 + sub * 128 + 128],
                                    rhs=h2f[:, kc * 2048 + ts_ * 512:kc * 2048 + ts_ * 512 + 512],
                                    start=(kc == 0), stop=(kc == NKC - 1)), inc=(1 if kc == NKC - 1 else 0))
                            kb.used(r_w1r[slw], tk)
                            kb.used(r_h2, tk)
                            kb.produce(r_pb[bank], tk)
                            kb.wt("s", tk)
                            kb.consume("s", r_b1p[e % 2])
                            if ts_ == 0:
                                kb.reuse("s", r_gmh[hc])
                                kb.reuse("s", r_t2s)
                            tg = kb.em("s", lambda E, hc=hc, ts_=ts_, e=e, hq=hq, bank=bank: E.activation(
                                out=gmv[:, hc * 2048 + ts_ * 512:hc * 2048 + ts_ * 512 + 512],
                                in_=pb[bank][:], func=AF.Gelu,
                                bias=b1pp[e % 2][:, hq * 8 + hc:hq * 8 + hc + 1]), inc=1, serial=False)
                            kb.used(r_pb[bank], tg)
                            kb.used(r_b1p[e % 2], tg)
                            if ts_ == 3:
                                kb.produce(r_gmh[hc], tg)
                if idx + 2 < 32:
                    w2_load(idx + 2)
                kb.consume("t", r_w2[idx % 3])
                for tt in range(NT):
                    for ds in range(2):
                        bank = (tt * 2 + ds) % 4
                        kb.reuse("t", gres4[bank])
                        tk = None
                        for hc in range(8):
                            kb.consume("t", r_gmh[hc])
                            tk = kb.em("t", lambda E, tt=tt, ds=ds, hc=hc, bank=bank, idx=idx, hq=hq: E.matmul(
                                out=g4[bank],
                                lhsT=gmv[:, hc * 2048 + tt * 128:hc * 2048 + tt * 128 + 128],
                                rhs=w2buf[idx % 3][:, hc * 1024 + ds * 512:hc * 1024 + ds * 512 + 512],
                                start=(hc == 0), stop=(hc == 7 and hq != 0)), inc=(1 if (hc == 7 and hq != 0) else 0))
                        if hq == 0:
                            kb.consume("t", r_b2r[e % 2])
                            tk = kb.em("t", lambda E, ds=ds, e=e, bank=bank: E.matmul(
                                out=g4[bank], lhsT=ones_f[0:1, 0:128],
                                rhs=B2R[0:1, (e % 2) * 1024 + ds * 512:(e % 2) * 1024 + ds * 512 + 512],
                                start=False, stop=True), inc=1)
                            kb.used(r_b2r[e % 2], tk)
                        for hc in range(8):
                            kb.used(r_gmh[hc], tk)
                        kb.used(r_w2[idx % 3], tk)
                        kb.produce(gres4[bank], tk)
                        kb.wt("v", tk)
                        tv = kb.em("v", lambda E, tt=tt, ds=ds, e=e, bank=bank: E.scalar_tensor_tensor(
                            out=u_sl(tt, ds * 512, ds * 512 + 512), in0=g4[bank],
                            scalar=wfull[:, tt * 8 + e:tt * 8 + e + 1],
                            in1=u_sl(tt, ds * 512, ds * 512 + 512),
                            op0=OP.mult, op1=OP.add), inc=1, serial=False)
                        kb.used(gres4[bank], tv)
                        u_toks[tt] = tv
                        if e == NE - 1 and hq == 3 and ds == 1:
                            emit_final(tt)


        for _tok in out_toks:
            kb.ops["q"].append((("WAIT", _tok[0], _tok[1]), 0))

        for _k in kb.dcnt:
            sems[_k] = ctx.enter_context(nc.semaphore("sem_" + _k))
        block = ctx.enter_context(nc.Block())

        def runner(eng, e):
            for fn, inc in kb.ops[e]:
                if isinstance(fn, tuple):
                    if fn[0] == "WAIT":
                        eng.wait_ge(sems[fn[1]], fn[2])
                    elif fn[0] == "DMA":
                        instr = fn[1](eng)
                        instr.then_inc(sems[fn[2]], 16)
                    continue
                instr = fn(eng)
                if inc:
                    instr.then_inc(sems[e], inc)

        @block.tensor
        def _(eng):
            runner(eng, "t")

        @block.vector
        def _(eng):
            runner(eng, "v")

        @block.scalar
        def _(eng):
            runner(eng, "s")

        @block.gpsimd
        def _(eng):
            runner(eng, "g")

        @block.sync
        def _(eng):
            runner(eng, "q")

    return nc


_NC_CACHE = None


def kernel(**inputs):
    global _NC_CACHE
    if _NC_CACHE is None:
        _NC_CACHE = build_nc()
    nc = _NC_CACHE

    x = np.ascontiguousarray(np.asarray(inputs["L0_attn_x"], dtype=np.float32))
    consts = {
        "cc_identb": np.eye(128, dtype=np.float32).astype(ml_dtypes.bfloat16),
        "cc_identf": np.eye(128, dtype=np.float32),
    }
    mvs = []
    r = np.arange(128)[:, None]
    c = np.arange(512)[None, :]
    for j in range(4):
        mvs.append(np.where(r > c - 128 * j, np.float32(NEG), np.float32(0.0)))
    consts["cc_masks"] = np.concatenate(mvs, axis=1).astype(ml_dtypes.bfloat16)

    shared = {}
    for k, v in inputs.items():
        if k == "L0_attn_x":
            continue
        shared[k] = np.ascontiguousarray(np.asarray(v, dtype=np.float32))

    in_maps = []
    for b in range(8):
        m = {"xcore": np.ascontiguousarray(x[b])}
        m.update(shared)
        m.update(consts)
        in_maps.append(m)

    res = bass_utils.run_bass_kernel_spmd(nc, in_maps, core_ids=list(range(8)))
    out = np.stack([np.asarray(r["out"], dtype=np.float32) for r in res.results], axis=0)
    return out
